# version 24 (final)
# baseline (speedup 1.0000x reference)
import contextlib
import math

import numpy as np
import ml_dtypes

import concourse.bass as bass
import concourse.mybir as mybir
from concourse.bass_utils import run_bass_kernel_spmd

F32 = mybir.dt.float32
BF16 = mybir.dt.bfloat16
AF = mybir.ActivationFunctionType
ALU = mybir.AluOpType
AX = mybir.AxisListType

D = 1024
L = 4096
NCH = 8
NH = 16
DH = 64
NE = 32
HID = 256
E2 = 2048
DSTATE = 128
THETA = 64
EPS = 1e-5
BIG = 32768.0

ENGINES = ("sync", "scalar", "tensor", "vector", "gpsimd")


class Sem:
    def __init__(self, handle):
        self.h = handle
        self.n = 0

    def bump(self, k):
        self.n += k
        return self.n


class Prog:

    def __init__(self, nc):
        self.nc = nc
        self.q = {e: [] for e in ENGINES}

    def op(self, eng, fn, inc=None, k=1, raw=False, dr=False):
        if inc is None:
            self.q[eng].append(fn)
            if eng != "tensor" and not raw:
                self.q[eng].append(lambda e: e.drain())
            return None
        tgt = inc.bump(k)
        h = inc.h
        if raw or eng == "tensor":
            def wrapped(e, fn=fn, h=h, k=k):
                fn(e).then_inc(h, k)
            self.q[eng].append(wrapped)
        else:
            self.q[eng].append(fn)
            self.q[eng].append(lambda e: e.drain())
            self.q[eng].append(lambda e, h=h, k=k: e.nop().then_inc(h, k))
        return tgt

    def wait(self, eng, sem, val=None):
        v = sem.n if val is None else val
        if v <= 0:
            return
        self.q[eng].append(lambda e, h=sem.h, v=v: e.wait_ge(h, v))

    serial_dma = True

    def dma(self, eng, out, in_, sem, serial=True, **kw):
        if self.serial_dma and serial:
            self.wait(eng, sem, sem.n)
        return self.op(eng, lambda e, o=out, i=in_, kw=kw: e.dma_start(out=o, in_=i, **kw),
                       inc=sem, k=16, raw=True)

    def barrier(self, bsem, engines=ENGINES):
        tgt = bsem.n + len(engines)
        for e in engines:
            self.q[e].append(lambda eng: eng.drain())
            self.q[e].append(lambda eng, h=bsem.h: eng.sem_inc(h, 1))
        bsem.n = tgt
        for e in engines:
            self.wait(e, bsem, tgt)

    def emit(self, block):
        prog = self

        @block.sync
        def _(eng):
            for f in prog.q["sync"]:
                f(eng)

        @block.scalar
        def _(eng):
            for f in prog.q["scalar"]:
                f(eng)

        @block.tensor
        def _(eng):
            for f in prog.q["tensor"]:
                f(eng)

        @block.vector
        def _(eng):
            for f in prog.q["vector"]:
                f(eng)

        @block.gpsimd
        def _(eng):
            for f in prog.q["gpsimd"]:
                f(eng)


def host_prep(inputs):
    f32 = np.float32
    bf = ml_dtypes.bfloat16
    d = {}
    x = np.asarray(inputs["L0_attn_a_x"], f32)
    d["x_cores"] = [np.ascontiguousarray(x[b].T) for b in range(8)]

    perm = np.empty(D, np.int64)
    for h in range(NH):
        perm[h * DH:h * DH + 32] = h * DH + 2 * np.arange(32)
        perm[h * DH + 32:h * DH + 64] = h * DH + 2 * np.arange(32) + 1
    d["wqT"] = np.ascontiguousarray(np.asarray(inputs["L0_attn_a_W_q"], f32)[perm].T)
    d["wkT"] = np.ascontiguousarray(np.asarray(inputs["L0_attn_a_W_k"], f32)[perm].T)
    d["wvT"] = np.ascontiguousarray(np.asarray(inputs["L0_attn_a_W_v"], f32).T)
    d["woT"] = np.ascontiguousarray(np.asarray(inputs["L0_attn_a_W_o"], f32).T)

    pos = np.arange(L, dtype=np.float64)[:, None]
    freqs = np.exp(-math.log(10000.0) * np.arange(0, DH, 2, dtype=np.float64) / DH)
    ang = pos * freqs
    cos, sin = np.cos(ang), np.sin(ang)
    d["ropeq"] = np.ascontiguousarray(
        (np.concatenate([cos, sin], 1) * 0.125).astype(f32))
    d["ropek"] = np.ascontiguousarray(np.concatenate([cos, sin], 1).astype(f32))

    d["rwT"] = np.ascontiguousarray(np.asarray(inputs["L0_ffn_router_w"], f32).T)
    d["rb"] = np.asarray(inputs["L0_ffn_router_b"], f32).reshape(1, NE)
    d["w1"] = np.ascontiguousarray(np.asarray(inputs["L0_ffn_weights1"], f32))
    d["w2"] = np.ascontiguousarray(np.asarray(inputs["L0_ffn_weights2"], f32))
    b1 = np.asarray(inputs["L0_ffn_bias1"], f32).reshape(NE * HID)
    d["b1cat"] = np.ascontiguousarray(np.stack([b1, np.ones_like(b1)], 0))
    d["b1cols"] = np.ascontiguousarray(
        b1.reshape(NE, 2, 128).transpose(2, 0, 1).reshape(128, NE * 2))
    d["b2"] = np.ascontiguousarray(np.asarray(inputs["L0_ffn_bias2"], f32))

    win = np.asarray(inputs["L0_attn_b_W_in"], f32)
    sp = np.cumsum([E2, E2, DSTATE, DSTATE, 1, 1, THETA])
    xg_w, gate_w, B_w, C_w, dt_w, Alog_w, theta_w, lam_w = np.split(win, sp, axis=1)
    misc2 = np.zeros((D, 128), f32)
    misc2[:, 0] = dt_w[:, 0]
    misc2[:, 32] = Alog_w[:, 0]
    misc2[:, 64] = lam_w[:, 0]
    d["wincat"] = np.ascontiguousarray(
        np.concatenate([xg_w, gate_w, B_w, C_w, theta_w, misc2], axis=1))
    d["wout"] = np.ascontiguousarray(np.asarray(inputs["L0_attn_b_W_out"], f32))
    d["wB"] = np.asarray(inputs["L0_attn_b_wB"], f32).reshape(DSTATE, 1)
    d["wC"] = np.asarray(inputs["L0_attn_b_wC"], f32).reshape(DSTATE, 1)
    d["Bb"] = np.asarray(inputs["L0_attn_b_B_bias"], f32).reshape(DSTATE, 1)
    d["Cb"] = np.asarray(inputs["L0_attn_b_C_bias"], f32).reshape(DSTATE, 1)
    d["dtb"] = np.asarray(inputs["L0_attn_b_dt_bias"], f32).reshape(1, 1)
    d["Alog"] = np.asarray(inputs["L0_attn_b_A_log"], f32).reshape(1, 1)

    d["identb"] = np.eye(128, dtype=bf)
    d["identf"] = np.eye(128, dtype=f32)
    d["onesf"] = np.ones((1, 128), f32)
    d["onesb"] = np.ones((1, 128), bf)
    return d


T3 = 256


class FfnCtx:
    pass


def make_ffn_ctx(nc, prog, g, S, psb, pps, cst, ones_f, onesc_f, ident_f,
                 consts_ready, partial_scr_name="part_scr", ident_b=None,
                 onesc_b=None):
    c = FfnCtx()
    c.nc, c.prog, c.g_, c.S = nc, prog, g, S
    c.cst, c.ones_f, c.onesc_f, c.ident_f = cst, ones_f, onesc_f, ident_f
    c.ident_b = ident_b
    c.onesc_b = onesc_b
    dS, dC, dG = S["dS"], S["dC"], S["dG"]
    pe, act, dve = S["pe"], S["act"], S["dve"]

    c.h3 = psb("f_h3", [128, 2 * NCH * T3], BF16)
    c.xsq = psb("f_xsq", [128, NCH * T3], BF16)
    c.rows = psb("f_rows", [1, 4 * T3], F32)
    c.mcol = psb("f_mcol", [128, 2], F32)
    c.logit = psb("f_logit", [128, 32], F32)
    c.mx = psb("f_mx", [128, 1], F32)
    c.mtm = psb("f_mtm", [128, 32], BF16)
    c.m_fm = psb("f_mfm", [32, 2 * T3], BF16)
    c.vfm = psb("f_vfm", [32, T3], BF16)
    c.b1cols = psb("f_b1cols", [128, NE * 2], BF16)
    c.mz = psb("f_mz", [2, 128], BF16)
    c.vcr = psb("f_vcr", [2, NE * T3], BF16)[:, :]
    c.b2 = psb("f_b2", [32, D], BF16)
    c.rwT = psb("f_rwT", [128, NCH * 32], F32)
    c.rb_rep = psb("f_rbrep", [128, 32], F32)
    c.sw_rep = psb("f_swrep", [128, 32], F32)
    c.rb_row = psb("f_rbrow", [1, 32], F32)
    c.sw_row = psb("f_swrow", [1, 32], F32)
    c.a1m = psb("f_a1m", [128, 2 * 2 * T3], BF16)
    c.wq = psb("f_wq", [128, 16 * 2048], BF16)
    c.w2q = psb("f_w2q", [128, 16 * 2048], BF16)
    c.part = psb("f_part", [128, 2 * NCH * T3], F32)

    c.z2ps = pps("f_z2ps", [128, NCH * T3], F32)
    c.a1ps = pps("f_a1ps", [128, 2 * 512], F32)
    c.mps = pps("f_mps", [128, 2 * T3], F32)
    c.aps = pps("f_aps", [128, 256], F32)

    c.part_scr = g.get(partial_scr_name)
    c.mrow_scr = g["mrow_scr"]

    prog.dma("gpsimd", c.b1cols[:], g["b1cols"], dG)
    prog.dma("gpsimd", c.b2[:], g["b2"], dG)
    prog.dma("gpsimd", c.rwT[:].rearrange("p (c e) -> p c e", c=NCH),
             g["rwT"].rearrange("(c p) e -> p c e", p=128), dG)
    prog.dma("gpsimd", c.rb_row[:], g["rb"], dG)
    ld = dG.n
    prog.op("vector", lambda e: e.memset(c.vcr[0:1, :], 1.0))
    prog.op("vector", lambda e: e.memset(c.mz[:], 0.0), inc=dve)
    prog.wait("gpsimd", dve, dve.n)
    prog.dma("gpsimd", c.mz[1:2, :], g["onesb"], dG)
    prog.wait("tensor", dG, ld)
    prog.wait("tensor", dS, consts_ready)
    prog.wait("tensor", dve, dve.n)
    for cc in range(NCH):
        prog.op("tensor", lambda e, cc=cc: e.matmul(
            c.aps[0:1, 0:32], onesc_f[:], c.rwT[:, cc * 32:(cc + 1) * 32],
            start=(cc == 0), stop=(cc == NCH - 1)), inc=(pe if cc == NCH - 1 else None))
    prog.wait("vector", pe, pe.n)
    prog.op("vector", lambda e: e.tensor_copy(out=c.sw_row[:], in_=c.aps[0:1, 0:32]),
            inc=dve)
    prog.wait("tensor", dve, dve.n)
    prog.op("tensor", lambda e: e.matmul(
        c.aps[:, 32:64], ones_f[:], c.sw_row[:], start=True, stop=True))
    prog.op("tensor", lambda e: e.matmul(
        c.aps[:, 64:96], ones_f[:], c.rb_row[:], start=True, stop=True), inc=pe)
    prog.wait("vector", pe, pe.n)
    prog.op("vector", lambda e: e.tensor_copy(out=c.sw_rep[:], in_=c.aps[:, 32:64]))
    prog.op("vector", lambda e: e.tensor_copy(out=c.rb_rep[:], in_=c.aps[:, 64:96]),
            inc=dve)
    c.zrow_b = psb("f_zrow", [1, 512], BF16)
    prog.op("vector", lambda e: e.memset(c.zrow_b[:], 0.0), inc=dve)
    c.quarter = -1
    return c


def emit_wcast(prog, g, dW):
    for q in range(4):
        sl = slice(q * 8, (q + 1) * 8)
        prog.dma("gpsimd", g["w1b_scr"][sl], g["w1"][sl], dW, serial=False)
        prog.dma("gpsimd", g["w2b_scr"][sl], g["w2"][sl], dW, serial=False)
    g["_wcast_done"] = dW.n


def ffn_load_half(c, half):
    prog, g = c.prog, c.g_
    dG = c.S["dG"]
    pe = c.S["pe"]
    prog.wait("gpsimd", pe, pe.n)
    marks = []
    for i in range(16):
        e = half * 16 + i
        prog.dma("gpsimd", c.wq[:].rearrange("p (s j f) -> p s j f", s=16, j=NCH)[:, i],
                 g["w1"][e].rearrange("(j p) f -> p j f", p=128), dG, serial=False)
        prog.dma("gpsimd", c.w2q[:].rearrange("p (s f o) -> p s f o", s=16, f=2)[:, i],
                 g["w2"][e].rearrange("(f p) o -> p f o", p=128), dG, serial=False)
        marks.append(dG.n)
    return marks


def emit_ffn_tile(c, x_t, part_ap, t, half, w_ld, ld_val=None, hdr_ld=None):
    prog = c.prog
    S = c.S
    dS, dC, dG = S["dS"], S["dC"], S["dG"]
    pe, act, dve = S["pe"], S["act"], S["dve"]
    cst = c.cst
    sl = t % 2

    if half == 1:
        prog.wait("sync", pe, pe.n)
        prog.dma("sync", c.vcr[1:2, :], c.g_["vrow_scr"][t:t + 1, :], dS)
        vl = dS.n
        prog.wait("tensor", dS, max(vl, hdr_ld if hdr_ld is not None else 0))
        emit_ffn_experts(c, x_t, part_ap, t, half, w_ld, ld_val)
        return
    if ld_val is not None:
        prog.wait("scalar", dS, ld_val)
    prog.wait("scalar", dve, dve.n)
    prog.op("scalar", lambda e: e.activation(out=c.xsq[:], in_=x_t[:],
                                             func=AF.Square), inc=act)
    prog.wait("tensor", act, act.n)
    prog.wait("tensor", dve, dve.n)
    for cc in range(NCH):
        prog.op("tensor", lambda e, cc=cc: e.matmul(
            c.mps[0:1, 0:T3], c.onesc_f[:], x_t[:, cc * T3:(cc + 1) * T3],
            start=(cc == 0), stop=(cc == NCH - 1)))
    for cc in range(NCH):
        prog.op("tensor", lambda e, cc=cc: e.matmul(
            c.mps[0:1, T3:2 * T3], c.onesc_b[:], c.xsq[:, cc * T3:(cc + 1) * T3],
            start=(cc == 0), stop=(cc == NCH - 1)),
            inc=(pe if cc == NCH - 1 else None))
    prog.wait("vector", pe, pe.n)
    prog.op("vector", lambda e: e.tensor_scalar_mul(
        out=c.rows[0:1, 0:T3], in0=c.mps[0:1, 0:T3], scalar1=1.0 / D))
    prog.op("vector", lambda e: e.tensor_scalar_mul(
        out=c.rows[0:1, 2 * T3:3 * T3], in0=c.mps[0:1, T3:2 * T3], scalar1=1.0 / D))
    prog.op("vector", lambda e: e.tensor_tensor(
        out=c.rows[0:1, 3 * T3:4 * T3], in0=c.rows[0:1, 0:T3],
        in1=c.rows[0:1, 0:T3], op=ALU.mult))
    prog.op("vector", lambda e: e.tensor_tensor(
        out=c.rows[0:1, 2 * T3:3 * T3], in0=c.rows[0:1, 2 * T3:3 * T3],
        in1=c.rows[0:1, 3 * T3:4 * T3], op=ALU.subtract), inc=dve)
    prog.wait("scalar", dve, dve.n)
    prog.op("scalar", lambda e: e.activation(
        out=c.rows[0:1, 3 * T3:4 * T3], in_=c.rows[0:1, 2 * T3:3 * T3],
        func=AF.Ln, bias=cst[0:1, 0:1]))
    prog.op("scalar", lambda e: e.activation(
        out=c.rows[0:1, T3:2 * T3], in_=c.rows[0:1, 3 * T3:4 * T3],
        func=AF.Exp, scale=cst[0:1, 2:3]), inc=act)
    prog.wait("tensor", act, act.n)
    prog.op("tensor", lambda e: e.matmul(
        c.mps[:, 0:T3], c.ones_f[:], c.rows[0:1, 0:T3], start=True, stop=True))
    prog.op("tensor", lambda e: e.matmul(
        c.mps[:, T3:2 * T3], c.ones_f[:], c.rows[0:1, T3:2 * T3],
        start=True, stop=True), inc=pe)
    prog.wait("vector", pe, pe.n)
    for cc in range(NCH):
        prog.op("vector", lambda e, cc=cc: e.tensor_tensor(
            out=c.xsq[:, cc * T3:(cc + 1) * T3], in0=x_t[:, cc * T3:(cc + 1) * T3],
            in1=c.mps[:, 0:T3], op=ALU.subtract))
        prog.op("vector", lambda e, cc=cc, sl=sl: e.tensor_tensor(
            out=c.h3[:, sl * NCH * T3 + cc * T3:sl * NCH * T3 + (cc + 1) * T3],
            in0=c.xsq[:, cc * T3:(cc + 1) * T3],
            in1=c.mps[:, T3:2 * T3], op=ALU.mult),
            inc=(dve if cc == NCH - 1 else None))
    h_done = dve.n

    for hh in range(2):
        tok = slice(hh * 128, hh * 128 + 128)
        prog.wait("tensor", dve, dve.n)
        prog.op("tensor", lambda e, hh=hh: e.transpose(
            c.aps[:, 128:129], c.rows[0:1, hh * 128:hh * 128 + 128],
            c.ident_f[0:1, 0:1]))
        prog.op("tensor", lambda e, hh=hh: e.transpose(
            c.aps[:, 129:130], c.rows[0:1, T3 + hh * 128:T3 + hh * 128 + 128],
            c.ident_f[0:1, 0:1]), inc=pe)
        prog.wait("vector", pe, pe.n)
        prog.op("vector", lambda e: e.tensor_copy(out=c.mcol[:], in_=c.aps[:, 128:130]),
                inc=dve)
        prog.wait("tensor", dve, dve.n)
        for cc in range(NCH):
            prog.op("tensor", lambda e, cc=cc, hh=hh: e.matmul(
                c.aps[:, 0:32],
                x_t[:, cc * T3 + hh * 128:cc * T3 + hh * 128 + 128],
                c.rwT[:, cc * 32:(cc + 1) * 32],
                start=(cc == 0), stop=(cc == NCH - 1)),
                inc=(pe if cc == NCH - 1 else None))
        prog.wait("vector", pe, pe.n)
        prog.op("vector", lambda e: e.tensor_scalar(
            out=c.logit[:], in0=c.sw_rep[:], scalar1=c.mcol[:, 0:1], scalar2=-1.0,
            op0=ALU.mult, op1=ALU.mult))
        prog.op("vector", lambda e: e.tensor_tensor(
            out=c.logit[:], in0=c.logit[:], in1=c.aps[:, 0:32], op=ALU.add))
        prog.op("vector", lambda e: e.tensor_scalar(
            out=c.logit[:], in0=c.logit[:], scalar1=c.mcol[:, 1:2], scalar2=None,
            op0=ALU.mult))
        prog.op("vector", lambda e: e.tensor_tensor(
            out=c.logit[:], in0=c.logit[:], in1=c.rb_rep[:], op=ALU.add), dr=True)
        prog.op("vector", lambda e: e.reduce_max(
            out=c.mx[:], in_=c.logit[:], axis=AX.X), dr=True)
        prog.op("vector", lambda e: e.tensor_single_scalar(
            out=c.mtm[:], in_=c.logit[:], scalar=c.mx[:], op=ALU.is_equal), inc=dve)
        prog.wait("tensor", dve, dve.n)
        prog.op("tensor", lambda e: e.transpose(
            c.aps[0:32, 96:160].bitcast(BF16), c.mtm[:], c.ident_b[:]), inc=pe)
        prog.wait("vector", pe, pe.n)
        prog.op("vector", lambda e, hh=hh, sl=sl: e.tensor_copy(
            out=c.m_fm[:, sl * T3 + hh * 128:sl * T3 + hh * 128 + 128],
            in_=c.aps[0:32, 96:160].bitcast(BF16)), inc=dve)
    prog.op("vector", lambda e, sl=sl: e.tensor_scalar(
        out=c.vfm[:], in0=c.m_fm[:, sl * T3:(sl + 1) * T3], scalar1=BIG,
        scalar2=-BIG, op0=ALU.mult, op1=ALU.add), inc=dve)
    prog.wait("sync", dve, dve.n)
    prog.dma("sync", c.mrow_scr, c.vfm[:], dS)
    prog.dma("sync", c.vcr[1:2, :],
             c.mrow_scr.rearrange("a b -> (a b)").unsqueeze(0), dS)
    prog.dma("sync", c.g_["h_scr"][:].rearrange("p (c q) -> p c q", c=NCH)[
        :, :, t * T3:(t + 1) * T3],
        c.h3[:, sl * NCH * T3:(sl + 1) * NCH * T3].rearrange(
            "p (c t) -> p c t", c=NCH), dS)
    prog.dma("sync", c.g_["mfm_scr"][t * 32:(t + 1) * 32, :],
             c.m_fm[:, sl * T3:(sl + 1) * T3], dS)
    vcr_ld = dS.n
    prog.dma("sync", c.g_["vrow_scr"][t:t + 1, :], c.vcr[1:2, :], dS)
    emit_ffn_experts(c, x_t, part_ap, t, half, w_ld, ld_val, vcr_ld)


def emit_ffn_experts(c, x_t, part_ap, t, half, w_ld, ld_val=None, vcr_ld=None):
    prog = c.prog
    S = c.S
    dS, dC, dG = S["dS"], S["dC"], S["dG"]
    pe, act, dve = S["pe"], S["act"], S["dve"]
    sl = t % 2
    if vcr_ld is None:
        vcr_ld = dS.n
    wl = w_ld if isinstance(w_ld, (list, tuple)) else [w_ld] * 16
    prog.wait("tensor", dS, vcr_ld)
    prog.wait("tensor", dve, dve.n)
    for bb in range(4):
        prog.op("tensor", lambda e, bb=bb: e.matmul(
            c.z2ps[:, bb * 512:(bb + 1) * 512], c.zrow_b[0:1, 0:128], c.zrow_b[:],
            start=True, stop=False, skip_group_check=True))
    relu_done = {0: 0, 1: 0}

    def emit_s2(i, slot, rv):
        prog.wait("tensor", act, rv)
        for fc in range(2):
            for oc in range(NCH):
                prog.op("tensor", lambda e, i=i, fc=fc, oc=oc, slot=slot, half=half:
                        e.matmul(
                    c.z2ps[:, oc * T3:(oc + 1) * T3],
                    c.w2q[:, (i * 2 + fc) * D + oc * 128:(i * 2 + fc) * D + oc * 128 + 128],
                    c.a1m[:, slot * 512 + fc * T3:slot * 512 + (fc + 1) * T3],
                    start=False,
                    stop=False, skip_group_check=True),
                    inc=(pe if (half == 0 and i == 15 and fc == 1 and oc == NCH - 1)
                         else None))

    s2_pending = None
    for i in range(16):
        e_g = half * 16 + i
        slot = i % 2
        prog.wait("tensor", dG, wl[i])
        prog.wait("tensor", act, relu_done[slot])
        for fc in range(2):
            for jc in range(NCH):
                prog.op("tensor", lambda e, i=i, fc=fc, jc=jc, slot=slot, sl=sl:
                        e.matmul(
                    c.a1ps[:, slot * 512 + fc * T3:slot * 512 + (fc + 1) * T3],
                    c.wq[:, (i * NCH + jc) * T3 + fc * 128:
                         (i * NCH + jc) * T3 + fc * 128 + 128],
                    c.h3[:, sl * NCH * T3 + jc * T3:sl * NCH * T3 + (jc + 1) * T3],
                    start=(jc == 0), stop=False, skip_group_check=True))
            prog.op("tensor", lambda e, e_g=e_g, fc=fc, slot=slot, sl=sl: e.matmul(
                c.a1ps[:, slot * 512 + fc * T3:slot * 512 + (fc + 1) * T3],
                c.mz[:, 0:128],
                c.vcr[:, e_g * T3:(e_g + 1) * T3],
                start=False, stop=True, skip_group_check=True),
                inc=(pe if fc == 1 else None))
        s1_done = pe.n
        prog.wait("scalar", pe, s1_done)
        for fc in range(2):
            prog.op("scalar", lambda e, slot=slot, fc=fc, e_g=e_g: e.activation(
                out=c.a1m[:, slot * 512 + fc * T3:slot * 512 + (fc + 1) * T3],
                in_=c.a1ps[:, slot * 512 + fc * T3:slot * 512 + (fc + 1) * T3],
                func=AF.Relu, bias=c.b1cols[:, e_g * 2 + fc:e_g * 2 + fc + 1]),
                inc=(act if fc == 1 else None))
        relu_done[slot] = act.n
        if s2_pending is not None:
            emit_s2(*s2_pending)
        s2_pending = (i, slot, relu_done[slot])
    emit_s2(*s2_pending)
    if half == 0:
        prog.wait("scalar", pe, pe.n)
        prog.op("scalar", lambda e, part_ap=part_ap: e.activation(
            out=part_ap, in_=c.z2ps[:], func=AF.Copy), inc=act)
        prog.wait("sync", act, act.n)
        prog.dma("sync", c.part_scr[:, t * T3:(t + 1) * T3].rearrange(
            "(c p) t -> p c t", p=128),
            part_ap.rearrange("p (c t) -> p c t", c=NCH), dS)
    else:
        prog.wait("tensor", dve, dve.n)
        for oc in range(NCH):
            prog.op("tensor", lambda e, oc=oc, sl=sl: e.matmul(
                c.z2ps[:, oc * T3:(oc + 1) * T3],
                c.b2[:, oc * 128:(oc + 1) * 128], c.m_fm[:, sl * T3:(sl + 1) * T3],
                start=False, stop=True, skip_group_check=True),
                inc=(pe if oc == NCH - 1 else None))
        prog.wait("scalar", dS, ld_val if ld_val is not None else dS.n)
        prog.wait("scalar", pe, pe.n)
        prog.wait("vector", pe, pe.n)
        prog.op("vector", lambda e, part_ap=part_ap: e.tensor_tensor(
            out=part_ap, in0=c.z2ps[:], in1=part_ap, op=ALU.add))
        prog.op("vector", lambda e, part_ap=part_ap: e.tensor_scalar_max(
            out=part_ap, in0=part_ap, scalar1=0.0))
        for cc in range(NCH):
            prog.op("vector", lambda e, cc=cc, part_ap=part_ap: e.tensor_tensor(
                out=x_t[:, cc * T3:(cc + 1) * T3],
                in0=x_t[:, cc * T3:(cc + 1) * T3],
                in1=part_ap[:, cc * T3:(cc + 1) * T3], op=ALU.add),
                inc=(dve if cc == NCH - 1 else None))


def emit_ffn_pass(nc, prog, g, S, cst, ones_f, onesc_f, ident_f, consts_ready,
                  src, dst, ident_b=None, onesc_b=None):
    dS = S["dS"]
    pe, act, dve = S["pe"], S["act"], S["dve"]
    with contextlib.ExitStack() as ph:
        psb = lambda name, shape, dt: ph.enter_context(nc.sbuf_tensor(name, shape, dt))
        pps = lambda name, shape, dt: ph.enter_context(nc.psum_tensor(name, shape, dt))
        c = make_ffn_ctx(nc, prog, g, S, psb, pps,
                         cst=cst, ones_f=ones_f, onesc_f=onesc_f,
                         ident_f=ident_f, ident_b=ident_b, onesc_b=onesc_b,
                         consts_ready=consts_ready)
        x_t3 = psb("x_t3", [128, 2 * NCH * T3], F32)
        xb = lambda t: x_t3[:, (t % 2) * NCH * T3:((t % 2) + 1) * NCH * T3]
        NT = L // T3
        buf_marks = {0: (0, 0, 0), 1: (0, 0, 0)}
        load_val = {}
        part_val = {}

        def issue_load(t):
            mp, ma, md = buf_marks[t % 2]
            prog.wait("sync", S["pe"], mp)
            prog.wait("sync", S["act"], ma)
            prog.wait("sync", S["dve"], md)
            prog.dma("sync", xb(t).rearrange("p (c t) -> p c t", c=NCH),
                     g[src][:, t * T3:(t + 1) * T3].rearrange(
                         "(c p) t -> p c t", p=128), dS)
            load_val[t] = dS.n

        pv = lambda t: c.part[:, (t % 2) * NCH * T3:((t % 2) + 1) * NCH * T3]
        hdr_val = {}

        def issue_part(t):
            prog.wait("sync", S["dve"], dve.n)
            prog.dma("sync", pv(t).rearrange("p (c t) -> p c t", c=NCH),
                     g["part_scr"][:, t * T3:(t + 1) * T3].rearrange(
                         "(c p) t -> p c t", p=128), dS)
            part_val[t] = dS.n

        def issue_hdr(t):
            sl = t % 2
            prog.wait("sync", pe, pe.n)
            prog.dma("sync",
                     c.h3[:, sl * NCH * T3:(sl + 1) * NCH * T3].rearrange(
                         "p (c t) -> p c t", c=NCH),
                     g["h_scr"][:].rearrange("p (c q) -> p c q", c=NCH)[
                         :, :, t * T3:(t + 1) * T3], dS)
            prog.dma("sync", c.m_fm[:, sl * T3:(sl + 1) * T3],
                     g["mfm_scr"][t * 32:(t + 1) * 32, :], dS)
            hdr_val[t] = dS.n

        for half in range(2):
            w_ld = ffn_load_half(c, half)
            issue_load(0)
            if half == 1:
                issue_part(0)
                issue_hdr(0)
            for t in range(NT):
                if t + 1 < NT:
                    issue_load(t + 1)
                    if half == 1:
                        issue_part(t + 1)
                        issue_hdr(t + 1)
                prog.wait("tensor", dS, load_val[t])
                prog.wait("vector", dS, load_val[t])
                prog.wait("scalar", dS, load_val[t])
                lv = max(load_val[t], part_val.get(t, 0))
                emit_ffn_tile(c, xb(t), pv(t), t, half, w_ld, ld_val=lv,
                              hdr_ld=hdr_val.get(t))
                buf_marks[t % 2] = (pe.n, act.n, dve.n)
                if half == 1:
                    prog.wait("sync", dve, dve.n)
                    prog.dma("sync",
                             g[dst][:, t * T3:(t + 1) * T3].rearrange(
                                 "(c p) t -> p c t", p=128),
                             xb(t).rearrange("p (c t) -> p c t", c=NCH), dS)



def build_launch_A(phases=("P1", "P2", "P3", "P4"), dbg=(), p2_mode="full", p2_tiles=None, p2_heads=NH, p2_groups=None, x1_input=False):
    nc = bass.Bass("TRN2", target_bir_lowering=False, dynamic_dma_scratch_size=16384)
    T1 = 128
    NT1 = L // T1
    T2 = 512
    NT2 = L // T2

    g = {}

    def din(name, shape, dt):
        g[name] = nc.dram_tensor(name, shape, dt, kind="ExternalInput").ap()
        return g[name]

    def dout(name, shape, dt):
        g[name] = nc.dram_tensor(name, shape, dt, kind="ExternalOutput").ap()
        return g[name]

    def dscr(name, shape, dt):
        g[name] = nc.dram_tensor(name, shape, dt, kind="Internal").ap()
        return g[name]

    x_in = din("x_in", [D, L], F32)
    for nm, shp in (("wqT", [D, D]), ("wkT", [D, D]), ("wvT", [D, D]), ("woT", [D, D])):
        din(nm, shp, F32)
    ropeq = din("ropeq", [L, 64], F32)
    ropek = din("ropek", [L, 64], F32)
    din("rwT", [D, NE], F32)
    din("rb", [1, NE], F32)
    din("w1", [NE, D, HID], F32)
    din("w2", [NE, HID, D], F32)
    din("b1cat", [2, NE * HID], F32)
    din("b1cols", [128, NE * 2], F32)
    din("b2", [NE, D], F32)
    din("wincat", [D, 4544], F32)
    din("wB", [DSTATE, 1], F32)
    din("wC", [DSTATE, 1], F32)
    din("Bb", [DSTATE, 1], F32)
    din("Cb", [DSTATE, 1], F32)
    din("dtb", [1, 1], F32)
    din("Alog", [1, 1], F32)
    identb = din("identb", [128, 128], BF16)
    identf = din("identf", [128, 128], F32)
    onesf = din("onesf", [1, 128], F32)
    onesb = din("onesb", [1, 128], BF16)

    o_x2 = dout("o_x2", [D, L], F32)
    o_gate = dout("o_gate", [128, 16 * L], BF16)
    o_B = dout("o_B", [DSTATE, L], F32)
    o_C = dout("o_C", [DSTATE, L], F32)
    o_rows = dout("o_rows", [2, L], F32)
    o_apart = dout("o_apart", [1, 1], F32)

    qfm_scr = dscr("qfm_scr", [128, NCH * L], BF16)
    dscr("w1b_scr", [NE, D, HID], BF16)
    dscr("w2b_scr", [NE, HID, D], BF16)
    x1_scr = (din("x1_scr", [D, L], F32) if x1_input
              else dscr("x1_scr", [D, L], F32))
    g["x1_scr"] = x1_scr
    mrow_scr = dscr("mrow_scr", [NE, 256], BF16)
    g["h_scr"] = qfm_scr
    dscr("mfm_scr", [512, 256], BF16)
    dscr("vrow_scr", [16, NE * T3], BF16)
    row_scr = dscr("row_scr", [6, L], F32)
    g["row_scr"] = row_scr
    dup_scr = dscr("dup_scr", [64, L], F32)
    g["dup_scr"] = dup_scr
    part_scr = dscr("part_scr", [D, L], F32)
    g["part_scr"] = part_scr
    g["mrow_scr"] = mrow_scr

    dbg_outs = {}
    for name, shape, dt in dbg:
        dbg_outs[name] = dout(name, shape, dt)

    prog = Prog(nc)
    stack = contextlib.ExitStack()
    sb = lambda name, shape, dt: stack.enter_context(nc.sbuf_tensor(name, shape, dt))

    S = {n: Sem(stack.enter_context(nc.semaphore(n))) for n in
         ("dS", "dC", "dG", "dW", "pe", "act", "dve", "bar")}
    dS, dC, dG, pe, act, dve, bar = (S[n] for n in ("dS", "dC", "dG", "pe", "act", "dve", "bar"))

    ident_b = sb("ident_b", [128, 128], BF16)
    ident_f = sb("ident_f", [128, 128], F32)
    ones_f = sb("ones_f", [1, 128], F32)
    ones_b = sb("ones_b", [1, 128], BF16)
    prog.dma("sync", ident_b[:], identb, dS)
    prog.dma("sync", ident_f[:], identf, dS)
    prog.dma("sync", ones_f[:], onesf, dS)
    prog.dma("sync", ones_b[:], onesb, dS)
    consts_ready = dS.n
    cst = sb("cst", [128, 8], F32)
    onesc_f = sb("onesc_f", [128, 1], F32)
    onesc_b = sb("onesc_b", [128, 1], BF16)
    prog.op("vector", lambda e: e.memset(onesc_f[:], 1.0))
    prog.op("vector", lambda e: e.memset(onesc_b[:], 1.0))
    prog.op("vector", lambda e: e.memset(cst[:, 0:1], EPS))
    prog.op("vector", lambda e: e.memset(cst[:, 1:2], 0.5))
    prog.op("vector", lambda e: e.memset(cst[:, 2:3], -0.5))
    prog.op("vector", lambda e: e.memset(cst[:, 3:4], 1.0))
    prog.op("vector", lambda e: e.memset(cst[:, 4:5], -1.0))
    prog.op("vector", lambda e: e.memset(cst[:, 5:6], 1.5707963267948966))
    prog.op("vector", lambda e: e.memset(cst[:, 6:7], 0.15915494309189535))
    prog.op("vector", lambda e: e.memset(cst[:, 7:8], 0.0), inc=dve)
    cst_ready = dve.n
    prog.wait("scalar", dve, cst_ready)

    attn_stack = contextlib.ExitStack()
    asb = lambda name, shape, dt: attn_stack.enter_context(nc.sbuf_tensor(name, shape, dt))
    k_sb = asb("k_sb", [128, NCH * L], BF16)
    v_sb = asb("v_sb", [128, 32 * NH * 65], BF16)
    qmax_sb = asb("qmax_sb", [16, NT1], F32)
    kmax_sb = asb("kmax_sb", [16, 1], F32)

    if "P1" in phases:
        with contextlib.ExitStack() as ph:
            psb = lambda name, shape, dt: ph.enter_context(nc.sbuf_tensor(name, shape, dt))
            pps = lambda name, shape, dt: ph.enter_context(nc.psum_tensor(name, shape, dt))

            wq_sb = psb("wq_sb", [128, NCH * D], BF16)
            wk_sb = psb("wk_sb", [128, NCH * D], BF16)
            wv_sb = psb("wv_sb", [128, NCH * D], BF16)
            for nm, dst in (("wqT", wq_sb), ("wkT", wk_sb), ("wvT", wv_sb)):
                prog.dma("gpsimd", dst[:].rearrange("p (c i) -> p c i", c=NCH),
                         g[nm].rearrange("(c p) i -> p c i", p=128), dG, serial=False)
            w_ld = dG.n
            prog.op("vector", lambda e: e.memset(
                v_sb[:].rearrange("p (a b) -> p a b", b=65)[:, :, 64:65], 1.0), inc=dve)
            vones_done = dve.n

            x_t1 = psb("x_t1", [128, D], F32)
            xsq_t1 = psb("xsq_t1", [128, D], F32)
            h_t1 = psb("h_t1", [128, D], BF16)
            ropeq_t = psb("ropeq_t", [128, 64], F32)
            ropek_t = psb("ropek_t", [128, 64], F32)
            rope_s1 = psb("rope_s1", [128, 512], F32)
            rope_s2 = psb("rope_s2", [128, 512], F32)
            qrot = psb("qrot", [128, D], BF16)
            krot = psb("krot", [128, D], BF16)
            st_row = psb("st_row", [1, 6 * T1], F32)
            nrm_sc = psb("nrm_sc", [128, 16], F32)
            nrm_row = psb("nrm_row", [16, 1], F32)
            qfm_stage = psb("qfm_stage", [128, NCH * T1], BF16)

            ps_misc = pps("ps_misc1", [128, 512], F32)
            ps_q = pps("ps_q1", [128, 1024], F32)
            ps_k = pps("ps_k1", [128, 1024], F32)
            ps_v = pps("ps_v1", [128, 1024], F32)
            ps_tr = pps("ps_tr1", [128, 1024], BF16)

            prog.wait("tensor", dS, consts_ready)
            prog.wait("tensor", dG, w_ld)

            for t in range(NT1):
                tok = slice(t * T1, (t + 1) * T1)
                prog.wait("tensor", act, act.n)
                prog.wait("tensor", dve, dve.n)
                prog.wait("sync", pe, pe.n)
                prog.wait("sync", dve, dve.n)
                prog.wait("sync", act, act.n)
                prog.dma("sync", x_t1[:].rearrange("p (c t) -> p c t", c=NCH),
                         x_in[:, tok].rearrange("(c p) t -> p c t", p=128), dS)
                ldx = dS.n
                prog.wait("scalar", dve, dve.n)
                prog.dma("scalar", ropeq_t[:], ropeq[tok, :], dC)
                prog.dma("scalar", ropek_t[:], ropek[tok, :], dC)
                ldr = dC.n

                prog.wait("scalar", dS, ldx)
                prog.op("scalar", lambda e: e.activation(
                    out=xsq_t1[:], in_=x_t1[:], func=AF.Square), inc=act)
                sq_done = act.n
                prog.wait("tensor", dS, ldx)
                prog.wait("tensor", act, sq_done)
                prog.wait("tensor", dve, dve.n)
                for c in range(NCH):
                    prog.op("tensor", lambda e, c=c: e.matmul(
                        ps_misc[0:1, 0:T1], onesc_f[:], x_t1[:, c * T1:(c + 1) * T1],
                        start=(c == 0), stop=(c == NCH - 1)))
                for c in range(NCH):
                    prog.op("tensor", lambda e, c=c: e.matmul(
                        ps_misc[0:1, T1:2 * T1], onesc_f[:], xsq_t1[:, c * T1:(c + 1) * T1],
                        start=(c == 0), stop=(c == NCH - 1)),
                        inc=(pe if c == NCH - 1 else None))
                stats_done = pe.n
                prog.wait("vector", pe, stats_done)
                prog.op("vector", lambda e: e.tensor_scalar_mul(
                    out=st_row[0:1, 0:T1], in0=ps_misc[0:1, 0:T1], scalar1=1.0 / D))
                prog.op("vector", lambda e: e.tensor_scalar_mul(
                    out=st_row[0:1, 2 * T1:3 * T1], in0=ps_misc[0:1, T1:2 * T1],
                    scalar1=1.0 / D))
                prog.op("vector", lambda e: e.tensor_tensor(
                    out=st_row[0:1, 3 * T1:4 * T1], in0=st_row[0:1, 0:T1],
                    in1=st_row[0:1, 0:T1], op=ALU.mult))
                prog.op("vector", lambda e: e.tensor_tensor(
                    out=st_row[0:1, 2 * T1:3 * T1], in0=st_row[0:1, 2 * T1:3 * T1],
                    in1=st_row[0:1, 3 * T1:4 * T1], op=ALU.subtract), inc=dve)
                var_done = dve.n
                prog.wait("scalar", dve, var_done)
                prog.op("scalar", lambda e: e.activation(
                    out=st_row[0:1, 4 * T1:5 * T1], in_=st_row[0:1, 2 * T1:3 * T1],
                    func=AF.Ln, bias=cst[0:1, 0:1]))
                prog.op("scalar", lambda e: e.activation(
                    out=st_row[0:1, T1:2 * T1], in_=st_row[0:1, 4 * T1:5 * T1],
                    func=AF.Exp, scale=cst[0:1, 2:3]), inc=act)
                rstd_done = act.n

                prog.wait("tensor", act, rstd_done)
                prog.op("tensor", lambda e: e.matmul(
                    ps_misc[:, 2 * T1:3 * T1], ones_f[:], st_row[0:1, 0:T1],
                    start=True, stop=True))
                prog.op("tensor", lambda e: e.matmul(
                    ps_misc[:, 3 * T1:4 * T1], ones_f[:], st_row[0:1, T1:2 * T1],
                    start=True, stop=True), inc=pe)
                bc_done = pe.n
                prog.wait("vector", pe, bc_done)
                for c in range(NCH):
                    prog.op("vector", lambda e, c=c: e.tensor_tensor(
                        out=xsq_t1[:, c * T1:(c + 1) * T1],
                        in0=x_t1[:, c * T1:(c + 1) * T1],
                        in1=ps_misc[:, 2 * T1:3 * T1], op=ALU.subtract))
                    prog.op("vector", lambda e, c=c: e.tensor_tensor(
                        out=h_t1[:, c * T1:(c + 1) * T1],
                        in0=xsq_t1[:, c * T1:(c + 1) * T1],
                        in1=ps_misc[:, 3 * T1:4 * T1], op=ALU.mult),
                        inc=(dve if c == NCH - 1 else None))
                h_done = dve.n

                prog.wait("tensor", dve, h_done)
                projs = [(w_s, ps_o, half, c)
                         for (w_s, ps_o) in ((wq_sb, ps_q), (wk_sb, ps_k), (wv_sb, ps_v))
                         for half in range(2) for c in range(NCH)]
                for i, (w_s, ps_o, half, c) in enumerate(projs):
                    prog.op("tensor",
                            lambda e, w_s=w_s, ps_o=ps_o, half=half, c=c:
                            e.matmul(ps_o[:, half * 512:(half + 1) * 512],
                                     h_t1[:, c * T1:(c + 1) * T1],
                                     w_s[:, c * D + half * 512:
                                         c * D + (half + 1) * 512],
                                     start=(c == 0), stop=(c == NCH - 1)),
                            inc=(pe if i == len(projs) - 1 else None))
                proj_done = pe.n

                prog.wait("scalar", pe, proj_done)
                prog.op("scalar", lambda e: e.activation(
                    out=xsq_t1[:], in_=ps_q[:], func=AF.Square))
                prog.op("scalar", lambda e: e.activation(
                    out=x_t1[:], in_=ps_k[:], func=AF.Square), inc=act)
                nsq_done = act.n
                prog.wait("vector", act, nsq_done)
                for which, src in (("q", xsq_t1), ("k", x_t1)):
                    prog.op("vector", lambda e, src=src: e.reduce_sum(
                        out=nrm_sc[:], in_=src[:].rearrange("p (h d) -> p h d", h=16),
                        axis=AX.X), inc=dve)
                    rdone = dve.n
                    prog.wait("tensor", dve, rdone)
                    prog.op("tensor", lambda e: e.transpose(
                        ps_misc[0:16, 0:T1], nrm_sc[:], ident_f[:]), inc=pe)
                    tdone = pe.n
                    prog.wait("vector", pe, tdone)
                    if which == "q":
                        prog.op("vector", lambda e, t=t: e.reduce_max(
                            out=qmax_sb[:, t:t + 1], in_=ps_misc[0:16, 0:T1],
                            axis=AX.X), inc=dve)
                    else:
                        prog.op("vector", lambda e: e.reduce_max(
                            out=nrm_row[:], in_=ps_misc[0:16, 0:T1], axis=AX.X),
                            inc=dve)
                        if t == 0:
                            prog.op("vector", lambda e: e.tensor_copy(
                                out=kmax_sb[:], in_=nrm_row[:]), inc=dve)
                        else:
                            prog.op("vector", lambda e: e.tensor_tensor(
                                out=kmax_sb[:], in0=kmax_sb[:], in1=nrm_row[:],
                                op=ALU.max), inc=dve)
                norm_done = dve.n

                prog.wait("vector", dC, ldr)
                for (psrc, rtab, rout) in ((ps_q, ropeq_t, qrot), (ps_k, ropek_t, krot)):
                    pv = psrc[:].rearrange("p (h y) -> p h y", h=16)
                    evv = pv[:, :, 0:32]
                    odv = pv[:, :, 32:64]
                    cosv = rtab[:, 0:32].unsqueeze(1).broadcast_to([128, 16, 32])
                    sinv = rtab[:, 32:64].unsqueeze(1).broadcast_to([128, 16, 32])
                    ov = rout[:].rearrange("p (h y) -> p h y", h=16)
                    rev = ov[:, :, 0:32]
                    rodv = ov[:, :, 32:64]
                    s1v = rope_s1[:].rearrange("p (h d) -> p h d", h=16)
                    s2v = rope_s2[:].rearrange("p (h d) -> p h d", h=16)
                    prog.op("vector", lambda e, a=evv, c=cosv, o=s1v: e.tensor_tensor(
                        out=o, in0=a, in1=c, op=ALU.mult))
                    prog.op("vector", lambda e, a=odv, s=sinv, o=s2v: e.tensor_tensor(
                        out=o, in0=a, in1=s, op=ALU.mult))
                    prog.op("vector", lambda e, o=rev, a=s1v, b=s2v: e.tensor_tensor(
                        out=o, in0=a, in1=b, op=ALU.subtract))
                    prog.op("vector", lambda e, a=evv, s=sinv, o=s1v: e.tensor_tensor(
                        out=o, in0=a, in1=s, op=ALU.mult))
                    prog.op("vector", lambda e, a=odv, c=cosv, o=s2v: e.tensor_tensor(
                        out=o, in0=a, in1=c, op=ALU.mult))
                    prog.op("vector", lambda e, o=rodv, a=s1v, b=s2v: e.tensor_tensor(
                        out=o, in0=a, in1=b, op=ALU.add), inc=dve)
                rope_done = dve.n

                if t == 0:
                    prog.wait("scalar", dve, vones_done)
                prog.wait("scalar", dve, rope_done)
                prog.op("scalar", lambda e, t=t: e.activation(
                    out=v_sb[:].rearrange("p (a b) -> p a b", b=65)[
                        :, t * 16:(t + 1) * 16, 0:64],
                    in_=ps_v[:].rearrange("p (h d) -> p h d", h=16),
                    func=AF.Copy), inc=act)
                v_done = act.n

                prog.wait("tensor", dve, rope_done)
                for c in range(NCH):
                    prog.op("tensor", lambda e, c=c: e.transpose(
                        ps_tr[:, c * 128:(c + 1) * 128],
                        qrot[:, c * 128:(c + 1) * 128], ident_b[:]),
                        inc=(pe if c == NCH - 1 else None))
                prog.wait("vector", pe, pe.n)
                prog.op("vector", lambda e: e.tensor_copy(
                    out=qfm_stage[:], in_=ps_tr[:]), inc=dve)
                prog.wait("tensor", dve, dve.n)
                for c in range(NCH):
                    prog.op("tensor", lambda e, c=c: e.transpose(
                        ps_tr[:, c * 128:(c + 1) * 128],
                        krot[:, c * 128:(c + 1) * 128], ident_b[:]),
                        inc=(pe if c == NCH - 1 else None))
                prog.wait("vector", pe, pe.n)
                prog.op("vector", lambda e, t=t: e.tensor_copy(
                    out=k_sb[:].rearrange("p (c q) -> p c q", c=NCH)[
                        :, :, t * T1:(t + 1) * T1],
                    in_=ps_tr[:].rearrange("p (c q) -> p c q", c=NCH)), inc=dve)
                tr_done = dve.n
                prog.wait("sync", dve, tr_done)
                prog.wait("sync", act, v_done)
                prog.dma("sync",
                         qfm_scr[:].rearrange("p (c q) -> p c q", c=NCH)[
                             :, :, t * T1:(t + 1) * T1],
                         qfm_stage[:].rearrange("p (c t) -> p c t", c=NCH), dS)
                prog.wait("tensor", dS, dS.n)

        prog.barrier(bar)

    if "P2" in phases:
        with contextlib.ExitStack() as ph:
            psb = lambda name, shape, dt: ph.enter_context(nc.sbuf_tensor(name, shape, dt))
            pps = lambda name, shape, dt: ph.enter_context(nc.psum_tensor(name, shape, dt))

            wo_sb = psb("wo_sb", [128, NCH * D], BF16)
            prog.dma("gpsimd", wo_sb[:].rearrange("p (c i) -> p c i", c=NCH),
                     g["woT"].rearrange("(c p) i -> p c i", p=128), dG)
            wo_ld = dG.n

            q_pad = psb("q_pad", [128, 8 * T2], BF16)
            x_t2 = psb("x_t2", [128, NCH * T2], F32)
            pt_sb = psb("pt_sb", [128, 2 * 3 * T2], BF16)
            attn_sb = psb("attn_sb", [128, NCH * T2], BF16)
            l_sb = psb("l_sb", [1, NH * T2], BF16)
            cbuf = psb("cbuf", [16, 4], F32)
            crow = psb("crow", [1, 16], F32)
            negc = psb("negc", [128, 16], F32)

            ps_scA = pps("ps_scA", [128, 3 * T2], F32)
            ps_scB = pps("ps_scB", [128, 3 * T2], F32)
            ps_av = pps("ps_av", [128, T2], F32)
            ps_rb = pps("ps_rb", [128, T2], F32)

            NKC = L // 128
            groups = [(i, min(3, NKC - i)) for i in range(0, NKC, 3)]

            prog.wait("tensor", dG, wo_ld)
            prog.op("vector", lambda e: e.memset(q_pad[:], 0.0), inc=dve)
            prog.wait("scalar", dve, dve.n)
            NT2e = NT2 if p2_tiles is None else p2_tiles
            reg_dve_guard = {0: 0, 1: 0}
            for t in range(NT2e):
                prog.wait("sync", dve, dve.n)
                prog.dma("sync", x_t2[:].rearrange("p (c t) -> p c t", c=NCH),
                         x_in[:, t * T2:(t + 1) * T2].rearrange(
                             "(c p) t -> p c t", p=128), dS)
                prog.wait("scalar", pe, pe.n)
                qsrc = qfm_scr.rearrange("p (c q) -> p c q", c=NCH)
                qdst = q_pad[:].rearrange("p (h q) -> p h q", h=8)

                def load_qpad(h0, t=t):
                    for h in range(h0, h0 + 8):
                        pb = (h % 2) * 64
                        prog.dma("scalar", qdst[pb:pb + 64, (h % 8):(h % 8) + 1, :],
                                 qsrc[pb:pb + 64, (h // 2):(h // 2) + 1,
                                      t * T2:(t + 1) * T2], dC, serial=False)
                    return dC.n

                ldq = load_qpad(0)
                x_v = x_t2[:]
                ldx_t = dS.n

                prog.wait("vector", pe, pe.n)
                prog.op("vector", lambda e, t=t: e.reduce_max(
                    out=cbuf[:, 0:1], in_=qmax_sb[:, t * 4:(t + 1) * 4], axis=AX.X),
                    dr=True)
                prog.op("vector", lambda e: e.tensor_tensor(
                    out=cbuf[:, 1:2], in0=cbuf[:, 0:1], in1=kmax_sb[:], op=ALU.mult))
                prog.op("vector", lambda e: e.tensor_scalar_max(
                    out=cbuf[:, 2:3], in0=cbuf[:, 1:2], scalar1=1e-30), inc=dve)
                prog.wait("scalar", dve, dve.n)
                prog.op("scalar", lambda e: e.activation(
                    out=cbuf[:, 2:3], in_=cbuf[:, 2:3], func=AF.Ln))
                prog.op("scalar", lambda e: e.activation(
                    out=cbuf[:, 3:4], in_=cbuf[:, 2:3], func=AF.Exp, scale=cst[0:16, 1:2]),
                    inc=act)
                prog.wait("vector", act, act.n)
                prog.op("vector", lambda e: e.tensor_scalar_mul(
                    out=cbuf[:, 3:4], in0=cbuf[:, 3:4], scalar1=0.125), inc=dve)
                prog.wait("tensor", dve, dve.n)
                prog.op("tensor", lambda e: e.transpose(
                    ps_rb[0:1, 0:16], cbuf[0:16, 3:4], ident_f[0:16, 0:16]), inc=pe)
                prog.wait("vector", pe, pe.n)
                prog.op("vector", lambda e: e.tensor_copy(
                    out=crow[:], in_=ps_rb[0:1, 0:16]), inc=dve)
                prog.wait("tensor", dve, dve.n)
                prog.op("tensor", lambda e: e.matmul(
                    ps_rb[:, 16:32], ones_f[:], crow[:], start=True, stop=True), inc=pe)
                prog.wait("vector", pe, pe.n)
                prog.op("vector", lambda e: e.tensor_scalar_mul(
                    out=negc[:], in0=ps_rb[:, 16:32], scalar1=-1.0), inc=dve)
                negc_done = dve.n
                prog.wait("scalar", dve, negc_done)
                prog.wait("tensor", dC, ldq)
                prog.wait("tensor", dS, ldx_t)

                if p2_mode == "cbound":
                    continue
                reg_exp_done = {0: 0, 1: 0}
                slot_av_done = {0: 0, 1: 0}
                head_evac = [0]
                last_sc = [0]
                for h in range(p2_heads):
                    kbase = (h % 2) * 64
                    koff = (h // 2) * L
                    if h == 8:
                        prog.wait("scalar", pe, last_sc[0])
                        ldq = load_qpad(8)
                        prog.wait("tensor", dC, ldq)
                    exp_of = {}

                    def emit_scores(gi, g0, glen):
                        reg = ps_scA if (gi % 2 == 0) else ps_scB
                        prog.wait("tensor", act, reg_exp_done[gi % 2])
                        if gi <= 1:
                            prog.wait("tensor", dve, reg_dve_guard[gi % 2])
                        for j in range(glen):
                            kc = g0 + j
                            prog.op("tensor",
                                    lambda e, reg=reg, j=j, kc=kc, h=h,
                                    koff=koff:
                                    e.matmul(reg[:, j * T2:(j + 1) * T2],
                                             k_sb[:, koff + kc * 128:
                                                  koff + (kc + 1) * 128],
                                             q_pad[:, (h % 8) * T2:
                                                   (h % 8 + 1) * T2],
                                             start=True, stop=True,
                                             skip_group_check=True),
                                    inc=(pe if j == glen - 1 else None))
                        last_sc[0] = pe.n
                        return pe.n

                    def emit_exp(gi, glen, sc_done):
                        reg = ps_scA if (gi % 2 == 0) else ps_scB
                        prog.wait("scalar", pe, max(sc_done, slot_av_done[gi % 2]))
                        prog.op("scalar",
                                lambda e, reg=reg, glen=glen, gi=gi, h=h:
                                e.activation(
                                    out=pt_sb[:, (gi % 2) * 3 * T2:
                                              (gi % 2) * 3 * T2 + glen * T2],
                                    in_=reg[:, 0:glen * T2],
                                    func=AF.Exp, bias=negc[:, h:h + 1]), inc=act)
                        reg_exp_done[gi % 2] = act.n
                        exp_of[gi] = act.n

                    def emit_av(gi, g0, glen):
                        prog.wait("tensor", act, exp_of[gi])
                        if gi == 0:
                            prog.wait("tensor", dve, head_evac[0])
                        for j in range(glen):
                            kc = g0 + j
                            prog.op("tensor", lambda e, kc=kc, j=j, gi=gi, h=h:
                                    e.matmul(ps_av[0:65, :],
                                             v_sb[:, (kc * 16 + h) * 65:
                                                  (kc * 16 + h) * 65 + 65],
                                             pt_sb[:, (gi % 2) * 3 * T2 + j * T2:
                                                   (gi % 2) * 3 * T2 + (j + 1) * T2],
                                             start=(kc == 0), stop=(kc == NKC - 1),
                                             skip_group_check=True),
                                    inc=(pe if j == glen - 1 else None))
                        slot_av_done[gi % 2] = pe.n

                    gl = groups[:p2_groups]
                    for gi, (g0, glen) in enumerate(gl):
                        sc_done = emit_scores(gi, g0, glen)
                        if p2_mode == "sconly":
                            continue
                        emit_exp(gi, glen, sc_done)
                        if p2_mode in ("scores",):
                            continue
                        if gi >= 1:
                            emit_av(gi - 1, gl[gi - 1][0], gl[gi - 1][1])
                    if p2_mode in ("scores", "sconly"):
                        continue
                    emit_av(len(gl) - 1, gl[-1][0], gl[-1][1])
                    av_done = pe.n
                    prog.wait("vector", pe, av_done)
                    prog.op("vector", lambda e, h=h: e.tensor_copy(
                        out=attn_sb[(h % 2) * 64:(h % 2) * 64 + 64,
                                    (h // 2) * T2:(h // 2 + 1) * T2],
                        in_=ps_av[0:64, :]))
                    prog.op("vector", lambda e, h=h: e.tensor_copy(
                        out=l_sb[0:1, h * T2:(h + 1) * T2], in_=ps_av[64:65, :]),
                        inc=dve)
                    head_evac[0] = dve.n

                if p2_mode != "full":
                    continue
                prog.wait("tensor", dve, dve.n)
                rcp = pt_sb[:, 0:T2]
                for c in range(NCH):
                    prog.op("tensor", lambda e, c=c: e.matmul(
                        ps_rb[0:64, :], ones_b[0:1, 0:64],
                        l_sb[0:1, (2 * c) * T2:(2 * c + 1) * T2],
                        start=True, stop=True))
                    prog.op("tensor", lambda e, c=c: e.matmul(
                        ps_rb[64:128, :], ones_b[0:1, 0:64],
                        l_sb[0:1, (2 * c + 1) * T2:(2 * c + 2) * T2],
                        start=True, stop=True, tile_position=(0, 64)), inc=pe)
                    rb_done = pe.n
                    prog.wait("vector", pe, rb_done)

                    def _recip(e, rcp=rcp):
                        with e.bass.allow_low_precision(reason="bf16 softmax denom"):
                            return e.reciprocal(out=rcp, in_=ps_rb[:])
                    prog.op("vector", _recip)
                    prog.op("vector", lambda e, c=c, rcp=rcp: e.tensor_tensor(
                        out=attn_sb[:, c * T2:(c + 1) * T2],
                        in0=attn_sb[:, c * T2:(c + 1) * T2],
                        in1=rcp, op=ALU.mult), inc=dve)
                    prog.wait("tensor", dve, dve.n)
                prog.wait("scalar", dve, dve.n)
                prog.wait("tensor", dS, ldx_t)
                wo_evac = {0: 0, 1: 0}
                for oc in range(NCH):
                    wops = ps_scA if oc % 2 == 0 else ps_scB
                    prog.wait("tensor", dve, wo_evac[oc % 2])
                    for jc in range(NCH):
                        prog.op("tensor", lambda e, oc=oc, jc=jc, wops=wops: e.matmul(
                            wops[:, 0:T2],
                            wo_sb[:, jc * D + oc * 128:jc * D + oc * 128 + 128],
                            attn_sb[:, jc * T2:(jc + 1) * T2],
                            start=(jc == 0), stop=(jc == NCH - 1),
                            skip_group_check=True),
                            inc=(pe if jc == NCH - 1 else None))
                    wo_done = pe.n
                    prog.wait("vector", pe, wo_done)
                    prog.op("vector", lambda e, oc=oc, x_v=x_v, wops=wops: e.tensor_tensor(
                        out=x_v[:, oc * T2:(oc + 1) * T2],
                        in0=x_v[:, oc * T2:(oc + 1) * T2],
                        in1=wops[:, 0:T2], op=ALU.add), inc=dve)
                    wo_evac[oc % 2] = dve.n
                    reg_dve_guard[oc % 2] = dve.n
                x1_done = dve.n
                prog.wait("sync", dve, x1_done)
                prog.dma("sync",
                         x1_scr[:, t * T2:(t + 1) * T2].rearrange(
                             "(c p) t -> p c t", p=128),
                         x_v.rearrange("p (c t) -> p c t", c=NCH), dS)
                prog.wait("vector", dS, dS.n)

            prog.barrier(bar)
            if "d_cbuf" in dbg_outs:
                prog.dma("gpsimd", dbg_outs["d_cbuf"], cbuf[:], dG)
            if "d_crow" in dbg_outs:
                prog.dma("gpsimd", dbg_outs["d_crow"], crow[:], dG)
            if "d_qmax" in dbg_outs:
                prog.dma("gpsimd", dbg_outs["d_qmax"], qmax_sb[:], dG)
            if "d_kmax" in dbg_outs:
                prog.dma("gpsimd", dbg_outs["d_kmax"], kmax_sb[:], dG)
            if "d_negc" in dbg_outs:
                prog.dma("gpsimd", dbg_outs["d_negc"], negc[:], dG)
            if "d_pt" in dbg_outs:
                prog.dma("gpsimd", dbg_outs["d_pt"], pt_sb[:], dG)
            if "d_attn" in dbg_outs:
                prog.dma("gpsimd", dbg_outs["d_attn"], attn_sb[:], dG)
            if "d_l" in dbg_outs:
                prog.dma("gpsimd", dbg_outs["d_l"], l_sb[:], dG)
            prog.wait("gpsimd", dG, dG.n)
        prog.barrier(bar)

    attn_stack.close()

    if "P3" in phases:
        emit_ffn_pass(nc, prog, g, S, cst, ones_f, onesc_f, ident_f,
                      consts_ready, src="x1_scr", dst="o_x2", ident_b=ident_b,
                      onesc_b=onesc_b)
        prog.barrier(bar)


    if "P4" in phases:
        with contextlib.ExitStack() as p4s:
            qsb = lambda name, shape, dt: p4s.enter_context(nc.sbuf_tensor(name, shape, dt))
            B_raw = qsb("B_raw", [128, L], F32)
            C_raw = qsb("C_raw", [128, L], F32)
            th_sb = qsb("th_sb", [64, L], F32)
            mean_row = qsb("mean_row", [1, L], F32)
            rstd_row = qsb("rstd_row", [1, L], F32)
            FR = qsb("FR", [128, 16 * 32], F32)
            fr = lambda i: FR[:, i * 32:(i + 1) * 32]
            dtb_col = qsb("dtb_col", [128, 1], F32)
            alog_col = qsb("alog_col", [128, 1], F32)
            wb_sb = qsb("wb_sb", [128, 4], F32)
            m1_sb = qsb("m1_sb", [1, 2], F32)
            aps_hold = p4s.enter_context(nc.psum_tensor("aps_hold", [128, 512], F32))

            with contextlib.ExitStack() as ph:
                psb = lambda name, shape, dt: ph.enter_context(nc.sbuf_tensor(name, shape, dt))
                pps = lambda name, shape, dt: ph.enter_context(nc.psum_tensor(name, shape, dt))
                x_t4 = psb("x_t4", [128, NCH * T3], F32)
                xsq4 = psb("xsq4", [128, NCH * T3], F32)
                srow4 = psb("srow4", [1, 4 * T3], F32)
                mps4 = pps("mps4", [128, 2 * T3], F32)
                for t in range(L // T3):
                    prog.wait("sync", pe, pe.n)
                    prog.wait("sync", act, act.n)
                    prog.wait("sync", dve, dve.n)
                    prog.dma("sync", x_t4[:].rearrange("p (c t) -> p c t", c=NCH),
                             g["o_x2"][:, t * T3:(t + 1) * T3].rearrange(
                                 "(c p) t -> p c t", p=128), dS)
                    prog.wait("scalar", dS, dS.n)
                    prog.op("scalar", lambda e: e.activation(
                        out=xsq4[:], in_=x_t4[:], func=AF.Square), inc=act)
                    prog.wait("tensor", dS, dS.n)
                    prog.wait("tensor", act, act.n)
                    prog.wait("tensor", dve, dve.n)
                    for cc in range(NCH):
                        prog.op("tensor", lambda e, cc=cc: e.matmul(
                            mps4[0:1, 0:T3], onesc_f[:], x_t4[:, cc * T3:(cc + 1) * T3],
                            start=(cc == 0), stop=(cc == NCH - 1)))
                    for cc in range(NCH):
                        prog.op("tensor", lambda e, cc=cc: e.matmul(
                            mps4[0:1, T3:2 * T3], onesc_f[:], xsq4[:, cc * T3:(cc + 1) * T3],
                            start=(cc == 0), stop=(cc == NCH - 1)),
                            inc=(pe if cc == NCH - 1 else None))
                    prog.wait("vector", pe, pe.n)
                    prog.op("vector", lambda e, t=t: e.tensor_scalar_mul(
                        out=mean_row[0:1, t * T3:(t + 1) * T3], in0=mps4[0:1, 0:T3],
                        scalar1=1.0 / D))
                    prog.op("vector", lambda e: e.tensor_scalar_mul(
                        out=srow4[0:1, 0:T3], in0=mps4[0:1, T3:2 * T3], scalar1=1.0 / D))
                    prog.op("vector", lambda e, t=t: e.tensor_tensor(
                        out=srow4[0:1, T3:2 * T3],
                        in0=mean_row[0:1, t * T3:(t + 1) * T3],
                        in1=mean_row[0:1, t * T3:(t + 1) * T3], op=ALU.mult))
                    prog.op("vector", lambda e: e.tensor_tensor(
                        out=srow4[0:1, 0:T3], in0=srow4[0:1, 0:T3],
                        in1=srow4[0:1, T3:2 * T3], op=ALU.subtract), inc=dve)
                    prog.wait("scalar", dve, dve.n)
                    prog.op("scalar", lambda e: e.activation(
                        out=srow4[0:1, 2 * T3:3 * T3], in_=srow4[0:1, 0:T3],
                        func=AF.Ln, bias=cst[0:1, 0:1]))
                    prog.op("scalar", lambda e, t=t: e.activation(
                        out=rstd_row[0:1, t * T3:(t + 1) * T3],
                        in_=srow4[0:1, 2 * T3:3 * T3],
                        func=AF.Exp, scale=cst[0:1, 2:3]), inc=act)

            prog.barrier(bar)
            with contextlib.ExitStack() as ph:
                psb = lambda name, shape, dt: ph.enter_context(nc.sbuf_tensor(name, shape, dt))
                pps = lambda name, shape, dt: ph.enter_context(nc.psum_tensor(name, shape, dt))
                win_sb = psb("win_sb", [128, NCH * 4544], BF16)
                misc_sb = psb("misc_sb", [128, L], F32)
                xb_t4 = psb("x_t4b", [128, NCH * T3], F32)
                h4 = psb("h4", [128, NCH * T3], BF16)
                hsc = psb("hsc", [128, T3], F32)
                xs4 = psb("xs4", [128, 2 * T3], F32)
                gst = psb("gst", [128, 2 * T3], BF16)
                mst = psb("mst", [1, T3], F32)
                zps = pps("zps", [128, 1024], F32)
                aps4 = pps("aps4", [128, 1024], F32)
                prog.dma("gpsimd", win_sb[:].rearrange("p (j f) -> p j f", j=NCH),
                         g["wincat"].rearrange("(j p) f -> p j f", p=128), dG, serial=False)
                win_ld = dG.n
                prog.wait("tensor", dG, win_ld)

                gst_store = [0]
                zchunks = ([("xg", f * 128, 128) for f in range(16)]
                           + [("gate", 2048 + f * 128, 128) for f in range(16)]
                           + [("B", 4096, 128), ("C", 4224, 128),
                              ("th", 4352, 64), ("mi", 4416, 128)])
                for t in range(L // T3):
                    prog.wait("sync", pe, pe.n)
                    prog.wait("sync", act, act.n)
                    prog.wait("sync", dve, dve.n)
                    prog.dma("sync", xb_t4[:].rearrange("p (c t) -> p c t", c=NCH),
                             g["o_x2"][:, t * T3:(t + 1) * T3].rearrange(
                                 "(c p) t -> p c t", p=128), dS)
                    prog.wait("tensor", dS, dS.n)
                    prog.wait("tensor", dve, dve.n)
                    prog.wait("tensor", act, act.n)
                    prog.op("tensor", lambda e, t=t: e.matmul(
                        aps4[:, 0:T3], ones_f[:], mean_row[0:1, t * T3:(t + 1) * T3],
                        start=True, stop=True))
                    prog.op("tensor", lambda e, t=t: e.matmul(
                        aps4[:, T3:2 * T3], ones_f[:], rstd_row[0:1, t * T3:(t + 1) * T3],
                        start=True, stop=True), inc=pe)
                    prog.wait("vector", pe, pe.n)
                    prog.wait("vector", dS, dS.n)
                    for cc in range(NCH):
                        prog.op("vector", lambda e, cc=cc: e.tensor_tensor(
                            out=hsc[:], in0=xb_t4[:, cc * T3:(cc + 1) * T3],
                            in1=aps4[:, 0:T3], op=ALU.subtract))
                        prog.op("vector", lambda e, cc=cc: e.tensor_tensor(
                            out=h4[:, cc * T3:(cc + 1) * T3], in0=hsc[:],
                            in1=aps4[:, T3:2 * T3], op=ALU.mult),
                            inc=(dve if cc == NCH - 1 else None))
                    prog.wait("tensor", dve, dve.n)
                    nmean = 0
                    evac_done = {0: 0, 1: 0}
                    for ci, (nm, col0, width) in enumerate(zchunks):
                        slot = ci % 2
                        prog.wait("tensor", act, evac_done[slot])
                        for jc in range(NCH):
                            prog.op("tensor", lambda e, jc=jc, col0=col0, width=width, slot=slot:
                                    e.matmul(zps[0:width, slot * 512:slot * 512 + T3],
                                             win_sb[:, jc * 4544 + col0:jc * 4544 + col0 + width],
                                             h4[:, jc * T3:(jc + 1) * T3],
                                             start=(jc == 0), stop=(jc == NCH - 1)),
                                    inc=(pe if jc == NCH - 1 else None))
                        prog.wait("scalar", pe, pe.n)
                        if nm == "xg":
                            xslot = ci % 2
                            prog.op("scalar", lambda e, slot=slot, xslot=xslot: e.activation(
                                out=xs4[:, xslot * T3:(xslot + 1) * T3],
                                in_=zps[:, slot * 512:slot * 512 + T3],
                                func=AF.Silu), inc=act)
                            prog.wait("tensor", act, act.n)
                            prog.op("tensor", lambda e, xslot=xslot, ci=ci: e.matmul(
                                aps4[0:1, 512:512 + T3],
                                onesc_f[:], xs4[:, xslot * T3:(xslot + 1) * T3],
                                start=(ci == 0), stop=(ci == 15), skip_group_check=True),
                                inc=(pe if ci == 15 else None))
                            nmean += 1
                        elif nm == "gate":
                            gi = ci - 16
                            gslot = gi % 2
                            if gslot == 0:
                                prog.wait("scalar", dS, gst_store[0])
                            prog.op("scalar", lambda e, slot=slot, gslot=gslot: e.activation(
                                out=gst[:, gslot * T3:(gslot + 1) * T3],
                                in_=zps[:, slot * 512:slot * 512 + T3],
                                func=AF.Silu), inc=act)
                            if gslot == 1:
                                prog.wait("sync", act, act.n)
                                prog.dma("sync",
                                         g["o_gate"][:].rearrange(
                                             "p (c q) -> p c q", c=16)[
                                             :, gi - 1:gi + 1, t * T3:(t + 1) * T3],
                                         gst[:].rearrange("p (c q) -> p c q", c=2), dS)
                                gst_store[0] = dS.n
                        elif nm == "B":
                            prog.op("scalar", lambda e, slot=slot, t=t: e.activation(
                                out=B_raw[:, t * T3:(t + 1) * T3],
                                in_=zps[:, slot * 512:slot * 512 + T3],
                                func=AF.Identity), inc=act)
                        elif nm == "C":
                            prog.op("scalar", lambda e, slot=slot, t=t: e.activation(
                                out=C_raw[:, t * T3:(t + 1) * T3],
                                in_=zps[:, slot * 512:slot * 512 + T3],
                                func=AF.Identity), inc=act)
                        elif nm == "th":
                            prog.op("scalar", lambda e, slot=slot, t=t: e.activation(
                                out=th_sb[:, t * T3:(t + 1) * T3],
                                in_=zps[0:64, slot * 512:slot * 512 + T3],
                                func=AF.Identity), inc=act)
                        else:
                            prog.op("scalar", lambda e, slot=slot, t=t: e.activation(
                                out=misc_sb[:, t * T3:(t + 1) * T3],
                                in_=zps[:, slot * 512:slot * 512 + T3],
                                func=AF.Identity), inc=act)
                        evac_done[slot] = act.n
                    prog.wait("vector", pe, pe.n)
                    prog.wait("vector", dS, dS.n)
                    prog.op("vector", lambda e: e.tensor_scalar_mul(
                        out=mst[:], in0=aps4[0:1, 512:512 + T3], scalar1=1.0 / E2),
                        inc=dve)
                    prog.wait("sync", dve, dve.n)
                    prog.dma("sync", g["row_scr"][0:1, t * T3:(t + 1) * T3], mst[:], dS)
                prog.wait("sync", act, act.n)
                prog.dma("sync", g["row_scr"][1:2, :], misc_sb[0:1, :], dS)
                prog.dma("sync", g["row_scr"][2:3, :], misc_sb[32:33, :], dS)
                prog.dma("sync", g["row_scr"][3:4, :], misc_sb[64:65, :], dS)

            prog.barrier(bar)
            prog.dma("sync", g["row_scr"][4:5, 1:L], g["row_scr"][0:1, 0:L - 1], dS)
            prog.op("vector", lambda e: e.memset(fr(15)[0:1, 0:1], 0.0), inc=dve)
            prog.wait("sync", dve, dve.n)
            prog.dma("sync", g["row_scr"][4:5, 0:1], fr(15)[0:1, 0:1], dS)
            prog.wait("scalar", dS, dS.n)
            for i, r in ((0, 1), (1, 2), (2, 3), (3, 0), (8, 4)):
                prog.dma("scalar", fr(i),
                         g["row_scr"][r:r + 1, :].rearrange("a (p f) -> (a p) f", p=128),
                         dC)
            prog.dma("scalar", dtb_col[0:1, :], g["dtb"], dC)
            prog.dma("scalar", alog_col[0:1, :], g["Alog"], dC)
            prog.dma("scalar", wb_sb[:, 0:1], g["wB"], dC)
            prog.dma("scalar", wb_sb[:, 1:2], g["Bb"], dC)
            prog.dma("scalar", wb_sb[:, 2:3], g["wC"], dC)
            prog.dma("scalar", wb_sb[:, 3:4], g["Cb"], dC)
            rows_ld = dC.n
            prog.wait("tensor", dC, rows_ld)
            prog.op("tensor", lambda e: e.matmul(
                aps_hold[:, 0:1], ones_f[:], dtb_col[0:1, 0:1], start=True, stop=True))
            prog.op("tensor", lambda e: e.matmul(
                aps_hold[:, 1:2], ones_f[:], alog_col[0:1, 0:1], start=True, stop=True),
                inc=pe)
            prog.wait("vector", pe, pe.n)
            prog.op("vector", lambda e: e.tensor_copy(out=dtb_col[:], in_=aps_hold[:, 0:1]))
            prog.op("vector", lambda e: e.tensor_copy(out=alog_col[:], in_=aps_hold[:, 1:2]),
                    inc=dve)

            def softplus_fold(dst, srci, bias_col):
                prog.op("vector", lambda e: e.tensor_scalar(
                    out=fr(11), in0=fr(srci), scalar1=bias_col[:, 0:1], scalar2=None,
                    op0=ALU.add), inc=dve)
                prog.wait("scalar", dve, dve.n)
                prog.op("scalar", lambda e: e.activation(
                    out=fr(12), in_=fr(11), func=AF.Abs))
                prog.op("scalar", lambda e: e.activation(
                    out=fr(12), in_=fr(12), func=AF.Exp, scale=cst[:, 4:5]))
                prog.op("scalar", lambda e: e.activation(
                    out=fr(12), in_=fr(12), func=AF.Ln, bias=cst[:, 3:4]), inc=act)
                prog.wait("vector", act, act.n)
                prog.op("vector", lambda e: e.tensor_scalar_max(
                    out=fr(13), in0=fr(11), scalar1=0.0))
                prog.op("vector", lambda e, dst=dst: e.tensor_tensor(
                    out=fr(dst), in0=fr(13), in1=fr(12), op=ALU.add), inc=dve)

            prog.wait("vector", dC, rows_ld)
            softplus_fold(4, 0, dtb_col)
            softplus_fold(5, 1, alog_col)
            prog.op("vector", lambda e: e.tensor_scalar_mul(
                out=fr(5), in0=fr(5), scalar1=-1.0), inc=dve)
            prog.wait("scalar", dve, dve.n)
            prog.op("scalar", lambda e: e.activation(
                out=fr(6), in_=fr(2), func=AF.Exp, scale=cst[:, 4:5]), inc=act)
            prog.wait("vector", act, act.n)
            prog.op("vector", lambda e: e.tensor_scalar_add(
                out=fr(6), in0=fr(6), scalar1=1.0), dr=True)
            prog.op("vector", lambda e: e.reciprocal(out=fr(6), in_=fr(6)), inc=dve)
            prog.op("vector", lambda e: e.tensor_tensor(
                out=fr(7), in0=fr(4), in1=fr(5), op=ALU.mult), inc=dve)
            prog.wait("scalar", dve, dve.n)
            prog.op("scalar", lambda e: e.activation(
                out=fr(7), in_=fr(7), func=AF.Exp), inc=act)
            prog.wait("tensor", dve, dve.n)
            prog.op("tensor", lambda e: e.matmul(
                aps_hold[0:1, 32:64], onesc_f[:], fr(5), start=True, stop=True), inc=pe)
            prog.wait("vector", pe, pe.n)
            prog.op("vector", lambda e: e.reduce_sum(
                out=m1_sb[0:1, 0:1], in_=aps_hold[0:1, 32:64], axis=AX.X), inc=dve)
            prog.wait("sync", dve, dve.n)
            prog.dma("sync", g["o_apart"], m1_sb[0:1, 0:1], dS)
            prog.wait("vector", act, act.n)
            prog.op("vector", lambda e: e.tensor_tensor(
                out=fr(9), in0=fr(6), in1=fr(3), op=ALU.mult))
            prog.op("vector", lambda e: e.tensor_scalar(
                out=fr(11), in0=fr(6), scalar1=-1.0, scalar2=1.0,
                op0=ALU.mult, op1=ALU.add))
            prog.op("vector", lambda e: e.tensor_tensor(
                out=fr(11), in0=fr(11), in1=fr(7), op=ALU.mult))
            prog.op("vector", lambda e: e.tensor_tensor(
                out=fr(11), in0=fr(11), in1=fr(8), op=ALU.mult))
            prog.op("vector", lambda e: e.tensor_tensor(
                out=fr(9), in0=fr(9), in1=fr(11), op=ALU.add))
            prog.op("vector", lambda e: e.tensor_tensor(
                out=fr(10), in0=fr(4), in1=fr(9), op=ALU.mult), inc=dve)
            prog.wait("sync", dve, dve.n)
            prog.dma("sync", g["o_rows"][0:1, :].rearrange("a (p f) -> (a p) f", p=128),
                     fr(4), dS)
            prog.dma("sync", g["o_rows"][1:2, :].rearrange("a (p f) -> (a p) f", p=128),
                     fr(10), dS)
            prog.dma("sync", g["row_scr"][5:6, :].rearrange("a (p f) -> (a p) f", p=128),
                     fr(4), dS)
            dt_row_st = dS.n

            prog.barrier(bar)
            with contextlib.ExitStack() as ph:
                psb = lambda name, shape, dt: ph.enter_context(nc.sbuf_tensor(name, shape, dt))
                pps = lambda name, shape, dt: ph.enter_context(nc.psum_tensor(name, shape, dt))
                bsq = psb("bsq", [128, L], F32)
                s3 = psb("s3", [64, L], F32)
                cosv = psb("cosv", [64, L], F32)
                sinv = psb("sinv", [64, L], F32)
                rrow = psb("rrow", [1, L], F32)
                irow = psb("irow", [64, L], mybir.dt.int32)
                rps = pps("rps", [128, 512], F32)
                rps2 = pps("rps2", [128, 512], F32)

                for (raw, wcol, bcol) in ((B_raw, 0, 1), (C_raw, 2, 3)):
                    prog.wait("scalar", dS, dS.n)
                    prog.wait("scalar", dve, dve.n)
                    prog.op("scalar", lambda e, raw=raw: e.activation(
                        out=bsq[:], in_=raw[:], func=AF.Square), inc=act)
                    for cc in range(NCH):
                        prog.wait("tensor", act, act.n)
                        prog.wait("tensor", dve, dve.n)
                        prog.op("tensor", lambda e, cc=cc: e.matmul(
                            rps[0:1, 0:512], onesc_f[:], bsq[:, cc * 512:(cc + 1) * 512],
                            start=True, stop=True), inc=pe)
                        prog.wait("vector", pe, pe.n)
                        prog.op("vector", lambda e, cc=cc: e.tensor_scalar_mul(
                            out=rrow[0:1, cc * 512:(cc + 1) * 512], in0=rps[0:1, 0:512],
                            scalar1=1.0 / DSTATE), inc=dve)
                    prog.wait("scalar", dve, dve.n)
                    prog.op("scalar", lambda e: e.activation(
                        out=rrow[:], in_=rrow[:], func=AF.Ln, bias=cst[0:1, 0:1]))
                    prog.op("scalar", lambda e: e.activation(
                        out=rrow[:], in_=rrow[:], func=AF.Exp, scale=cst[0:1, 2:3]),
                        inc=act)
                    prog.wait("tensor", act, act.n)
                    for cc in range(NCH):
                        prog.op("tensor", lambda e, cc=cc: e.matmul(
                            rps2[:, 0:512], ones_f[:], rrow[0:1, cc * 512:(cc + 1) * 512],
                            start=True, stop=True), inc=pe)
                        prog.wait("vector", pe, pe.n)
                        prog.op("vector", lambda e, cc=cc, raw=raw: e.tensor_tensor(
                            out=raw[:, cc * 512:(cc + 1) * 512],
                            in0=raw[:, cc * 512:(cc + 1) * 512],
                            in1=rps2[:, 0:512], op=ALU.mult), inc=dve)
                        prog.wait("tensor", dve, dve.n)
                    prog.wait("scalar", dve, dve.n)
                    prog.op("scalar", lambda e, raw=raw, wcol=wcol, bcol=bcol: e.activation(
                        out=raw[:], in_=raw[:], func=AF.Identity,
                        scale=wb_sb[:, wcol:wcol + 1], bias=wb_sb[:, bcol:bcol + 1]),
                        inc=act)

                prog.wait("scalar", dS, dt_row_st)
                prog.dma("scalar", rrow[:], g["row_scr"][5:6, :], dC)
                prog.wait("tensor", dC, dC.n)
                for cc in range(NCH):
                    prog.op("tensor", lambda e, cc=cc: e.matmul(
                        rps[0:64, 0:512], ones_f[0:1, 0:64],
                        rrow[0:1, cc * 512:(cc + 1) * 512],
                        start=True, stop=True), inc=pe)
                    prog.wait("vector", pe, pe.n)
                    prog.op("vector", lambda e, cc=cc: e.tensor_tensor(
                        out=th_sb[:, cc * 512:(cc + 1) * 512],
                        in0=th_sb[:, cc * 512:(cc + 1) * 512],
                        in1=rps[0:64, 0:512], op=ALU.mult), inc=dve)
                    prog.wait("tensor", dve, dve.n)
                prog.op("vector", lambda e: e.memset(s3[:], 0.0), dr=True)
                prog.op("vector", lambda e: e.tensor_tensor_scan(
                    out=th_sb[:], data0=th_sb[:], data1=s3[:],
                    initial=0.0, op0=ALU.add, op1=ALU.add), inc=dve)
                prog.op("vector", lambda e: e.tensor_scalar_mul(
                    out=s3[:], in0=th_sb[:], scalar1=0.15915494309189535), dr=True)
                prog.op("vector", lambda e: e.tensor_copy(out=irow[:], in_=s3[:]), dr=True)
                prog.op("vector", lambda e: e.tensor_copy(out=s3[:], in_=irow[:]), dr=True)
                prog.op("vector", lambda e: e.scalar_tensor_tensor(
                    out=bsq[0:64, :], in0=s3[:], scalar=-6.28125, in1=th_sb[:],
                    op0=ALU.mult, op1=ALU.add), dr=True)
                prog.op("vector", lambda e: e.scalar_tensor_tensor(
                    out=bsq[0:64, :], in0=s3[:], scalar=-0.0019353071795864769,
                    in1=bsq[0:64, :], op0=ALU.mult, op1=ALU.add), inc=dve)
                prog.wait("scalar", dve, dve.n)
                prog.op("scalar", lambda e: e.activation(
                    out=s3[:], in_=bsq[0:64, :], func=AF.Sign))
                prog.op("scalar", lambda e: e.activation(
                    out=irow[:].bitcast(F32), in_=bsq[0:64, :], func=AF.Abs), inc=act)
                prog.wait("vector", act, act.n)
                prog.op("vector", lambda e: e.tensor_single_scalar(
                    out=irow[:].bitcast(F32), in_=irow[:].bitcast(F32),
                    scalar=3.1415927410125732, op=ALU.is_gt))
                prog.op("vector", lambda e: e.tensor_tensor(
                    out=s3[:], in0=s3[:], in1=irow[:].bitcast(F32), op=ALU.mult),
                    dr=True)
                prog.op("vector", lambda e: e.scalar_tensor_tensor(
                    out=bsq[0:64, :], in0=s3[:], scalar=-6.283185307179586,
                    in1=bsq[0:64, :], op0=ALU.mult, op1=ALU.add), inc=dve)
                prog.wait("scalar", dve, dve.n)
                prog.op("scalar", lambda e: e.activation(
                    out=sinv[:], in_=bsq[0:64, :], func=AF.Sin))
                prog.op("scalar", lambda e: e.activation(
                    out=s3[:], in_=bsq[0:64, :], func=AF.Abs), inc=act)
                prog.wait("scalar", act, act.n)
                prog.op("scalar", lambda e: e.activation(
                    out=cosv[:], in_=s3[:], func=AF.Sin,
                    scale=cst[0:64, 4:5], bias=cst[0:64, 5:6]), inc=act)

                sA = psb("sA", [64, L], F32)
                prog.wait("vector", act, act.n)
                for raw, out_name in ((B_raw, "o_B"), (C_raw, "o_C")):
                    prog.wait("sync", dve, dve.n)
                    prog.wait("sync", act, act.n)
                    prog.dma("sync", g["dup_scr"], raw[64:128, :], dS)
                    prog.dma("sync", s3[:], g["dup_scr"], dS)
                    prog.wait("vector", dS, dS.n)
                    prog.op("vector", lambda e, raw=raw: e.tensor_tensor(
                        out=sA[:], in0=raw[0:64, :], in1=sinv[:], op=ALU.mult))
                    prog.op("vector", lambda e: e.tensor_tensor(
                        out=bsq[0:64, :], in0=s3[:], in1=cosv[:], op=ALU.mult))
                    prog.op("vector", lambda e: e.tensor_tensor(
                        out=sA[:], in0=sA[:], in1=bsq[0:64, :], op=ALU.add), dr=True)
                    prog.op("vector", lambda e, raw=raw: e.tensor_tensor(
                        out=bsq[0:64, :], in0=raw[0:64, :], in1=cosv[:], op=ALU.mult))
                    prog.op("vector", lambda e: e.tensor_tensor(
                        out=s3[:], in0=s3[:], in1=sinv[:], op=ALU.mult), dr=True)
                    prog.op("vector", lambda e, raw=raw: e.tensor_tensor(
                        out=raw[0:64, :], in0=bsq[0:64, :], in1=s3[:],
                        op=ALU.subtract), inc=dve)
                    prog.wait("sync", dve, dve.n)
                    prog.dma("sync", g["dup_scr"], sA[:], dS)
                    prog.dma("sync", raw[64:128, :], g["dup_scr"], dS)
                    prog.wait("sync", dS, dS.n)
                    prog.dma("sync", g[out_name], raw[:], dS)
        prog.barrier(bar)
    prog.wait("gpsimd", dS, dS.n)
    prog.wait("gpsimd", dC, dC.n)
    for name in dbg_outs:
        src = {"d_x1": x1_scr, "d_qfm": qfm_scr, "d_x2": g["o_x2"]}.get(name)
        if src is not None:
            prog.dma("gpsimd", dbg_outs[name], src, dG)
    prog.wait("gpsimd", dG, dG.n)
    prog.barrier(bar)
    prog.wait("sync", dS, dS.n)
    prog.wait("scalar", dC, dC.n)
    prog.wait("gpsimd", dG, dG.n)

    with nc.Block() as block:
        prog.emit(block)
    stack.close()
    return nc, g




def build_launch_B(dbg=()):
    nc = bass.Bass("TRN2", target_bir_lowering=False, dynamic_dma_scratch_size=16384)
    g = {}

    def din(name, shape, dt):
        g[name] = nc.dram_tensor(name, shape, dt, kind="ExternalInput").ap()
        return g[name]

    def dout(name, shape, dt):
        g[name] = nc.dram_tensor(name, shape, dt, kind="ExternalOutput").ap()
        return g[name]

    def dscr(name, shape, dt):
        g[name] = nc.dram_tensor(name, shape, dt, kind="Internal").ap()
        return g[name]

    x2_in = din("x2_in", [D, L], F32)
    gate_in = din("gate_in", [128, 16 * L], BF16)
    B_in = din("B_in", [DSTATE, L], F32)
    C_in = din("C_in", [DSTATE, L], F32)
    rows_in = din("rows_in", [2, L], F32)
    aparts = din("aparts", [1, 8], F32)
    din("wout", [E2, D], F32)
    din("rwT", [D, NE], F32)
    din("rb", [1, NE], F32)
    din("w1", [NE, D, HID], F32)
    din("w2", [NE, HID, D], F32)
    din("b1cat", [2, NE * HID], F32)
    din("b1cols", [128, NE * 2], F32)
    din("b2", [NE, D], F32)
    identf = din("identf", [128, 128], F32)
    identb_in = din("identb", [128, 128], BF16)
    onesf = din("onesf", [1, 128], F32)
    onesb = din("onesb", [1, 128], BF16)
    o_out = dout("o_out", [D, L], F32)
    dscr("x3_scr", [D, L], F32)
    dscr("part_scr", [D, L], F32)
    g["mrow_scr"] = dscr("mrow_scr", [NE, 256], BF16)
    g["h_scr"] = dscr("h_scrB", [128, NCH * L], BF16)
    dscr("mfm_scr", [512, 256], BF16)
    dscr("vrow_scr", [16, NE * T3], BF16)
    dscr("w1b_scr", [NE, D, HID], BF16)
    dscr("w2b_scr", [NE, HID, D], BF16)
    darow_scr = dscr("darow_scr", [1, L], F32)

    dbg_outs = {}
    for name, shape, dt in dbg:
        dbg_outs[name] = dout(name, shape, dt)

    prog = Prog(nc)
    stack = contextlib.ExitStack()
    sb = lambda name, shape, dt: stack.enter_context(nc.sbuf_tensor(name, shape, dt))
    S = {n: Sem(stack.enter_context(nc.semaphore(n))) for n in
         ("dS", "dC", "dG", "dW", "pe", "act", "dve", "bar")}
    dS, dC, dG, pe, act, dve, bar = (S[n] for n in
                                     ("dS", "dC", "dG", "pe", "act", "dve", "bar"))

    ident_f = sb("identB_f", [128, 128], F32)
    ident_bb = sb("identB_b", [128, 128], BF16)
    ones_f = sb("onesB_f", [1, 128], F32)
    ones_b = sb("onesB_b", [1, 128], BF16)
    prog.dma("sync", ident_f[:], identf, dS)
    prog.dma("sync", ident_bb[:], identb_in, dS)
    prog.dma("sync", ones_f[:], onesf, dS)
    prog.dma("sync", ones_b[:], onesb, dS)
    consts_ready = dS.n
    cst = sb("cstB", [128, 8], F32)
    onesc_f = sb("onescB_f", [128, 1], F32)
    onesc_b5 = sb("onescB_b", [128, 1], BF16)
    zrow_b5 = sb("zrowB_b", [1, 512], BF16)
    prog.op("vector", lambda e: e.memset(cst[:, 0:1], EPS))
    prog.op("vector", lambda e: e.memset(cst[:, 1:2], 0.5))
    prog.op("vector", lambda e: e.memset(cst[:, 2:3], -0.5))
    prog.op("vector", lambda e: e.memset(cst[:, 3:4], 1.0))
    prog.op("vector", lambda e: e.memset(cst[:, 4:5], -1.0))
    prog.op("vector", lambda e: e.memset(onesc_f[:], 1.0))
    prog.op("vector", lambda e: e.memset(onesc_b5[:], 1.0))
    prog.op("vector", lambda e: e.memset(zrow_b5[:], 0.0), inc=dve)
    cst_ready = dve.n
    prog.wait("scalar", dve, cst_ready)
    prog.wait("tensor", dve, cst_ready)

    with contextlib.ExitStack() as ph:
        psb = lambda name, shape, dt: ph.enter_context(nc.sbuf_tensor(name, shape, dt))
        pps = lambda name, shape, dt: ph.enter_context(nc.psum_tensor(name, shape, dt))
        ap8 = psb("ap8", [1, 8], F32)
        asc = psb("asc", [128, 2], F32)
        dtF = psb("dtF", [128, 32], F32)
        dAF = psb("dAF", [128, 32], F32)
        du_row = psb("du_row", [1, L], F32)
        dA_row = psb("dA_row", [1, L], F32)
        y_row = psb("y_row", [1, L], F32)
        dAb = psb("dAb", [128, L], F32)
        dBu = psb("dBu", [128, L], F32)
        Bst = psb("Bst", [128, L], F32)
        Cst = psb("Cst", [128, L], F32)
        states = psb("states", [128, L], F32)
        sps = pps("sps", [128, 1024], F32)

        prog.dma("sync", ap8[:], aparts, dS)
        prog.dma("sync", dtF[:],
                 rows_in[0:1, :].rearrange("a (p f) -> (a p) f", p=128), dS)
        prog.dma("sync", du_row[:], rows_in[1:2, :], dS)
        prog.dma("scalar", Bst[:], B_in, dC)
        prog.dma("scalar", Cst[:], C_in, dC)
        ld5 = dS.n
        prog.wait("vector", dS, ld5)
        prog.op("vector", lambda e: e.reduce_sum(
            out=asc[0:1, 0:1], in_=ap8[:], axis=AX.X), dr=True)
        prog.op("vector", lambda e: e.tensor_scalar_mul(
            out=asc[0:1, 0:1], in0=asc[0:1, 0:1], scalar1=1.0 / (8 * L)), inc=dve)
        prog.wait("tensor", dve, dve.n)
        prog.wait("tensor", dS, consts_ready)
        prog.op("tensor", lambda e: e.matmul(
            sps[:, 0:1], ones_f[:], asc[0:1, 0:1], start=True, stop=True), inc=pe)
        prog.wait("vector", pe, pe.n)
        prog.op("vector", lambda e: e.tensor_copy(out=asc[:, 1:2], in_=sps[:, 0:1]),
                inc=dve)
        prog.op("vector", lambda e: e.tensor_scalar(
            out=dAF[:], in0=dtF[:], scalar1=asc[:, 1:2], scalar2=None,
            op0=ALU.mult), inc=dve)
        prog.wait("scalar", dve, dve.n)
        prog.op("scalar", lambda e: e.activation(
            out=dAF[:], in_=dAF[:], func=AF.Exp), inc=act)
        prog.wait("sync", act, act.n)
        prog.dma("sync", darow_scr.rearrange("a (p f) -> (a p) f", p=128), dAF[:], dS)
        prog.dma("sync", dA_row[:], darow_scr, dS)
        rowb = dS.n
        prog.wait("tensor", dS, rowb)
        prog.wait("vector", dC, dC.n)
        for cc in range(NCH):
            prog.op("tensor", lambda e, cc=cc: e.matmul(
                sps[:, 0:512], ones_f[:], dA_row[0:1, cc * 512:(cc + 1) * 512],
                start=True, stop=True), inc=pe)
            prog.wait("vector", pe, pe.n)
            prog.op("vector", lambda e, cc=cc: e.tensor_copy(
                out=dAb[:, cc * 512:(cc + 1) * 512], in_=sps[:, 0:512]), inc=dve)
            prog.wait("tensor", dve, dve.n)
            prog.op("tensor", lambda e, cc=cc: e.matmul(
                sps[:, 512:1024], ones_f[:], du_row[0:1, cc * 512:(cc + 1) * 512],
                start=True, stop=True), inc=pe)
            prog.wait("vector", pe, pe.n)
            prog.op("vector", lambda e, cc=cc: e.tensor_tensor(
                out=dBu[:, cc * 512:(cc + 1) * 512],
                in0=Bst[:, cc * 512:(cc + 1) * 512],
                in1=sps[:, 512:1024], op=ALU.mult), inc=dve)
            prog.wait("tensor", dve, dve.n)
        prog.op("vector", lambda e: e.tensor_tensor_scan(
            out=states[:], data0=dAb[:], data1=dBu[:],
            initial=0.0, op0=ALU.mult, op1=ALU.add), dr=True)
        prog.op("vector", lambda e: e.tensor_tensor(
            out=states[:], in0=states[:], in1=Cst[:], op=ALU.mult), inc=dve)
        prog.wait("tensor", dve, dve.n)
        for cc in range(NCH):
            prog.op("tensor", lambda e, cc=cc: e.matmul(
                sps[0:1, 0:512], onesc_f[:], states[:, cc * 512:(cc + 1) * 512],
                start=True, stop=True), inc=pe)
            prog.wait("vector", pe, pe.n)
            prog.op("vector", lambda e, cc=cc: e.tensor_copy(
                out=y_row[0:1, cc * 512:(cc + 1) * 512], in_=sps[0:1, 0:512]),
                inc=dve)
            prog.wait("tensor", dve, dve.n)

        with contextlib.ExitStack() as p2h:
            qsb = lambda name, shape, dt: p2h.enter_context(nc.sbuf_tensor(name, shape, dt))
            qps = lambda name, shape, dt: p2h.enter_context(nc.psum_tensor(name, shape, dt))
            wout_sb = qsb("wout_sb", [128, 16 * D], BF16)
            prog.dma("gpsimd", wout_sb[:].rearrange("p (j o) -> p j o", j=16),
                     g["wout"].rearrange("(j p) o -> p j o", p=128), dG)
            wout_ld = dG.n
            gt5 = qsb("gt5", [128, 16 * T3], BF16)
            yg5 = qsb("yg5", [128, 16 * T3], BF16)
            x25 = qsb("x25", [128, NCH * T3], F32)
            zsc5 = qsb("zsc5", [128, NCH * T3], F32)
            zps5 = qps("zps5", [128, NCH * T3], F32)
            bps5 = qps("bps5", [128, 512], F32)
            prog.wait("tensor", dG, wout_ld)
            for t in range(L // T3):
                prog.wait("sync", pe, pe.n)
                prog.wait("sync", act, act.n)
                prog.wait("sync", dve, dve.n)
                prog.dma("sync", x25[:].rearrange("p (c t) -> p c t", c=NCH),
                         x2_in[:, t * T3:(t + 1) * T3].rearrange(
                             "(c p) t -> p c t", p=128), dS)
                prog.dma("sync", gt5[:].rearrange("p (c t) -> p c t", c=16),
                         gate_in[:].rearrange("p (c q) -> p c q", c=16)[
                             :, :, t * T3:(t + 1) * T3], dS)
                g_ld = dS.n
                prog.wait("tensor", dve, dve.n)
                prog.op("tensor", lambda e, t=t: e.matmul(
                    bps5[:, 0:T3], ones_f[:], y_row[0:1, t * T3:(t + 1) * T3],
                    start=True, stop=True), inc=pe)
                prog.wait("vector", pe, pe.n)
                prog.wait("vector", dS, g_ld)
                for gc in range(16):
                    prog.op("vector", lambda e, gc=gc: e.tensor_tensor(
                        out=yg5[:, gc * T3:(gc + 1) * T3],
                        in0=gt5[:, gc * T3:(gc + 1) * T3],
                        in1=bps5[:, 0:T3], op=ALU.mult),
                        inc=(dve if gc == 15 else None))
                prog.wait("tensor", dve, dve.n)
                prog.wait("tensor", act, act.n)
                for bb in range(4):
                    prog.op("tensor", lambda e, bb=bb: e.matmul(
                        zps5[:, bb * 512:(bb + 1) * 512], zrow_b5[0:1, 0:128],
                        zrow_b5[:], start=True, stop=False, skip_group_check=True))
                for jc in range(16):
                    for oc in range(NCH):
                        prog.op("tensor", lambda e, jc=jc, oc=oc: e.matmul(
                            zps5[:, oc * T3:(oc + 1) * T3],
                            wout_sb[:, jc * D + oc * 128:jc * D + oc * 128 + 128],
                            yg5[:, jc * T3:(jc + 1) * T3],
                            start=False, stop=(jc == 15), skip_group_check=True),
                            inc=(pe if jc == 15 and oc == NCH - 1 else None))
                prog.wait("scalar", pe, pe.n)
                prog.op("scalar", lambda e: e.activation(
                    out=zsc5[:], in_=zps5[:], func=AF.Identity), inc=act)
                prog.wait("vector", act, act.n)
                prog.wait("vector", dS, g_ld)
                prog.op("vector", lambda e: e.tensor_tensor(
                    out=x25[:], in0=x25[:], in1=zsc5[:], op=ALU.add), inc=dve)
                prog.wait("sync", dve, dve.n)
                prog.dma("sync",
                         g["x3_scr"][:, t * T3:(t + 1) * T3].rearrange(
                             "(c p) t -> p c t", p=128),
                         x25[:].rearrange("p (c t) -> p c t", c=NCH), dS)
    prog.barrier(bar)

    emit_ffn_pass(nc, prog, g, S, cst, ones_f, onesc_f, ident_f,
                  consts_ready, src="x3_scr", dst="o_out", ident_b=ident_bb,
                  onesc_b=onesc_b5)
    prog.barrier(bar)

    prog.wait("gpsimd", dS, dS.n)
    prog.wait("gpsimd", dC, dC.n)
    for name in dbg_outs:
        srcmap = {"d_x3": g["x3_scr"], "d_y": None}
        if name == "d_x3":
            prog.dma("gpsimd", dbg_outs[name], g["x3_scr"], dG)
    prog.wait("gpsimd", dG, dG.n)
    prog.barrier(bar)
    prog.wait("sync", dS, dS.n)
    prog.wait("scalar", dC, dC.n)
    prog.wait("gpsimd", dG, dG.n)

    with nc.Block() as block:
        prog.emit(block)
    stack.close()
    return nc, g


def weight_map(prep):
    return {
        "rwT": prep["rwT"], "rb": prep["rb"], "w1": prep["w1"], "w2": prep["w2"],
        "b1cat": prep["b1cat"], "b1cols": prep["b1cols"], "b2": prep["b2"],
        "identf": prep["identf"], "onesf": prep["onesf"], "onesb": prep["onesb"],
        "identb": prep["identb"],
    }


def in_map_A(prep, b):
    m = weight_map(prep)
    m.update({
        "x_in": prep["x_cores"][b],
        "wqT": prep["wqT"], "wkT": prep["wkT"], "wvT": prep["wvT"], "woT": prep["woT"],
        "ropeq": prep["ropeq"], "ropek": prep["ropek"],
        "wincat": prep["wincat"],
        "wB": prep["wB"], "wC": prep["wC"], "Bb": prep["Bb"], "Cb": prep["Cb"],
        "dtb": prep["dtb"], "Alog": prep["Alog"],
        "identb": prep["identb"],
    })
    return m


def in_map_B(prep, resA_b, aparts):
    m = weight_map(prep)
    m.update({
        "x2_in": np.asarray(resA_b["o_x2"]),
        "gate_in": np.asarray(resA_b["o_gate"]),
        "B_in": np.asarray(resA_b["o_B"]),
        "C_in": np.asarray(resA_b["o_C"]),
        "rows_in": np.asarray(resA_b["o_rows"]),
        "aparts": aparts,
        "wout": prep["wout"],
    })
    return m


def kernel(**inputs):
    prep = host_prep(inputs)
    ncA, gA = build_launch_A()
    mapsA = [in_map_A(prep, b) for b in range(8)]
    resA = run_bass_kernel_spmd(ncA, mapsA, core_ids=list(range(8)))
    aparts = np.concatenate(
        [np.asarray(resA.results[b]["o_apart"], np.float32) for b in range(8)],
        axis=1)
    ncB, gB = build_launch_B()
    mapsB = [in_map_B(prep, resA.results[b], aparts) for b in range(8)]
    resB = run_bass_kernel_spmd(ncB, mapsB, core_ids=list(range(8)))
    out = np.stack([np.ascontiguousarray(np.asarray(resB.results[b]["o_out"]).T)
                    for b in range(8)], axis=0)
    return out



# version 25
# speedup vs baseline: 1.1776x; 1.1776x over previous
import contextlib
import math

import numpy as np
import ml_dtypes

import concourse.bass as bass
import concourse.mybir as mybir
from concourse.bass_utils import run_bass_kernel_spmd

F32 = mybir.dt.float32
BF16 = mybir.dt.bfloat16
AF = mybir.ActivationFunctionType
ALU = mybir.AluOpType
AX = mybir.AxisListType

D = 1024
L = 4096
NCH = 8
NH = 16
DH = 64
NE = 32
HID = 256
E2 = 2048
DSTATE = 128
THETA = 64
EPS = 1e-5
BIG = 32768.0

ENGINES = ("sync", "scalar", "tensor", "vector", "gpsimd")


class Sem:
    def __init__(self, handle):
        self.h = handle
        self.n = 0

    def bump(self, k):
        self.n += k
        return self.n


class Prog:

    def __init__(self, nc):
        self.nc = nc
        self.q = {e: [] for e in ENGINES}

    def op(self, eng, fn, inc=None, k=1, raw=False, dr=False):
        if inc is None:
            self.q[eng].append(fn)
            if eng != "tensor" and not raw:
                self.q[eng].append(lambda e: e.drain())
            return None
        tgt = inc.bump(k)
        h = inc.h
        if raw or eng == "tensor":
            def wrapped(e, fn=fn, h=h, k=k):
                fn(e).then_inc(h, k)
            self.q[eng].append(wrapped)
        else:
            self.q[eng].append(fn)
            self.q[eng].append(lambda e: e.drain())
            self.q[eng].append(lambda e, h=h, k=k: e.nop().then_inc(h, k))
        return tgt

    def wait(self, eng, sem, val=None):
        v = sem.n if val is None else val
        if v <= 0:
            return
        self.q[eng].append(lambda e, h=sem.h, v=v: e.wait_ge(h, v))

    serial_dma = True

    def dma(self, eng, out, in_, sem, serial=True, **kw):
        if self.serial_dma and serial:
            self.wait(eng, sem, sem.n)
        return self.op(eng, lambda e, o=out, i=in_, kw=kw: e.dma_start(out=o, in_=i, **kw),
                       inc=sem, k=16, raw=True)

    def barrier(self, bsem, engines=ENGINES):
        tgt = bsem.n + len(engines)
        for e in engines:
            self.q[e].append(lambda eng: eng.drain())
            self.q[e].append(lambda eng, h=bsem.h: eng.sem_inc(h, 1))
        bsem.n = tgt
        for e in engines:
            self.wait(e, bsem, tgt)

    def emit(self, block):
        prog = self

        @block.sync
        def _(eng):
            for f in prog.q["sync"]:
                f(eng)

        @block.scalar
        def _(eng):
            for f in prog.q["scalar"]:
                f(eng)

        @block.tensor
        def _(eng):
            for f in prog.q["tensor"]:
                f(eng)

        @block.vector
        def _(eng):
            for f in prog.q["vector"]:
                f(eng)

        @block.gpsimd
        def _(eng):
            for f in prog.q["gpsimd"]:
                f(eng)


def host_prep(inputs):
    f32 = np.float32
    bf = ml_dtypes.bfloat16
    d = {}
    x = np.asarray(inputs["L0_attn_a_x"], f32)
    d["x_cores"] = [np.ascontiguousarray(x[b].T) for b in range(8)]

    perm = np.empty(D, np.int64)
    for h in range(NH):
        perm[h * DH:h * DH + 32] = h * DH + 2 * np.arange(32)
        perm[h * DH + 32:h * DH + 64] = h * DH + 2 * np.arange(32) + 1
    d["wqT"] = np.ascontiguousarray(np.asarray(inputs["L0_attn_a_W_q"], f32)[perm].T)
    d["wkT"] = np.ascontiguousarray(np.asarray(inputs["L0_attn_a_W_k"], f32)[perm].T)
    d["wvT"] = np.ascontiguousarray(np.asarray(inputs["L0_attn_a_W_v"], f32).T)
    d["woT"] = np.ascontiguousarray(np.asarray(inputs["L0_attn_a_W_o"], f32).T)

    pos = np.arange(L, dtype=np.float64)[:, None]
    freqs = np.exp(-math.log(10000.0) * np.arange(0, DH, 2, dtype=np.float64) / DH)
    ang = pos * freqs
    cos, sin = np.cos(ang), np.sin(ang)
    d["ropeq"] = np.ascontiguousarray(
        (np.concatenate([cos, sin], 1) * 0.125).astype(f32))
    d["ropek"] = np.ascontiguousarray(np.concatenate([cos, sin], 1).astype(f32))

    d["rwT"] = np.ascontiguousarray(np.asarray(inputs["L0_ffn_router_w"], f32).T)
    d["rb"] = np.asarray(inputs["L0_ffn_router_b"], f32).reshape(1, NE)
    d["w1"] = np.ascontiguousarray(np.asarray(inputs["L0_ffn_weights1"], f32))
    d["w2"] = np.ascontiguousarray(np.asarray(inputs["L0_ffn_weights2"], f32))
    b1 = np.asarray(inputs["L0_ffn_bias1"], f32).reshape(NE * HID)
    d["b1cat"] = np.ascontiguousarray(np.stack([b1, np.ones_like(b1)], 0))
    d["b1cols"] = np.ascontiguousarray(
        b1.reshape(NE, 2, 128).transpose(2, 0, 1).reshape(128, NE * 2))
    d["b2"] = np.ascontiguousarray(np.asarray(inputs["L0_ffn_bias2"], f32))

    win = np.asarray(inputs["L0_attn_b_W_in"], f32)
    sp = np.cumsum([E2, E2, DSTATE, DSTATE, 1, 1, THETA])
    xg_w, gate_w, B_w, C_w, dt_w, Alog_w, theta_w, lam_w = np.split(win, sp, axis=1)
    misc2 = np.zeros((D, 128), f32)
    misc2[:, 0] = dt_w[:, 0]
    misc2[:, 32] = Alog_w[:, 0]
    misc2[:, 64] = lam_w[:, 0]
    d["wincat"] = np.ascontiguousarray(
        np.concatenate([xg_w, gate_w, B_w, C_w, theta_w, misc2], axis=1))
    d["wout"] = np.ascontiguousarray(np.asarray(inputs["L0_attn_b_W_out"], f32))
    d["wB"] = np.asarray(inputs["L0_attn_b_wB"], f32).reshape(DSTATE, 1)
    d["wC"] = np.asarray(inputs["L0_attn_b_wC"], f32).reshape(DSTATE, 1)
    d["Bb"] = np.asarray(inputs["L0_attn_b_B_bias"], f32).reshape(DSTATE, 1)
    d["Cb"] = np.asarray(inputs["L0_attn_b_C_bias"], f32).reshape(DSTATE, 1)
    d["dtb"] = np.asarray(inputs["L0_attn_b_dt_bias"], f32).reshape(1, 1)
    d["Alog"] = np.asarray(inputs["L0_attn_b_A_log"], f32).reshape(1, 1)

    d["identb"] = np.eye(128, dtype=bf)
    d["identf"] = np.eye(128, dtype=f32)
    d["onesf"] = np.ones((1, 128), f32)
    d["onesb"] = np.ones((1, 128), bf)
    return d


T3 = 256


class FfnCtx:
    pass


def make_ffn_ctx(nc, prog, g, S, psb, pps, cst, ones_f, onesc_f, ident_f,
                 consts_ready, partial_scr_name="part_scr", ident_b=None,
                 onesc_b=None):
    c = FfnCtx()
    c.nc, c.prog, c.g_, c.S = nc, prog, g, S
    c.cst, c.ones_f, c.onesc_f, c.ident_f = cst, ones_f, onesc_f, ident_f
    c.ident_b = ident_b
    c.onesc_b = onesc_b
    dS, dC, dG = S["dS"], S["dC"], S["dG"]
    pe, act, dve = S["pe"], S["act"], S["dve"]

    c.h3 = psb("f_h3", [128, 2 * NCH * T3], BF16)
    c.xsq = psb("f_xsq", [128, NCH * T3], BF16)
    c.rows = psb("f_rows", [1, 4 * T3], F32)
    c.mcol = psb("f_mcol", [128, 2], F32)
    c.logit = psb("f_logit", [128, 32], F32)
    c.mx = psb("f_mx", [128, 1], F32)
    c.mtm = psb("f_mtm", [128, 32], BF16)
    c.m_fm = psb("f_mfm", [32, 2 * T3], BF16)
    c.vfm = psb("f_vfm", [32, T3], BF16)
    c.b1cols = psb("f_b1cols", [128, NE * 2], BF16)
    c.mz = psb("f_mz", [2, 128], BF16)
    c.vcr = psb("f_vcr", [2, NE * T3], BF16)[:, :]
    c.b2 = psb("f_b2", [32, D], BF16)
    c.rwT = psb("f_rwT", [128, NCH * 32], F32)
    c.rb_rep = psb("f_rbrep", [128, 32], F32)
    c.sw_rep = psb("f_swrep", [128, 32], F32)
    c.rb_row = psb("f_rbrow", [1, 32], F32)
    c.sw_row = psb("f_swrow", [1, 32], F32)
    c.a1m = psb("f_a1m", [128, 2 * 2 * T3], BF16)
    c.wq = psb("f_wq", [128, 16 * 2048], BF16)
    c.w2q = psb("f_w2q", [128, 16 * 2048], BF16)
    c.part = psb("f_part", [128, 2 * NCH * T3], F32)

    c.z2ps = pps("f_z2ps", [128, NCH * T3], F32)
    c.a1ps = pps("f_a1ps", [128, 2 * 512], F32)
    c.mps = pps("f_mps", [128, 2 * T3], F32)
    c.aps = pps("f_aps", [128, 256], F32)

    c.part_scr = g.get(partial_scr_name)
    c.mrow_scr = g["mrow_scr"]

    prog.dma("gpsimd", c.b1cols[:], g["b1cols"], dG)
    prog.dma("gpsimd", c.b2[:], g["b2"], dG)
    prog.dma("gpsimd", c.rwT[:].rearrange("p (c e) -> p c e", c=NCH),
             g["rwT"].rearrange("(c p) e -> p c e", p=128), dG)
    prog.dma("gpsimd", c.rb_row[:], g["rb"], dG)
    ld = dG.n
    prog.op("vector", lambda e: e.memset(c.vcr[0:1, :], 1.0))
    prog.op("vector", lambda e: e.memset(c.mz[:], 0.0), inc=dve)
    prog.wait("gpsimd", dve, dve.n)
    prog.dma("gpsimd", c.mz[1:2, :], g["onesb"], dG)
    prog.wait("tensor", dG, ld)
    prog.wait("tensor", dS, consts_ready)
    prog.wait("tensor", dve, dve.n)
    for cc in range(NCH):
        prog.op("tensor", lambda e, cc=cc: e.matmul(
            c.aps[0:1, 0:32], onesc_f[:], c.rwT[:, cc * 32:(cc + 1) * 32],
            start=(cc == 0), stop=(cc == NCH - 1)), inc=(pe if cc == NCH - 1 else None))
    prog.wait("vector", pe, pe.n)
    prog.op("vector", lambda e: e.tensor_copy(out=c.sw_row[:], in_=c.aps[0:1, 0:32]),
            inc=dve)
    prog.wait("tensor", dve, dve.n)
    prog.op("tensor", lambda e: e.matmul(
        c.aps[:, 32:64], ones_f[:], c.sw_row[:], start=True, stop=True))
    prog.op("tensor", lambda e: e.matmul(
        c.aps[:, 64:96], ones_f[:], c.rb_row[:], start=True, stop=True), inc=pe)
    prog.wait("vector", pe, pe.n)
    prog.op("vector", lambda e: e.tensor_copy(out=c.sw_rep[:], in_=c.aps[:, 32:64]))
    prog.op("vector", lambda e: e.tensor_copy(out=c.rb_rep[:], in_=c.aps[:, 64:96]),
            inc=dve)
    c.zrow_b = psb("f_zrow", [1, 512], BF16)
    prog.op("vector", lambda e: e.memset(c.zrow_b[:], 0.0), inc=dve)
    c.quarter = -1
    return c


def emit_wcast(prog, g, dW):
    for q in range(4):
        sl = slice(q * 8, (q + 1) * 8)
        prog.dma("gpsimd", g["w1b_scr"][sl], g["w1"][sl], dW, serial=False)
        prog.dma("gpsimd", g["w2b_scr"][sl], g["w2"][sl], dW, serial=False)
    g["_wcast_done"] = dW.n


def ffn_load_half(c, half):
    prog, g = c.prog, c.g_
    dG = c.S["dG"]
    pe = c.S["pe"]
    prog.wait("gpsimd", pe, pe.n)
    marks = []
    for i in range(16):
        e = half * 16 + i
        prog.dma("gpsimd", c.wq[:].rearrange("p (s j f) -> p s j f", s=16, j=NCH)[:, i],
                 g["w1"][e].rearrange("(j p) f -> p j f", p=128), dG, serial=False)
        prog.dma("gpsimd", c.w2q[:].rearrange("p (s f o) -> p s f o", s=16, f=2)[:, i],
                 g["w2"][e].rearrange("(f p) o -> p f o", p=128), dG, serial=False)
        marks.append(dG.n)
    return marks


def emit_ffn_tile(c, x_t, part_ap, t, half, w_ld, ld_val=None, hdr_ld=None):
    prog = c.prog
    S = c.S
    dS, dC, dG = S["dS"], S["dC"], S["dG"]
    pe, act, dve = S["pe"], S["act"], S["dve"]
    cst = c.cst
    sl = t % 2

    if half == 1:
        prog.wait("sync", pe, pe.n)
        prog.dma("sync", c.vcr[1:2, :], c.g_["vrow_scr"][t:t + 1, :], dS)
        vl = dS.n
        prog.wait("tensor", dS, max(vl, hdr_ld if hdr_ld is not None else 0))
        emit_ffn_experts(c, x_t, part_ap, t, half, w_ld, ld_val)
        return
    if ld_val is not None:
        prog.wait("scalar", dS, ld_val)
    prog.wait("scalar", dve, dve.n)
    prog.op("scalar", lambda e: e.activation(out=c.xsq[:], in_=x_t[:],
                                             func=AF.Square), inc=act)
    prog.wait("tensor", act, act.n)
    prog.wait("tensor", dve, dve.n)
    for cc in range(NCH):
        prog.op("tensor", lambda e, cc=cc: e.matmul(
            c.mps[0:1, 0:T3], c.onesc_f[:], x_t[:, cc * T3:(cc + 1) * T3],
            start=(cc == 0), stop=(cc == NCH - 1)))
    for cc in range(NCH):
        prog.op("tensor", lambda e, cc=cc: e.matmul(
            c.mps[0:1, T3:2 * T3], c.onesc_b[:], c.xsq[:, cc * T3:(cc + 1) * T3],
            start=(cc == 0), stop=(cc == NCH - 1)),
            inc=(pe if cc == NCH - 1 else None))
    prog.wait("vector", pe, pe.n)
    prog.op("vector", lambda e: e.tensor_scalar_mul(
        out=c.rows[0:1, 0:T3], in0=c.mps[0:1, 0:T3], scalar1=1.0 / D))
    prog.op("vector", lambda e: e.tensor_scalar_mul(
        out=c.rows[0:1, 2 * T3:3 * T3], in0=c.mps[0:1, T3:2 * T3], scalar1=1.0 / D))
    prog.op("vector", lambda e: e.tensor_tensor(
        out=c.rows[0:1, 3 * T3:4 * T3], in0=c.rows[0:1, 0:T3],
        in1=c.rows[0:1, 0:T3], op=ALU.mult))
    prog.op("vector", lambda e: e.tensor_tensor(
        out=c.rows[0:1, 2 * T3:3 * T3], in0=c.rows[0:1, 2 * T3:3 * T3],
        in1=c.rows[0:1, 3 * T3:4 * T3], op=ALU.subtract), inc=dve)
    prog.wait("scalar", dve, dve.n)
    prog.op("scalar", lambda e: e.activation(
        out=c.rows[0:1, 3 * T3:4 * T3], in_=c.rows[0:1, 2 * T3:3 * T3],
        func=AF.Ln, bias=cst[0:1, 0:1]))
    prog.op("scalar", lambda e: e.activation(
        out=c.rows[0:1, T3:2 * T3], in_=c.rows[0:1, 3 * T3:4 * T3],
        func=AF.Exp, scale=cst[0:1, 2:3]), inc=act)
    prog.wait("tensor", act, act.n)
    prog.op("tensor", lambda e: e.matmul(
        c.mps[:, 0:T3], c.ones_f[:], c.rows[0:1, 0:T3], start=True, stop=True))
    prog.op("tensor", lambda e: e.matmul(
        c.mps[:, T3:2 * T3], c.ones_f[:], c.rows[0:1, T3:2 * T3],
        start=True, stop=True), inc=pe)
    prog.wait("vector", pe, pe.n)
    for cc in range(NCH):
        prog.op("vector", lambda e, cc=cc: e.tensor_tensor(
            out=c.xsq[:, cc * T3:(cc + 1) * T3], in0=x_t[:, cc * T3:(cc + 1) * T3],
            in1=c.mps[:, 0:T3], op=ALU.subtract))
        prog.op("vector", lambda e, cc=cc, sl=sl: e.tensor_tensor(
            out=c.h3[:, sl * NCH * T3 + cc * T3:sl * NCH * T3 + (cc + 1) * T3],
            in0=c.xsq[:, cc * T3:(cc + 1) * T3],
            in1=c.mps[:, T3:2 * T3], op=ALU.mult),
            inc=(dve if cc == NCH - 1 else None))
    h_done = dve.n

    for hh in range(2):
        tok = slice(hh * 128, hh * 128 + 128)
        prog.wait("tensor", dve, dve.n)
        prog.op("tensor", lambda e, hh=hh: e.transpose(
            c.aps[:, 128:129], c.rows[0:1, hh * 128:hh * 128 + 128],
            c.ident_f[0:1, 0:1]))
        prog.op("tensor", lambda e, hh=hh: e.transpose(
            c.aps[:, 129:130], c.rows[0:1, T3 + hh * 128:T3 + hh * 128 + 128],
            c.ident_f[0:1, 0:1]), inc=pe)
        prog.wait("vector", pe, pe.n)
        prog.op("vector", lambda e: e.tensor_copy(out=c.mcol[:], in_=c.aps[:, 128:130]),
                inc=dve)
        prog.wait("tensor", dve, dve.n)
        for cc in range(NCH):
            prog.op("tensor", lambda e, cc=cc, hh=hh: e.matmul(
                c.aps[:, 0:32],
                x_t[:, cc * T3 + hh * 128:cc * T3 + hh * 128 + 128],
                c.rwT[:, cc * 32:(cc + 1) * 32],
                start=(cc == 0), stop=(cc == NCH - 1)),
                inc=(pe if cc == NCH - 1 else None))
        prog.wait("vector", pe, pe.n)
        prog.op("vector", lambda e: e.tensor_scalar(
            out=c.logit[:], in0=c.sw_rep[:], scalar1=c.mcol[:, 0:1], scalar2=-1.0,
            op0=ALU.mult, op1=ALU.mult))
        prog.op("vector", lambda e: e.tensor_tensor(
            out=c.logit[:], in0=c.logit[:], in1=c.aps[:, 0:32], op=ALU.add))
        prog.op("vector", lambda e: e.tensor_scalar(
            out=c.logit[:], in0=c.logit[:], scalar1=c.mcol[:, 1:2], scalar2=None,
            op0=ALU.mult))
        prog.op("vector", lambda e: e.tensor_tensor(
            out=c.logit[:], in0=c.logit[:], in1=c.rb_rep[:], op=ALU.add), dr=True)
        prog.op("vector", lambda e: e.reduce_max(
            out=c.mx[:], in_=c.logit[:], axis=AX.X), dr=True)
        prog.op("vector", lambda e: e.tensor_single_scalar(
            out=c.mtm[:], in_=c.logit[:], scalar=c.mx[:], op=ALU.is_equal), inc=dve)
        prog.wait("tensor", dve, dve.n)
        prog.op("tensor", lambda e: e.transpose(
            c.aps[0:32, 96:160].bitcast(BF16), c.mtm[:], c.ident_b[:]), inc=pe)
        prog.wait("vector", pe, pe.n)
        prog.op("vector", lambda e, hh=hh, sl=sl: e.tensor_copy(
            out=c.m_fm[:, sl * T3 + hh * 128:sl * T3 + hh * 128 + 128],
            in_=c.aps[0:32, 96:160].bitcast(BF16)), inc=dve)
    prog.op("vector", lambda e, sl=sl: e.tensor_scalar(
        out=c.vfm[:], in0=c.m_fm[:, sl * T3:(sl + 1) * T3], scalar1=BIG,
        scalar2=-BIG, op0=ALU.mult, op1=ALU.add), inc=dve)
    prog.wait("sync", dve, dve.n)
    prog.dma("sync", c.mrow_scr, c.vfm[:], dS)
    prog.dma("sync", c.vcr[1:2, :],
             c.mrow_scr.rearrange("a b -> (a b)").unsqueeze(0), dS)
    prog.dma("sync", c.g_["h_scr"][:].rearrange("p (c q) -> p c q", c=NCH)[
        :, :, t * T3:(t + 1) * T3],
        c.h3[:, sl * NCH * T3:(sl + 1) * NCH * T3].rearrange(
            "p (c t) -> p c t", c=NCH), dS)
    prog.dma("sync", c.g_["mfm_scr"][t * 32:(t + 1) * 32, :],
             c.m_fm[:, sl * T3:(sl + 1) * T3], dS)
    vcr_ld = dS.n
    prog.dma("sync", c.g_["vrow_scr"][t:t + 1, :], c.vcr[1:2, :], dS)
    emit_ffn_experts(c, x_t, part_ap, t, half, w_ld, ld_val, vcr_ld)


def emit_ffn_experts(c, x_t, part_ap, t, half, w_ld, ld_val=None, vcr_ld=None):
    prog = c.prog
    S = c.S
    dS, dC, dG = S["dS"], S["dC"], S["dG"]
    pe, act, dve = S["pe"], S["act"], S["dve"]
    sl = t % 2
    if vcr_ld is None:
        vcr_ld = dS.n
    wl = w_ld if isinstance(w_ld, (list, tuple)) else [w_ld] * 16
    prog.wait("tensor", dS, vcr_ld)
    prog.wait("tensor", dve, dve.n)
    for bb in range(4):
        prog.op("tensor", lambda e, bb=bb: e.matmul(
            c.z2ps[:, bb * 512:(bb + 1) * 512], c.zrow_b[0:1, 0:128], c.zrow_b[:],
            start=True, stop=False, skip_group_check=True))
    relu_done = {0: 0, 1: 0}

    def emit_s2(i, slot, rv):
        prog.wait("tensor", act, rv)
        for fc in range(2):
            for oc in range(NCH):
                prog.op("tensor", lambda e, i=i, fc=fc, oc=oc, slot=slot, half=half:
                        e.matmul(
                    c.z2ps[:, oc * T3:(oc + 1) * T3],
                    c.w2q[:, (i * 2 + fc) * D + oc * 128:(i * 2 + fc) * D + oc * 128 + 128],
                    c.a1m[:, slot * 512 + fc * T3:slot * 512 + (fc + 1) * T3],
                    start=False,
                    stop=False, skip_group_check=True),
                    inc=(pe if (half == 0 and i == 15 and fc == 1 and oc == NCH - 1)
                         else None))

    s2_pending = None
    for i in range(16):
        e_g = half * 16 + i
        slot = i % 2
        prog.wait("tensor", dG, wl[i])
        prog.wait("tensor", act, relu_done[slot])
        for fc in range(2):
            for jc in range(NCH):
                prog.op("tensor", lambda e, i=i, fc=fc, jc=jc, slot=slot, sl=sl:
                        e.matmul(
                    c.a1ps[:, slot * 512 + fc * T3:slot * 512 + (fc + 1) * T3],
                    c.wq[:, (i * NCH + jc) * T3 + fc * 128:
                         (i * NCH + jc) * T3 + fc * 128 + 128],
                    c.h3[:, sl * NCH * T3 + jc * T3:sl * NCH * T3 + (jc + 1) * T3],
                    start=(jc == 0), stop=False, skip_group_check=True))
            prog.op("tensor", lambda e, e_g=e_g, fc=fc, slot=slot, sl=sl: e.matmul(
                c.a1ps[:, slot * 512 + fc * T3:slot * 512 + (fc + 1) * T3],
                c.mz[:, 0:128],
                c.vcr[:, e_g * T3:(e_g + 1) * T3],
                start=False, stop=True, skip_group_check=True),
                inc=(pe if fc == 1 else None))
        s1_done = pe.n
        prog.wait("scalar", pe, s1_done)
        for fc in range(2):
            prog.op("scalar", lambda e, slot=slot, fc=fc, e_g=e_g: e.activation(
                out=c.a1m[:, slot * 512 + fc * T3:slot * 512 + (fc + 1) * T3],
                in_=c.a1ps[:, slot * 512 + fc * T3:slot * 512 + (fc + 1) * T3],
                func=AF.Relu, bias=c.b1cols[:, e_g * 2 + fc:e_g * 2 + fc + 1]),
                inc=(act if fc == 1 else None))
        relu_done[slot] = act.n
        if s2_pending is not None:
            emit_s2(*s2_pending)
        s2_pending = (i, slot, relu_done[slot])
    emit_s2(*s2_pending)
    if half == 0:
        prog.wait("scalar", pe, pe.n)
        prog.op("scalar", lambda e, part_ap=part_ap: e.activation(
            out=part_ap, in_=c.z2ps[:], func=AF.Copy), inc=act)
        prog.wait("sync", act, act.n)
        prog.dma("sync", c.part_scr[:, t * T3:(t + 1) * T3].rearrange(
            "(c p) t -> p c t", p=128),
            part_ap.rearrange("p (c t) -> p c t", c=NCH), dS)
    else:
        prog.wait("tensor", dve, dve.n)
        for oc in range(NCH):
            prog.op("tensor", lambda e, oc=oc, sl=sl: e.matmul(
                c.z2ps[:, oc * T3:(oc + 1) * T3],
                c.b2[:, oc * 128:(oc + 1) * 128], c.m_fm[:, sl * T3:(sl + 1) * T3],
                start=False, stop=True, skip_group_check=True),
                inc=(pe if oc == NCH - 1 else None))
        prog.wait("scalar", dS, ld_val if ld_val is not None else dS.n)
        prog.wait("scalar", pe, pe.n)
        prog.wait("vector", pe, pe.n)
        prog.op("vector", lambda e, part_ap=part_ap: e.tensor_tensor(
            out=part_ap, in0=c.z2ps[:], in1=part_ap, op=ALU.add))
        for cc in range(NCH):
            prog.op("vector", lambda e, cc=cc, part_ap=part_ap: e.scalar_tensor_tensor(
                out=x_t[:, cc * T3:(cc + 1) * T3],
                in0=part_ap[:, cc * T3:(cc + 1) * T3], scalar=0.0,
                in1=x_t[:, cc * T3:(cc + 1) * T3],
                op0=ALU.max, op1=ALU.add),
                inc=(dve if cc == NCH - 1 else None))


def emit_ffn_pass(nc, prog, g, S, cst, ones_f, onesc_f, ident_f, consts_ready,
                  src, dst, ident_b=None, onesc_b=None):
    dS = S["dS"]
    pe, act, dve = S["pe"], S["act"], S["dve"]
    with contextlib.ExitStack() as ph:
        psb = lambda name, shape, dt: ph.enter_context(nc.sbuf_tensor(name, shape, dt))
        pps = lambda name, shape, dt: ph.enter_context(nc.psum_tensor(name, shape, dt))
        c = make_ffn_ctx(nc, prog, g, S, psb, pps,
                         cst=cst, ones_f=ones_f, onesc_f=onesc_f,
                         ident_f=ident_f, ident_b=ident_b, onesc_b=onesc_b,
                         consts_ready=consts_ready)
        x_t3 = psb("x_t3", [128, 2 * NCH * T3], F32)
        xb = lambda t: x_t3[:, (t % 2) * NCH * T3:((t % 2) + 1) * NCH * T3]
        NT = L // T3
        buf_marks = {0: (0, 0, 0), 1: (0, 0, 0)}
        load_val = {}
        part_val = {}

        def issue_load(t):
            mp, ma, md = buf_marks[t % 2]
            prog.wait("sync", S["pe"], mp)
            prog.wait("sync", S["act"], ma)
            prog.wait("sync", S["dve"], md)
            prog.dma("sync", xb(t).rearrange("p (c t) -> p c t", c=NCH),
                     g[src][:, t * T3:(t + 1) * T3].rearrange(
                         "(c p) t -> p c t", p=128), dS)
            load_val[t] = dS.n

        pv = lambda t: c.part[:, (t % 2) * NCH * T3:((t % 2) + 1) * NCH * T3]
        hdr_val = {}

        def issue_part(t):
            prog.wait("sync", S["dve"], dve.n)
            prog.dma("sync", pv(t).rearrange("p (c t) -> p c t", c=NCH),
                     g["part_scr"][:, t * T3:(t + 1) * T3].rearrange(
                         "(c p) t -> p c t", p=128), dS)
            part_val[t] = dS.n

        def issue_hdr(t):
            sl = t % 2
            prog.wait("sync", pe, pe.n)
            prog.dma("sync",
                     c.h3[:, sl * NCH * T3:(sl + 1) * NCH * T3].rearrange(
                         "p (c t) -> p c t", c=NCH),
                     g["h_scr"][:].rearrange("p (c q) -> p c q", c=NCH)[
                         :, :, t * T3:(t + 1) * T3], dS)
            prog.dma("sync", c.m_fm[:, sl * T3:(sl + 1) * T3],
                     g["mfm_scr"][t * 32:(t + 1) * 32, :], dS)
            hdr_val[t] = dS.n

        for half in range(2):
            w_ld = ffn_load_half(c, half)
            issue_load(0)
            if half == 1:
                issue_part(0)
                issue_hdr(0)
            for t in range(NT):
                if t + 1 < NT:
                    issue_load(t + 1)
                    if half == 1:
                        issue_part(t + 1)
                        issue_hdr(t + 1)
                prog.wait("tensor", dS, load_val[t])
                prog.wait("vector", dS, load_val[t])
                prog.wait("scalar", dS, load_val[t])
                lv = max(load_val[t], part_val.get(t, 0))
                emit_ffn_tile(c, xb(t), pv(t), t, half, w_ld, ld_val=lv,
                              hdr_ld=hdr_val.get(t))
                buf_marks[t % 2] = (pe.n, act.n, dve.n)
                if half == 1:
                    prog.wait("sync", dve, dve.n)
                    prog.dma("sync",
                             g[dst][:, t * T3:(t + 1) * T3].rearrange(
                                 "(c p) t -> p c t", p=128),
                             xb(t).rearrange("p (c t) -> p c t", c=NCH), dS)



def build_launch_A(phases=("P1", "P2", "P3", "P4"), dbg=(), p2_mode="full", p2_tiles=None, p2_heads=NH, p2_groups=None, x1_input=False):
    nc = bass.Bass("TRN2", target_bir_lowering=False, dynamic_dma_scratch_size=16384)
    T1 = 128
    NT1 = L // T1
    T2 = 512
    NT2 = L // T2

    g = {}

    def din(name, shape, dt):
        g[name] = nc.dram_tensor(name, shape, dt, kind="ExternalInput").ap()
        return g[name]

    def dout(name, shape, dt):
        g[name] = nc.dram_tensor(name, shape, dt, kind="ExternalOutput").ap()
        return g[name]

    def dscr(name, shape, dt):
        g[name] = nc.dram_tensor(name, shape, dt, kind="Internal").ap()
        return g[name]

    x_in = din("x_in", [D, L], F32)
    for nm, shp in (("wqT", [D, D]), ("wkT", [D, D]), ("wvT", [D, D]), ("woT", [D, D])):
        din(nm, shp, F32)
    ropeq = din("ropeq", [L, 64], F32)
    ropek = din("ropek", [L, 64], F32)
    din("rwT", [D, NE], F32)
    din("rb", [1, NE], F32)
    din("w1", [NE, D, HID], F32)
    din("w2", [NE, HID, D], F32)
    din("b1cat", [2, NE * HID], F32)
    din("b1cols", [128, NE * 2], F32)
    din("b2", [NE, D], F32)
    din("wincat", [D, 4544], F32)
    din("wB", [DSTATE, 1], F32)
    din("wC", [DSTATE, 1], F32)
    din("Bb", [DSTATE, 1], F32)
    din("Cb", [DSTATE, 1], F32)
    din("dtb", [1, 1], F32)
    din("Alog", [1, 1], F32)
    identb = din("identb", [128, 128], BF16)
    identf = din("identf", [128, 128], F32)
    onesf = din("onesf", [1, 128], F32)
    onesb = din("onesb", [1, 128], BF16)

    o_x2 = dout("o_x2", [D, L], F32)
    o_gate = dout("o_gate", [128, 16 * L], BF16)
    o_B = dout("o_B", [DSTATE, L], F32)
    o_C = dout("o_C", [DSTATE, L], F32)
    o_rows = dout("o_rows", [2, L], F32)
    o_apart = dout("o_apart", [1, 1], F32)

    qfm_scr = dscr("qfm_scr", [128, NCH * L], BF16)
    dscr("w1b_scr", [NE, D, HID], BF16)
    dscr("w2b_scr", [NE, HID, D], BF16)
    x1_scr = (din("x1_scr", [D, L], F32) if x1_input
              else dscr("x1_scr", [D, L], F32))
    g["x1_scr"] = x1_scr
    mrow_scr = dscr("mrow_scr", [NE, 256], BF16)
    g["h_scr"] = qfm_scr
    dscr("mfm_scr", [512, 256], BF16)
    dscr("vrow_scr", [16, NE * T3], BF16)
    row_scr = dscr("row_scr", [6, L], F32)
    g["row_scr"] = row_scr
    dup_scr = dscr("dup_scr", [64, L], F32)
    g["dup_scr"] = dup_scr
    part_scr = dscr("part_scr", [D, L], F32)
    g["part_scr"] = part_scr
    g["mrow_scr"] = mrow_scr

    dbg_outs = {}
    for name, shape, dt in dbg:
        dbg_outs[name] = dout(name, shape, dt)

    prog = Prog(nc)
    stack = contextlib.ExitStack()
    sb = lambda name, shape, dt: stack.enter_context(nc.sbuf_tensor(name, shape, dt))

    S = {n: Sem(stack.enter_context(nc.semaphore(n))) for n in
         ("dS", "dC", "dG", "dW", "pe", "act", "dve", "bar")}
    dS, dC, dG, pe, act, dve, bar = (S[n] for n in ("dS", "dC", "dG", "pe", "act", "dve", "bar"))

    ident_b = sb("ident_b", [128, 128], BF16)
    ident_f = sb("ident_f", [128, 128], F32)
    ones_f = sb("ones_f", [1, 128], F32)
    ones_b = sb("ones_b", [1, 128], BF16)
    prog.dma("sync", ident_b[:], identb, dS)
    prog.dma("sync", ident_f[:], identf, dS)
    prog.dma("sync", ones_f[:], onesf, dS)
    prog.dma("sync", ones_b[:], onesb, dS)
    consts_ready = dS.n
    cst = sb("cst", [128, 8], F32)
    onesc_f = sb("onesc_f", [128, 1], F32)
    onesc_b = sb("onesc_b", [128, 1], BF16)
    prog.op("vector", lambda e: e.memset(onesc_f[:], 1.0))
    prog.op("vector", lambda e: e.memset(onesc_b[:], 1.0))
    prog.op("vector", lambda e: e.memset(cst[:, 0:1], EPS))
    prog.op("vector", lambda e: e.memset(cst[:, 1:2], 0.5))
    prog.op("vector", lambda e: e.memset(cst[:, 2:3], -0.5))
    prog.op("vector", lambda e: e.memset(cst[:, 3:4], 1.0))
    prog.op("vector", lambda e: e.memset(cst[:, 4:5], -1.0))
    prog.op("vector", lambda e: e.memset(cst[:, 5:6], 1.5707963267948966))
    prog.op("vector", lambda e: e.memset(cst[:, 6:7], 0.15915494309189535))
    prog.op("vector", lambda e: e.memset(cst[:, 7:8], 0.0), inc=dve)
    cst_ready = dve.n
    prog.wait("scalar", dve, cst_ready)

    attn_stack = contextlib.ExitStack()
    asb = lambda name, shape, dt: attn_stack.enter_context(nc.sbuf_tensor(name, shape, dt))
    k_sb = asb("k_sb", [128, NCH * L], BF16)
    v_sb = asb("v_sb", [128, 32 * NH * 65], BF16)
    qmax_sb = asb("qmax_sb", [16, NT1], F32)
    kmax_sb = asb("kmax_sb", [16, 1], F32)

    if "P1" in phases:
        with contextlib.ExitStack() as ph:
            psb = lambda name, shape, dt: ph.enter_context(nc.sbuf_tensor(name, shape, dt))
            pps = lambda name, shape, dt: ph.enter_context(nc.psum_tensor(name, shape, dt))

            wq_sb = psb("wq_sb", [128, NCH * D], BF16)
            wk_sb = psb("wk_sb", [128, NCH * D], BF16)
            wv_sb = psb("wv_sb", [128, NCH * D], BF16)
            for nm, dst in (("wqT", wq_sb), ("wkT", wk_sb), ("wvT", wv_sb)):
                prog.dma("gpsimd", dst[:].rearrange("p (c i) -> p c i", c=NCH),
                         g[nm].rearrange("(c p) i -> p c i", p=128), dG, serial=False)
            w_ld = dG.n
            prog.op("vector", lambda e: e.memset(
                v_sb[:].rearrange("p (a b) -> p a b", b=65)[:, :, 64:65], 1.0), inc=dve)
            vones_done = dve.n

            x_t1 = psb("x_t1", [128, D], F32)
            xsq_t1 = psb("xsq_t1", [128, D], F32)
            h_t1 = psb("h_t1", [128, D], BF16)
            ropeq_t = psb("ropeq_t", [128, 64], F32)
            ropek_t = psb("ropek_t", [128, 64], F32)
            rope_s1 = psb("rope_s1", [128, 512], F32)
            rope_s2 = psb("rope_s2", [128, 512], F32)
            qrot = psb("qrot", [128, D], BF16)
            krot = psb("krot", [128, D], BF16)
            st_row = psb("st_row", [1, 6 * T1], F32)
            nrm_sc = psb("nrm_sc", [128, 16], F32)
            nrm_row = psb("nrm_row", [16, 1], F32)
            qfm_stage = psb("qfm_stage", [128, NCH * T1], BF16)

            ps_misc = pps("ps_misc1", [128, 512], F32)
            ps_q = pps("ps_q1", [128, 1024], F32)
            ps_k = pps("ps_k1", [128, 1024], F32)
            ps_v = pps("ps_v1", [128, 1024], F32)
            ps_tr = pps("ps_tr1", [128, 1024], BF16)

            prog.wait("tensor", dS, consts_ready)
            prog.wait("tensor", dG, w_ld)

            for t in range(NT1):
                tok = slice(t * T1, (t + 1) * T1)
                prog.wait("tensor", act, act.n)
                prog.wait("tensor", dve, dve.n)
                prog.wait("sync", pe, pe.n)
                prog.wait("sync", dve, dve.n)
                prog.wait("sync", act, act.n)
                prog.dma("sync", x_t1[:].rearrange("p (c t) -> p c t", c=NCH),
                         x_in[:, tok].rearrange("(c p) t -> p c t", p=128), dS)
                ldx = dS.n
                prog.wait("scalar", dve, dve.n)
                prog.dma("scalar", ropeq_t[:], ropeq[tok, :], dC)
                prog.dma("scalar", ropek_t[:], ropek[tok, :], dC)
                ldr = dC.n

                prog.wait("scalar", dS, ldx)
                prog.op("scalar", lambda e: e.activation(
                    out=xsq_t1[:], in_=x_t1[:], func=AF.Square), inc=act)
                sq_done = act.n
                prog.wait("tensor", dS, ldx)
                prog.wait("tensor", act, sq_done)
                prog.wait("tensor", dve, dve.n)
                for c in range(NCH):
                    prog.op("tensor", lambda e, c=c: e.matmul(
                        ps_misc[0:1, 0:T1], onesc_f[:], x_t1[:, c * T1:(c + 1) * T1],
                        start=(c == 0), stop=(c == NCH - 1)))
                for c in range(NCH):
                    prog.op("tensor", lambda e, c=c: e.matmul(
                        ps_misc[0:1, T1:2 * T1], onesc_f[:], xsq_t1[:, c * T1:(c + 1) * T1],
                        start=(c == 0), stop=(c == NCH - 1)),
                        inc=(pe if c == NCH - 1 else None))
                stats_done = pe.n
                prog.wait("vector", pe, stats_done)
                prog.op("vector", lambda e: e.tensor_scalar_mul(
                    out=st_row[0:1, 0:T1], in0=ps_misc[0:1, 0:T1], scalar1=1.0 / D))
                prog.op("vector", lambda e: e.tensor_scalar_mul(
                    out=st_row[0:1, 2 * T1:3 * T1], in0=ps_misc[0:1, T1:2 * T1],
                    scalar1=1.0 / D))
                prog.op("vector", lambda e: e.tensor_tensor(
                    out=st_row[0:1, 3 * T1:4 * T1], in0=st_row[0:1, 0:T1],
                    in1=st_row[0:1, 0:T1], op=ALU.mult))
                prog.op("vector", lambda e: e.tensor_tensor(
                    out=st_row[0:1, 2 * T1:3 * T1], in0=st_row[0:1, 2 * T1:3 * T1],
                    in1=st_row[0:1, 3 * T1:4 * T1], op=ALU.subtract), inc=dve)
                var_done = dve.n
                prog.wait("scalar", dve, var_done)
                prog.op("scalar", lambda e: e.activation(
                    out=st_row[0:1, 4 * T1:5 * T1], in_=st_row[0:1, 2 * T1:3 * T1],
                    func=AF.Ln, bias=cst[0:1, 0:1]))
                prog.op("scalar", lambda e: e.activation(
                    out=st_row[0:1, T1:2 * T1], in_=st_row[0:1, 4 * T1:5 * T1],
                    func=AF.Exp, scale=cst[0:1, 2:3]), inc=act)
                rstd_done = act.n

                prog.wait("tensor", act, rstd_done)
                prog.op("tensor", lambda e: e.matmul(
                    ps_misc[:, 2 * T1:3 * T1], ones_f[:], st_row[0:1, 0:T1],
                    start=True, stop=True))
                prog.op("tensor", lambda e: e.matmul(
                    ps_misc[:, 3 * T1:4 * T1], ones_f[:], st_row[0:1, T1:2 * T1],
                    start=True, stop=True), inc=pe)
                bc_done = pe.n
                prog.wait("vector", pe, bc_done)
                for c in range(NCH):
                    prog.op("vector", lambda e, c=c: e.tensor_tensor(
                        out=xsq_t1[:, c * T1:(c + 1) * T1],
                        in0=x_t1[:, c * T1:(c + 1) * T1],
                        in1=ps_misc[:, 2 * T1:3 * T1], op=ALU.subtract))
                    prog.op("vector", lambda e, c=c: e.tensor_tensor(
                        out=h_t1[:, c * T1:(c + 1) * T1],
                        in0=xsq_t1[:, c * T1:(c + 1) * T1],
                        in1=ps_misc[:, 3 * T1:4 * T1], op=ALU.mult),
                        inc=(dve if c == NCH - 1 else None))
                h_done = dve.n

                prog.wait("tensor", dve, h_done)
                projs = [(w_s, ps_o, half, c)
                         for (w_s, ps_o) in ((wq_sb, ps_q), (wk_sb, ps_k), (wv_sb, ps_v))
                         for half in range(2) for c in range(NCH)]
                for i, (w_s, ps_o, half, c) in enumerate(projs):
                    prog.op("tensor",
                            lambda e, w_s=w_s, ps_o=ps_o, half=half, c=c:
                            e.matmul(ps_o[:, half * 512:(half + 1) * 512],
                                     h_t1[:, c * T1:(c + 1) * T1],
                                     w_s[:, c * D + half * 512:
                                         c * D + (half + 1) * 512],
                                     start=(c == 0), stop=(c == NCH - 1)),
                            inc=(pe if i == len(projs) - 1 else None))
                proj_done = pe.n

                prog.wait("scalar", pe, proj_done)
                prog.op("scalar", lambda e: e.activation(
                    out=xsq_t1[:], in_=ps_q[:], func=AF.Square))
                prog.op("scalar", lambda e: e.activation(
                    out=x_t1[:], in_=ps_k[:], func=AF.Square), inc=act)
                nsq_done = act.n
                prog.wait("vector", act, nsq_done)
                for which, src in (("q", xsq_t1), ("k", x_t1)):
                    prog.op("vector", lambda e, src=src: e.reduce_sum(
                        out=nrm_sc[:], in_=src[:].rearrange("p (h d) -> p h d", h=16),
                        axis=AX.X), inc=dve)
                    rdone = dve.n
                    prog.wait("tensor", dve, rdone)
                    prog.op("tensor", lambda e: e.transpose(
                        ps_misc[0:16, 0:T1], nrm_sc[:], ident_f[:]), inc=pe)
                    tdone = pe.n
                    prog.wait("vector", pe, tdone)
                    if which == "q":
                        prog.op("vector", lambda e, t=t: e.reduce_max(
                            out=qmax_sb[:, t:t + 1], in_=ps_misc[0:16, 0:T1],
                            axis=AX.X), inc=dve)
                    else:
                        prog.op("vector", lambda e: e.reduce_max(
                            out=nrm_row[:], in_=ps_misc[0:16, 0:T1], axis=AX.X),
                            inc=dve)
                        if t == 0:
                            prog.op("vector", lambda e: e.tensor_copy(
                                out=kmax_sb[:], in_=nrm_row[:]), inc=dve)
                        else:
                            prog.op("vector", lambda e: e.tensor_tensor(
                                out=kmax_sb[:], in0=kmax_sb[:], in1=nrm_row[:],
                                op=ALU.max), inc=dve)
                norm_done = dve.n

                prog.wait("vector", dC, ldr)
                for (psrc, rtab, rout) in ((ps_q, ropeq_t, qrot), (ps_k, ropek_t, krot)):
                    pv = psrc[:].rearrange("p (h y) -> p h y", h=16)
                    evv = pv[:, :, 0:32]
                    odv = pv[:, :, 32:64]
                    cosv = rtab[:, 0:32].unsqueeze(1).broadcast_to([128, 16, 32])
                    sinv = rtab[:, 32:64].unsqueeze(1).broadcast_to([128, 16, 32])
                    ov = rout[:].rearrange("p (h y) -> p h y", h=16)
                    rev = ov[:, :, 0:32]
                    rodv = ov[:, :, 32:64]
                    s1v = rope_s1[:].rearrange("p (h d) -> p h d", h=16)
                    s2v = rope_s2[:].rearrange("p (h d) -> p h d", h=16)
                    prog.op("vector", lambda e, a=evv, c=cosv, o=s1v: e.tensor_tensor(
                        out=o, in0=a, in1=c, op=ALU.mult))
                    prog.op("vector", lambda e, a=odv, s=sinv, o=s2v: e.tensor_tensor(
                        out=o, in0=a, in1=s, op=ALU.mult))
                    prog.op("vector", lambda e, o=rev, a=s1v, b=s2v: e.tensor_tensor(
                        out=o, in0=a, in1=b, op=ALU.subtract))
                    prog.op("vector", lambda e, a=evv, s=sinv, o=s1v: e.tensor_tensor(
                        out=o, in0=a, in1=s, op=ALU.mult))
                    prog.op("vector", lambda e, a=odv, c=cosv, o=s2v: e.tensor_tensor(
                        out=o, in0=a, in1=c, op=ALU.mult))
                    prog.op("vector", lambda e, o=rodv, a=s1v, b=s2v: e.tensor_tensor(
                        out=o, in0=a, in1=b, op=ALU.add), inc=dve)
                rope_done = dve.n

                if t == 0:
                    prog.wait("scalar", dve, vones_done)
                prog.wait("scalar", dve, rope_done)
                prog.op("scalar", lambda e, t=t: e.activation(
                    out=v_sb[:].rearrange("p (a b) -> p a b", b=65)[
                        :, t * 16:(t + 1) * 16, 0:64],
                    in_=ps_v[:].rearrange("p (h d) -> p h d", h=16),
                    func=AF.Copy), inc=act)
                v_done = act.n

                prog.wait("tensor", dve, rope_done)
                for c in range(NCH):
                    prog.op("tensor", lambda e, c=c: e.transpose(
                        ps_tr[:, c * 128:(c + 1) * 128],
                        qrot[:, c * 128:(c + 1) * 128], ident_b[:]),
                        inc=(pe if c == NCH - 1 else None))
                prog.wait("vector", pe, pe.n)
                prog.op("vector", lambda e: e.tensor_copy(
                    out=qfm_stage[:], in_=ps_tr[:]), inc=dve)
                prog.wait("tensor", dve, dve.n)
                for c in range(NCH):
                    prog.op("tensor", lambda e, c=c: e.transpose(
                        ps_tr[:, c * 128:(c + 1) * 128],
                        krot[:, c * 128:(c + 1) * 128], ident_b[:]),
                        inc=(pe if c == NCH - 1 else None))
                prog.wait("vector", pe, pe.n)
                prog.op("vector", lambda e, t=t: e.tensor_copy(
                    out=k_sb[:].rearrange("p (c q) -> p c q", c=NCH)[
                        :, :, t * T1:(t + 1) * T1],
                    in_=ps_tr[:].rearrange("p (c q) -> p c q", c=NCH)), inc=dve)
                tr_done = dve.n
                prog.wait("sync", dve, tr_done)
                prog.wait("sync", act, v_done)
                prog.dma("sync",
                         qfm_scr[:].rearrange("p (c q) -> p c q", c=NCH)[
                             :, :, t * T1:(t + 1) * T1],
                         qfm_stage[:].rearrange("p (c t) -> p c t", c=NCH), dS)
                prog.wait("tensor", dS, dS.n)

        prog.barrier(bar)

    if "P2" in phases:
        with contextlib.ExitStack() as ph:
            psb = lambda name, shape, dt: ph.enter_context(nc.sbuf_tensor(name, shape, dt))
            pps = lambda name, shape, dt: ph.enter_context(nc.psum_tensor(name, shape, dt))

            wo_sb = psb("wo_sb", [128, NCH * D], BF16)
            prog.dma("gpsimd", wo_sb[:].rearrange("p (c i) -> p c i", c=NCH),
                     g["woT"].rearrange("(c p) i -> p c i", p=128), dG)
            wo_ld = dG.n

            q_pad = psb("q_pad", [128, 8 * T2], BF16)
            x_t2 = psb("x_t2", [128, NCH * T2], F32)
            pt_sb = psb("pt_sb", [128, 2 * 3 * T2], BF16)
            attn_sb = psb("attn_sb", [128, NCH * T2], BF16)
            l_sb = psb("l_sb", [1, NH * T2], BF16)
            cbuf = psb("cbuf", [16, 4], F32)
            crow = psb("crow", [1, 16], F32)
            negc = psb("negc", [128, 16], F32)

            ps_scA = pps("ps_scA", [128, 3 * T2], F32)
            ps_scB = pps("ps_scB", [128, 3 * T2], F32)
            ps_av = pps("ps_av", [128, T2], F32)
            ps_rb = pps("ps_rb", [128, T2], F32)

            NKC = L // 128
            groups = [(i, min(3, NKC - i)) for i in range(0, NKC, 3)]

            prog.wait("tensor", dG, wo_ld)
            prog.op("vector", lambda e: e.memset(q_pad[:], 0.0), inc=dve)
            prog.wait("scalar", dve, dve.n)
            NT2e = NT2 if p2_tiles is None else p2_tiles
            reg_dve_guard = {0: 0, 1: 0}
            for t in range(NT2e):
                prog.wait("sync", dve, dve.n)
                prog.dma("sync", x_t2[:].rearrange("p (c t) -> p c t", c=NCH),
                         x_in[:, t * T2:(t + 1) * T2].rearrange(
                             "(c p) t -> p c t", p=128), dS)
                prog.wait("scalar", pe, pe.n)
                qsrc = qfm_scr.rearrange("p (c q) -> p c q", c=NCH)
                qdst = q_pad[:].rearrange("p (h q) -> p h q", h=8)

                def load_qpad(h0, t=t):
                    for h in range(h0, h0 + 8):
                        pb = (h % 2) * 64
                        prog.dma("scalar", qdst[pb:pb + 64, (h % 8):(h % 8) + 1, :],
                                 qsrc[pb:pb + 64, (h // 2):(h // 2) + 1,
                                      t * T2:(t + 1) * T2], dC, serial=False)
                    return dC.n

                ldq = load_qpad(0)
                x_v = x_t2[:]
                ldx_t = dS.n

                prog.wait("vector", pe, pe.n)
                prog.op("vector", lambda e, t=t: e.reduce_max(
                    out=cbuf[:, 0:1], in_=qmax_sb[:, t * 4:(t + 1) * 4], axis=AX.X),
                    dr=True)
                prog.op("vector", lambda e: e.tensor_tensor(
                    out=cbuf[:, 1:2], in0=cbuf[:, 0:1], in1=kmax_sb[:], op=ALU.mult))
                prog.op("vector", lambda e: e.tensor_scalar_max(
                    out=cbuf[:, 2:3], in0=cbuf[:, 1:2], scalar1=1e-30), inc=dve)
                prog.wait("scalar", dve, dve.n)
                prog.op("scalar", lambda e: e.activation(
                    out=cbuf[:, 2:3], in_=cbuf[:, 2:3], func=AF.Ln))
                prog.op("scalar", lambda e: e.activation(
                    out=cbuf[:, 3:4], in_=cbuf[:, 2:3], func=AF.Exp, scale=cst[0:16, 1:2]),
                    inc=act)
                prog.wait("vector", act, act.n)
                prog.op("vector", lambda e: e.tensor_scalar_mul(
                    out=cbuf[:, 3:4], in0=cbuf[:, 3:4], scalar1=0.125), inc=dve)
                prog.wait("tensor", dve, dve.n)
                prog.op("tensor", lambda e: e.transpose(
                    ps_rb[0:1, 0:16], cbuf[0:16, 3:4], ident_f[0:16, 0:16]), inc=pe)
                prog.wait("vector", pe, pe.n)
                prog.op("vector", lambda e: e.tensor_copy(
                    out=crow[:], in_=ps_rb[0:1, 0:16]), inc=dve)
                prog.wait("tensor", dve, dve.n)
                prog.op("tensor", lambda e: e.matmul(
                    ps_rb[:, 16:32], ones_f[:], crow[:], start=True, stop=True), inc=pe)
                prog.wait("vector", pe, pe.n)
                prog.op("vector", lambda e: e.tensor_scalar_mul(
                    out=negc[:], in0=ps_rb[:, 16:32], scalar1=-1.0), inc=dve)
                negc_done = dve.n
                prog.wait("scalar", dve, negc_done)
                prog.wait("tensor", dC, ldq)
                prog.wait("tensor", dS, ldx_t)

                if p2_mode == "cbound":
                    continue
                reg_exp_done = {0: 0, 1: 0}
                slot_av_done = {0: 0, 1: 0}
                head_evac = [0]
                last_sc = [0]
                for h in range(p2_heads):
                    kbase = (h % 2) * 64
                    koff = (h // 2) * L
                    if h == 8:
                        prog.wait("scalar", pe, last_sc[0])
                        ldq = load_qpad(8)
                        prog.wait("tensor", dC, ldq)
                    exp_of = {}

                    def emit_scores(gi, g0, glen):
                        reg = ps_scA if (gi % 2 == 0) else ps_scB
                        prog.wait("tensor", act, reg_exp_done[gi % 2])
                        if gi <= 1:
                            prog.wait("tensor", dve, reg_dve_guard[gi % 2])
                        for j in range(glen):
                            kc = g0 + j
                            prog.op("tensor",
                                    lambda e, reg=reg, j=j, kc=kc, h=h,
                                    koff=koff:
                                    e.matmul(reg[:, j * T2:(j + 1) * T2],
                                             k_sb[:, koff + kc * 128:
                                                  koff + (kc + 1) * 128],
                                             q_pad[:, (h % 8) * T2:
                                                   (h % 8 + 1) * T2],
                                             start=True, stop=True,
                                             skip_group_check=True),
                                    inc=(pe if j == glen - 1 else None))
                        last_sc[0] = pe.n
                        return pe.n

                    def emit_exp(gi, glen, sc_done):
                        reg = ps_scA if (gi % 2 == 0) else ps_scB
                        prog.wait("scalar", pe, max(sc_done, slot_av_done[gi % 2]))
                        prog.op("scalar",
                                lambda e, reg=reg, glen=glen, gi=gi, h=h:
                                e.activation(
                                    out=pt_sb[:, (gi % 2) * 3 * T2:
                                              (gi % 2) * 3 * T2 + glen * T2],
                                    in_=reg[:, 0:glen * T2],
                                    func=AF.Exp, bias=negc[:, h:h + 1]), inc=act)
                        reg_exp_done[gi % 2] = act.n
                        exp_of[gi] = act.n

                    def emit_av(gi, g0, glen):
                        prog.wait("tensor", act, exp_of[gi])
                        if gi == 0:
                            prog.wait("tensor", dve, head_evac[0])
                        for j in range(glen):
                            kc = g0 + j
                            prog.op("tensor", lambda e, kc=kc, j=j, gi=gi, h=h:
                                    e.matmul(ps_av[0:65, :],
                                             v_sb[:, (kc * 16 + h) * 65:
                                                  (kc * 16 + h) * 65 + 65],
                                             pt_sb[:, (gi % 2) * 3 * T2 + j * T2:
                                                   (gi % 2) * 3 * T2 + (j + 1) * T2],
                                             start=(kc == 0), stop=(kc == NKC - 1),
                                             skip_group_check=True),
                                    inc=(pe if j == glen - 1 else None))
                        slot_av_done[gi % 2] = pe.n

                    gl = groups[:p2_groups]
                    for gi, (g0, glen) in enumerate(gl):
                        sc_done = emit_scores(gi, g0, glen)
                        if p2_mode == "sconly":
                            continue
                        emit_exp(gi, glen, sc_done)
                        if p2_mode in ("scores",):
                            continue
                        if gi >= 1:
                            emit_av(gi - 1, gl[gi - 1][0], gl[gi - 1][1])
                    if p2_mode in ("scores", "sconly"):
                        continue
                    emit_av(len(gl) - 1, gl[-1][0], gl[-1][1])
                    av_done = pe.n
                    prog.wait("vector", pe, av_done)
                    prog.op("vector", lambda e, h=h: e.tensor_copy(
                        out=attn_sb[(h % 2) * 64:(h % 2) * 64 + 64,
                                    (h // 2) * T2:(h // 2 + 1) * T2],
                        in_=ps_av[0:64, :]))
                    prog.op("vector", lambda e, h=h: e.tensor_copy(
                        out=l_sb[0:1, h * T2:(h + 1) * T2], in_=ps_av[64:65, :]),
                        inc=dve)
                    head_evac[0] = dve.n

                if p2_mode != "full":
                    continue
                prog.wait("tensor", dve, dve.n)
                rcp = pt_sb[:, 0:T2]
                for c in range(NCH):
                    prog.op("tensor", lambda e, c=c: e.matmul(
                        ps_rb[0:64, :], ones_b[0:1, 0:64],
                        l_sb[0:1, (2 * c) * T2:(2 * c + 1) * T2],
                        start=True, stop=True))
                    prog.op("tensor", lambda e, c=c: e.matmul(
                        ps_rb[64:128, :], ones_b[0:1, 0:64],
                        l_sb[0:1, (2 * c + 1) * T2:(2 * c + 2) * T2],
                        start=True, stop=True, tile_position=(0, 64)), inc=pe)
                    rb_done = pe.n
                    prog.wait("vector", pe, rb_done)

                    def _recip(e, rcp=rcp):
                        with e.bass.allow_low_precision(reason="bf16 softmax denom"):
                            return e.reciprocal(out=rcp, in_=ps_rb[:])
                    prog.op("vector", _recip)
                    prog.op("vector", lambda e, c=c, rcp=rcp: e.tensor_tensor(
                        out=attn_sb[:, c * T2:(c + 1) * T2],
                        in0=attn_sb[:, c * T2:(c + 1) * T2],
                        in1=rcp, op=ALU.mult), inc=dve)
                    prog.wait("tensor", dve, dve.n)
                prog.wait("scalar", dve, dve.n)
                prog.wait("tensor", dS, ldx_t)
                wo_evac = {0: 0, 1: 0}
                for oc in range(NCH):
                    wops = ps_scA if oc % 2 == 0 else ps_scB
                    prog.wait("tensor", dve, wo_evac[oc % 2])
                    for jc in range(NCH):
                        prog.op("tensor", lambda e, oc=oc, jc=jc, wops=wops: e.matmul(
                            wops[:, 0:T2],
                            wo_sb[:, jc * D + oc * 128:jc * D + oc * 128 + 128],
                            attn_sb[:, jc * T2:(jc + 1) * T2],
                            start=(jc == 0), stop=(jc == NCH - 1),
                            skip_group_check=True),
                            inc=(pe if jc == NCH - 1 else None))
                    wo_done = pe.n
                    prog.wait("vector", pe, wo_done)
                    prog.op("vector", lambda e, oc=oc, x_v=x_v, wops=wops: e.tensor_tensor(
                        out=x_v[:, oc * T2:(oc + 1) * T2],
                        in0=x_v[:, oc * T2:(oc + 1) * T2],
                        in1=wops[:, 0:T2], op=ALU.add), inc=dve)
                    wo_evac[oc % 2] = dve.n
                    reg_dve_guard[oc % 2] = dve.n
                x1_done = dve.n
                prog.wait("sync", dve, x1_done)
                prog.dma("sync",
                         x1_scr[:, t * T2:(t + 1) * T2].rearrange(
                             "(c p) t -> p c t", p=128),
                         x_v.rearrange("p (c t) -> p c t", c=NCH), dS)
                prog.wait("vector", dS, dS.n)

            prog.barrier(bar)
            if "d_cbuf" in dbg_outs:
                prog.dma("gpsimd", dbg_outs["d_cbuf"], cbuf[:], dG)
            if "d_crow" in dbg_outs:
                prog.dma("gpsimd", dbg_outs["d_crow"], crow[:], dG)
            if "d_qmax" in dbg_outs:
                prog.dma("gpsimd", dbg_outs["d_qmax"], qmax_sb[:], dG)
            if "d_kmax" in dbg_outs:
                prog.dma("gpsimd", dbg_outs["d_kmax"], kmax_sb[:], dG)
            if "d_negc" in dbg_outs:
                prog.dma("gpsimd", dbg_outs["d_negc"], negc[:], dG)
            if "d_pt" in dbg_outs:
                prog.dma("gpsimd", dbg_outs["d_pt"], pt_sb[:], dG)
            if "d_attn" in dbg_outs:
                prog.dma("gpsimd", dbg_outs["d_attn"], attn_sb[:], dG)
            if "d_l" in dbg_outs:
                prog.dma("gpsimd", dbg_outs["d_l"], l_sb[:], dG)
            prog.wait("gpsimd", dG, dG.n)
        prog.barrier(bar)

    attn_stack.close()

    if "P3" in phases:
        emit_ffn_pass(nc, prog, g, S, cst, ones_f, onesc_f, ident_f,
                      consts_ready, src="x1_scr", dst="o_x2", ident_b=ident_b,
                      onesc_b=onesc_b)
        prog.barrier(bar)


    if "P4" in phases:
        with contextlib.ExitStack() as p4s:
            qsb = lambda name, shape, dt: p4s.enter_context(nc.sbuf_tensor(name, shape, dt))
            B_raw = qsb("B_raw", [128, L], F32)
            C_raw = qsb("C_raw", [128, L], F32)
            th_sb = qsb("th_sb", [64, L], F32)
            mean_row = qsb("mean_row", [1, L], F32)
            rstd_row = qsb("rstd_row", [1, L], F32)
            FR = qsb("FR", [128, 16 * 32], F32)
            fr = lambda i: FR[:, i * 32:(i + 1) * 32]
            dtb_col = qsb("dtb_col", [128, 1], F32)
            alog_col = qsb("alog_col", [128, 1], F32)
            wb_sb = qsb("wb_sb", [128, 4], F32)
            m1_sb = qsb("m1_sb", [1, 2], F32)
            aps_hold = p4s.enter_context(nc.psum_tensor("aps_hold", [128, 512], F32))

            with contextlib.ExitStack() as ph:
                psb = lambda name, shape, dt: ph.enter_context(nc.sbuf_tensor(name, shape, dt))
                pps = lambda name, shape, dt: ph.enter_context(nc.psum_tensor(name, shape, dt))
                x_t4 = psb("x_t4", [128, NCH * T3], F32)
                xsq4 = psb("xsq4", [128, NCH * T3], F32)
                srow4 = psb("srow4", [1, 4 * T3], F32)
                mps4 = pps("mps4", [128, 2 * T3], F32)
                for t in range(L // T3):
                    prog.wait("sync", pe, pe.n)
                    prog.wait("sync", act, act.n)
                    prog.wait("sync", dve, dve.n)
                    prog.dma("sync", x_t4[:].rearrange("p (c t) -> p c t", c=NCH),
                             g["o_x2"][:, t * T3:(t + 1) * T3].rearrange(
                                 "(c p) t -> p c t", p=128), dS)
                    prog.wait("scalar", dS, dS.n)
                    prog.op("scalar", lambda e: e.activation(
                        out=xsq4[:], in_=x_t4[:], func=AF.Square), inc=act)
                    prog.wait("tensor", dS, dS.n)
                    prog.wait("tensor", act, act.n)
                    prog.wait("tensor", dve, dve.n)
                    for cc in range(NCH):
                        prog.op("tensor", lambda e, cc=cc: e.matmul(
                            mps4[0:1, 0:T3], onesc_f[:], x_t4[:, cc * T3:(cc + 1) * T3],
                            start=(cc == 0), stop=(cc == NCH - 1)))
                    for cc in range(NCH):
                        prog.op("tensor", lambda e, cc=cc: e.matmul(
                            mps4[0:1, T3:2 * T3], onesc_f[:], xsq4[:, cc * T3:(cc + 1) * T3],
                            start=(cc == 0), stop=(cc == NCH - 1)),
                            inc=(pe if cc == NCH - 1 else None))
                    prog.wait("vector", pe, pe.n)
                    prog.op("vector", lambda e, t=t: e.tensor_scalar_mul(
                        out=mean_row[0:1, t * T3:(t + 1) * T3], in0=mps4[0:1, 0:T3],
                        scalar1=1.0 / D))
                    prog.op("vector", lambda e: e.tensor_scalar_mul(
                        out=srow4[0:1, 0:T3], in0=mps4[0:1, T3:2 * T3], scalar1=1.0 / D))
                    prog.op("vector", lambda e, t=t: e.tensor_tensor(
                        out=srow4[0:1, T3:2 * T3],
                        in0=mean_row[0:1, t * T3:(t + 1) * T3],
                        in1=mean_row[0:1, t * T3:(t + 1) * T3], op=ALU.mult))
                    prog.op("vector", lambda e: e.tensor_tensor(
                        out=srow4[0:1, 0:T3], in0=srow4[0:1, 0:T3],
                        in1=srow4[0:1, T3:2 * T3], op=ALU.subtract), inc=dve)
                    prog.wait("scalar", dve, dve.n)
                    prog.op("scalar", lambda e: e.activation(
                        out=srow4[0:1, 2 * T3:3 * T3], in_=srow4[0:1, 0:T3],
                        func=AF.Ln, bias=cst[0:1, 0:1]))
                    prog.op("scalar", lambda e, t=t: e.activation(
                        out=rstd_row[0:1, t * T3:(t + 1) * T3],
                        in_=srow4[0:1, 2 * T3:3 * T3],
                        func=AF.Exp, scale=cst[0:1, 2:3]), inc=act)

            prog.barrier(bar)
            with contextlib.ExitStack() as ph:
                psb = lambda name, shape, dt: ph.enter_context(nc.sbuf_tensor(name, shape, dt))
                pps = lambda name, shape, dt: ph.enter_context(nc.psum_tensor(name, shape, dt))
                win_sb = psb("win_sb", [128, NCH * 4544], BF16)
                misc_sb = psb("misc_sb", [128, L], F32)
                xb_t4 = psb("x_t4b", [128, NCH * T3], F32)
                h4 = psb("h4", [128, NCH * T3], BF16)
                hsc = psb("hsc", [128, T3], F32)
                xs4 = psb("xs4", [128, 2 * T3], F32)
                gst = psb("gst", [128, 2 * T3], BF16)
                mst = psb("mst", [1, T3], F32)
                zps = pps("zps", [128, 1024], F32)
                aps4 = pps("aps4", [128, 1024], F32)
                prog.dma("gpsimd", win_sb[:].rearrange("p (j f) -> p j f", j=NCH),
                         g["wincat"].rearrange("(j p) f -> p j f", p=128), dG, serial=False)
                win_ld = dG.n
                prog.wait("tensor", dG, win_ld)

                gst_store = [0]
                zchunks = ([("xg", f * 128, 128) for f in range(16)]
                           + [("gate", 2048 + f * 128, 128) for f in range(16)]
                           + [("B", 4096, 128), ("C", 4224, 128),
                              ("th", 4352, 64), ("mi", 4416, 128)])
                for t in range(L // T3):
                    prog.wait("sync", pe, pe.n)
                    prog.wait("sync", act, act.n)
                    prog.wait("sync", dve, dve.n)
                    prog.dma("sync", xb_t4[:].rearrange("p (c t) -> p c t", c=NCH),
                             g["o_x2"][:, t * T3:(t + 1) * T3].rearrange(
                                 "(c p) t -> p c t", p=128), dS)
                    prog.wait("tensor", dS, dS.n)
                    prog.wait("tensor", dve, dve.n)
                    prog.wait("tensor", act, act.n)
                    prog.op("tensor", lambda e, t=t: e.matmul(
                        aps4[:, 0:T3], ones_f[:], mean_row[0:1, t * T3:(t + 1) * T3],
                        start=True, stop=True))
                    prog.op("tensor", lambda e, t=t: e.matmul(
                        aps4[:, T3:2 * T3], ones_f[:], rstd_row[0:1, t * T3:(t + 1) * T3],
                        start=True, stop=True), inc=pe)
                    prog.wait("vector", pe, pe.n)
                    prog.wait("vector", dS, dS.n)
                    for cc in range(NCH):
                        prog.op("vector", lambda e, cc=cc: e.tensor_tensor(
                            out=hsc[:], in0=xb_t4[:, cc * T3:(cc + 1) * T3],
                            in1=aps4[:, 0:T3], op=ALU.subtract))
                        prog.op("vector", lambda e, cc=cc: e.tensor_tensor(
                            out=h4[:, cc * T3:(cc + 1) * T3], in0=hsc[:],
                            in1=aps4[:, T3:2 * T3], op=ALU.mult),
                            inc=(dve if cc == NCH - 1 else None))
                    prog.wait("tensor", dve, dve.n)
                    nmean = 0
                    evac_done = {0: 0, 1: 0}
                    for ci, (nm, col0, width) in enumerate(zchunks):
                        slot = ci % 2
                        prog.wait("tensor", act, evac_done[slot])
                        for jc in range(NCH):
                            prog.op("tensor", lambda e, jc=jc, col0=col0, width=width, slot=slot:
                                    e.matmul(zps[0:width, slot * 512:slot * 512 + T3],
                                             win_sb[:, jc * 4544 + col0:jc * 4544 + col0 + width],
                                             h4[:, jc * T3:(jc + 1) * T3],
                                             start=(jc == 0), stop=(jc == NCH - 1)),
                                    inc=(pe if jc == NCH - 1 else None))
                        prog.wait("scalar", pe, pe.n)
                        if nm == "xg":
                            xslot = ci % 2
                            prog.op("scalar", lambda e, slot=slot, xslot=xslot: e.activation(
                                out=xs4[:, xslot * T3:(xslot + 1) * T3],
                                in_=zps[:, slot * 512:slot * 512 + T3],
                                func=AF.Silu), inc=act)
                            prog.wait("tensor", act, act.n)
                            prog.op("tensor", lambda e, xslot=xslot, ci=ci: e.matmul(
                                aps4[0:1, 512:512 + T3],
                                onesc_f[:], xs4[:, xslot * T3:(xslot + 1) * T3],
                                start=(ci == 0), stop=(ci == 15), skip_group_check=True),
                                inc=(pe if ci == 15 else None))
                            nmean += 1
                        elif nm == "gate":
                            gi = ci - 16
                            gslot = gi % 2
                            if gslot == 0:
                                prog.wait("scalar", dS, gst_store[0])
                            prog.op("scalar", lambda e, slot=slot, gslot=gslot: e.activation(
                                out=gst[:, gslot * T3:(gslot + 1) * T3],
                                in_=zps[:, slot * 512:slot * 512 + T3],
                                func=AF.Silu), inc=act)
                            if gslot == 1:
                                prog.wait("sync", act, act.n)
                                prog.dma("sync",
                                         g["o_gate"][:].rearrange(
                                             "p (c q) -> p c q", c=16)[
                                             :, gi - 1:gi + 1, t * T3:(t + 1) * T3],
                                         gst[:].rearrange("p (c q) -> p c q", c=2), dS)
                                gst_store[0] = dS.n
                        elif nm == "B":
                            prog.op("scalar", lambda e, slot=slot, t=t: e.activation(
                                out=B_raw[:, t * T3:(t + 1) * T3],
                                in_=zps[:, slot * 512:slot * 512 + T3],
                                func=AF.Identity), inc=act)
                        elif nm == "C":
                            prog.op("scalar", lambda e, slot=slot, t=t: e.activation(
                                out=C_raw[:, t * T3:(t + 1) * T3],
                                in_=zps[:, slot * 512:slot * 512 + T3],
                                func=AF.Identity), inc=act)
                        elif nm == "th":
                            prog.op("scalar", lambda e, slot=slot, t=t: e.activation(
                                out=th_sb[:, t * T3:(t + 1) * T3],
                                in_=zps[0:64, slot * 512:slot * 512 + T3],
                                func=AF.Identity), inc=act)
                        else:
                            prog.op("scalar", lambda e, slot=slot, t=t: e.activation(
                                out=misc_sb[:, t * T3:(t + 1) * T3],
                                in_=zps[:, slot * 512:slot * 512 + T3],
                                func=AF.Identity), inc=act)
                        evac_done[slot] = act.n
                    prog.wait("vector", pe, pe.n)
                    prog.wait("vector", dS, dS.n)
                    prog.op("vector", lambda e: e.tensor_scalar_mul(
                        out=mst[:], in0=aps4[0:1, 512:512 + T3], scalar1=1.0 / E2),
                        inc=dve)
                    prog.wait("sync", dve, dve.n)
                    prog.dma("sync", g["row_scr"][0:1, t * T3:(t + 1) * T3], mst[:], dS)
                prog.wait("sync", act, act.n)
                prog.dma("sync", g["row_scr"][1:2, :], misc_sb[0:1, :], dS)
                prog.dma("sync", g["row_scr"][2:3, :], misc_sb[32:33, :], dS)
                prog.dma("sync", g["row_scr"][3:4, :], misc_sb[64:65, :], dS)

            prog.barrier(bar)
            prog.dma("sync", g["row_scr"][4:5, 1:L], g["row_scr"][0:1, 0:L - 1], dS)
            prog.op("vector", lambda e: e.memset(fr(15)[0:1, 0:1], 0.0), inc=dve)
            prog.wait("sync", dve, dve.n)
            prog.dma("sync", g["row_scr"][4:5, 0:1], fr(15)[0:1, 0:1], dS)
            prog.wait("scalar", dS, dS.n)
            for i, r in ((0, 1), (1, 2), (2, 3), (3, 0), (8, 4)):
                prog.dma("scalar", fr(i),
                         g["row_scr"][r:r + 1, :].rearrange("a (p f) -> (a p) f", p=128),
                         dC)
            prog.dma("scalar", dtb_col[0:1, :], g["dtb"], dC)
            prog.dma("scalar", alog_col[0:1, :], g["Alog"], dC)
            prog.dma("scalar", wb_sb[:, 0:1], g["wB"], dC)
            prog.dma("scalar", wb_sb[:, 1:2], g["Bb"], dC)
            prog.dma("scalar", wb_sb[:, 2:3], g["wC"], dC)
            prog.dma("scalar", wb_sb[:, 3:4], g["Cb"], dC)
            rows_ld = dC.n
            prog.wait("tensor", dC, rows_ld)
            prog.op("tensor", lambda e: e.matmul(
                aps_hold[:, 0:1], ones_f[:], dtb_col[0:1, 0:1], start=True, stop=True))
            prog.op("tensor", lambda e: e.matmul(
                aps_hold[:, 1:2], ones_f[:], alog_col[0:1, 0:1], start=True, stop=True),
                inc=pe)
            prog.wait("vector", pe, pe.n)
            prog.op("vector", lambda e: e.tensor_copy(out=dtb_col[:], in_=aps_hold[:, 0:1]))
            prog.op("vector", lambda e: e.tensor_copy(out=alog_col[:], in_=aps_hold[:, 1:2]),
                    inc=dve)

            def softplus_fold(dst, srci, bias_col):
                prog.op("vector", lambda e: e.tensor_scalar(
                    out=fr(11), in0=fr(srci), scalar1=bias_col[:, 0:1], scalar2=None,
                    op0=ALU.add), inc=dve)
                prog.wait("scalar", dve, dve.n)
                prog.op("scalar", lambda e: e.activation(
                    out=fr(12), in_=fr(11), func=AF.Abs))
                prog.op("scalar", lambda e: e.activation(
                    out=fr(12), in_=fr(12), func=AF.Exp, scale=cst[:, 4:5]))
                prog.op("scalar", lambda e: e.activation(
                    out=fr(12), in_=fr(12), func=AF.Ln, bias=cst[:, 3:4]), inc=act)
                prog.wait("vector", act, act.n)
                prog.op("vector", lambda e: e.tensor_scalar_max(
                    out=fr(13), in0=fr(11), scalar1=0.0))
                prog.op("vector", lambda e, dst=dst: e.tensor_tensor(
                    out=fr(dst), in0=fr(13), in1=fr(12), op=ALU.add), inc=dve)

            prog.wait("vector", dC, rows_ld)
            softplus_fold(4, 0, dtb_col)
            softplus_fold(5, 1, alog_col)
            prog.op("vector", lambda e: e.tensor_scalar_mul(
                out=fr(5), in0=fr(5), scalar1=-1.0), inc=dve)
            prog.wait("scalar", dve, dve.n)
            prog.op("scalar", lambda e: e.activation(
                out=fr(6), in_=fr(2), func=AF.Exp, scale=cst[:, 4:5]), inc=act)
            prog.wait("vector", act, act.n)
            prog.op("vector", lambda e: e.tensor_scalar_add(
                out=fr(6), in0=fr(6), scalar1=1.0), dr=True)
            prog.op("vector", lambda e: e.reciprocal(out=fr(6), in_=fr(6)), inc=dve)
            prog.op("vector", lambda e: e.tensor_tensor(
                out=fr(7), in0=fr(4), in1=fr(5), op=ALU.mult), inc=dve)
            prog.wait("scalar", dve, dve.n)
            prog.op("scalar", lambda e: e.activation(
                out=fr(7), in_=fr(7), func=AF.Exp), inc=act)
            prog.wait("tensor", dve, dve.n)
            prog.op("tensor", lambda e: e.matmul(
                aps_hold[0:1, 32:64], onesc_f[:], fr(5), start=True, stop=True), inc=pe)
            prog.wait("vector", pe, pe.n)
            prog.op("vector", lambda e: e.reduce_sum(
                out=m1_sb[0:1, 0:1], in_=aps_hold[0:1, 32:64], axis=AX.X), inc=dve)
            prog.wait("sync", dve, dve.n)
            prog.dma("sync", g["o_apart"], m1_sb[0:1, 0:1], dS)
            prog.wait("vector", act, act.n)
            prog.op("vector", lambda e: e.tensor_tensor(
                out=fr(9), in0=fr(6), in1=fr(3), op=ALU.mult))
            prog.op("vector", lambda e: e.tensor_scalar(
                out=fr(11), in0=fr(6), scalar1=-1.0, scalar2=1.0,
                op0=ALU.mult, op1=ALU.add))
            prog.op("vector", lambda e: e.tensor_tensor(
                out=fr(11), in0=fr(11), in1=fr(7), op=ALU.mult))
            prog.op("vector", lambda e: e.tensor_tensor(
                out=fr(11), in0=fr(11), in1=fr(8), op=ALU.mult))
            prog.op("vector", lambda e: e.tensor_tensor(
                out=fr(9), in0=fr(9), in1=fr(11), op=ALU.add))
            prog.op("vector", lambda e: e.tensor_tensor(
                out=fr(10), in0=fr(4), in1=fr(9), op=ALU.mult), inc=dve)
            prog.wait("sync", dve, dve.n)
            prog.dma("sync", g["o_rows"][0:1, :].rearrange("a (p f) -> (a p) f", p=128),
                     fr(4), dS)
            prog.dma("sync", g["o_rows"][1:2, :].rearrange("a (p f) -> (a p) f", p=128),
                     fr(10), dS)
            prog.dma("sync", g["row_scr"][5:6, :].rearrange("a (p f) -> (a p) f", p=128),
                     fr(4), dS)
            dt_row_st = dS.n

            prog.barrier(bar)
            with contextlib.ExitStack() as ph:
                psb = lambda name, shape, dt: ph.enter_context(nc.sbuf_tensor(name, shape, dt))
                pps = lambda name, shape, dt: ph.enter_context(nc.psum_tensor(name, shape, dt))
                bsq = psb("bsq", [128, L], F32)
                s3 = psb("s3", [64, L], F32)
                cosv = psb("cosv", [64, L], F32)
                sinv = psb("sinv", [64, L], F32)
                rrow = psb("rrow", [1, L], F32)
                irow = psb("irow", [64, L], mybir.dt.int32)
                rps = pps("rps", [128, 512], F32)
                rps2 = pps("rps2", [128, 512], F32)

                for (raw, wcol, bcol) in ((B_raw, 0, 1), (C_raw, 2, 3)):
                    prog.wait("scalar", dS, dS.n)
                    prog.wait("scalar", dve, dve.n)
                    prog.op("scalar", lambda e, raw=raw: e.activation(
                        out=bsq[:], in_=raw[:], func=AF.Square), inc=act)
                    for cc in range(NCH):
                        prog.wait("tensor", act, act.n)
                        prog.wait("tensor", dve, dve.n)
                        prog.op("tensor", lambda e, cc=cc: e.matmul(
                            rps[0:1, 0:512], onesc_f[:], bsq[:, cc * 512:(cc + 1) * 512],
                            start=True, stop=True), inc=pe)
                        prog.wait("vector", pe, pe.n)
                        prog.op("vector", lambda e, cc=cc: e.tensor_scalar_mul(
                            out=rrow[0:1, cc * 512:(cc + 1) * 512], in0=rps[0:1, 0:512],
                            scalar1=1.0 / DSTATE), inc=dve)
                    prog.wait("scalar", dve, dve.n)
                    prog.op("scalar", lambda e: e.activation(
                        out=rrow[:], in_=rrow[:], func=AF.Ln, bias=cst[0:1, 0:1]))
                    prog.op("scalar", lambda e: e.activation(
                        out=rrow[:], in_=rrow[:], func=AF.Exp, scale=cst[0:1, 2:3]),
                        inc=act)
                    prog.wait("tensor", act, act.n)
                    for cc in range(NCH):
                        prog.op("tensor", lambda e, cc=cc: e.matmul(
                            rps2[:, 0:512], ones_f[:], rrow[0:1, cc * 512:(cc + 1) * 512],
                            start=True, stop=True), inc=pe)
                        prog.wait("vector", pe, pe.n)
                        prog.op("vector", lambda e, cc=cc, raw=raw: e.tensor_tensor(
                            out=raw[:, cc * 512:(cc + 1) * 512],
                            in0=raw[:, cc * 512:(cc + 1) * 512],
                            in1=rps2[:, 0:512], op=ALU.mult), inc=dve)
                        prog.wait("tensor", dve, dve.n)
                    prog.wait("scalar", dve, dve.n)
                    prog.op("scalar", lambda e, raw=raw, wcol=wcol, bcol=bcol: e.activation(
                        out=raw[:], in_=raw[:], func=AF.Identity,
                        scale=wb_sb[:, wcol:wcol + 1], bias=wb_sb[:, bcol:bcol + 1]),
                        inc=act)

                prog.wait("scalar", dS, dt_row_st)
                prog.dma("scalar", rrow[:], g["row_scr"][5:6, :], dC)
                prog.wait("tensor", dC, dC.n)
                for cc in range(NCH):
                    prog.op("tensor", lambda e, cc=cc: e.matmul(
                        rps[0:64, 0:512], ones_f[0:1, 0:64],
                        rrow[0:1, cc * 512:(cc + 1) * 512],
                        start=True, stop=True), inc=pe)
                    prog.wait("vector", pe, pe.n)
                    prog.op("vector", lambda e, cc=cc: e.tensor_tensor(
                        out=th_sb[:, cc * 512:(cc + 1) * 512],
                        in0=th_sb[:, cc * 512:(cc + 1) * 512],
                        in1=rps[0:64, 0:512], op=ALU.mult), inc=dve)
                    prog.wait("tensor", dve, dve.n)
                prog.op("vector", lambda e: e.memset(s3[:], 0.0), dr=True)
                prog.op("vector", lambda e: e.tensor_tensor_scan(
                    out=th_sb[:], data0=th_sb[:], data1=s3[:],
                    initial=0.0, op0=ALU.add, op1=ALU.add), inc=dve)
                prog.op("vector", lambda e: e.tensor_scalar_mul(
                    out=s3[:], in0=th_sb[:], scalar1=0.15915494309189535), dr=True)
                prog.op("vector", lambda e: e.tensor_copy(out=irow[:], in_=s3[:]), dr=True)
                prog.op("vector", lambda e: e.tensor_copy(out=s3[:], in_=irow[:]), dr=True)
                prog.op("vector", lambda e: e.scalar_tensor_tensor(
                    out=bsq[0:64, :], in0=s3[:], scalar=-6.28125, in1=th_sb[:],
                    op0=ALU.mult, op1=ALU.add), dr=True)
                prog.op("vector", lambda e: e.scalar_tensor_tensor(
                    out=bsq[0:64, :], in0=s3[:], scalar=-0.0019353071795864769,
                    in1=bsq[0:64, :], op0=ALU.mult, op1=ALU.add), inc=dve)
                prog.wait("scalar", dve, dve.n)
                prog.op("scalar", lambda e: e.activation(
                    out=s3[:], in_=bsq[0:64, :], func=AF.Sign))
                prog.op("scalar", lambda e: e.activation(
                    out=irow[:].bitcast(F32), in_=bsq[0:64, :], func=AF.Abs), inc=act)
                prog.wait("vector", act, act.n)
                prog.op("vector", lambda e: e.tensor_single_scalar(
                    out=irow[:].bitcast(F32), in_=irow[:].bitcast(F32),
                    scalar=3.1415927410125732, op=ALU.is_gt))
                prog.op("vector", lambda e: e.tensor_tensor(
                    out=s3[:], in0=s3[:], in1=irow[:].bitcast(F32), op=ALU.mult),
                    dr=True)
                prog.op("vector", lambda e: e.scalar_tensor_tensor(
                    out=bsq[0:64, :], in0=s3[:], scalar=-6.283185307179586,
                    in1=bsq[0:64, :], op0=ALU.mult, op1=ALU.add), inc=dve)
                prog.wait("scalar", dve, dve.n)
                prog.op("scalar", lambda e: e.activation(
                    out=sinv[:], in_=bsq[0:64, :], func=AF.Sin))
                prog.op("scalar", lambda e: e.activation(
                    out=s3[:], in_=bsq[0:64, :], func=AF.Abs), inc=act)
                prog.wait("scalar", act, act.n)
                prog.op("scalar", lambda e: e.activation(
                    out=cosv[:], in_=s3[:], func=AF.Sin,
                    scale=cst[0:64, 4:5], bias=cst[0:64, 5:6]), inc=act)

                sA = psb("sA", [64, L], F32)
                prog.wait("vector", act, act.n)
                for raw, out_name in ((B_raw, "o_B"), (C_raw, "o_C")):
                    prog.wait("sync", dve, dve.n)
                    prog.wait("sync", act, act.n)
                    prog.dma("sync", g["dup_scr"], raw[64:128, :], dS)
                    prog.dma("sync", s3[:], g["dup_scr"], dS)
                    prog.wait("vector", dS, dS.n)
                    prog.op("vector", lambda e, raw=raw: e.tensor_tensor(
                        out=sA[:], in0=raw[0:64, :], in1=sinv[:], op=ALU.mult))
                    prog.op("vector", lambda e: e.tensor_tensor(
                        out=bsq[0:64, :], in0=s3[:], in1=cosv[:], op=ALU.mult))
                    prog.op("vector", lambda e: e.tensor_tensor(
                        out=sA[:], in0=sA[:], in1=bsq[0:64, :], op=ALU.add), dr=True)
                    prog.op("vector", lambda e, raw=raw: e.tensor_tensor(
                        out=bsq[0:64, :], in0=raw[0:64, :], in1=cosv[:], op=ALU.mult))
                    prog.op("vector", lambda e: e.tensor_tensor(
                        out=s3[:], in0=s3[:], in1=sinv[:], op=ALU.mult), dr=True)
                    prog.op("vector", lambda e, raw=raw: e.tensor_tensor(
                        out=raw[0:64, :], in0=bsq[0:64, :], in1=s3[:],
                        op=ALU.subtract), inc=dve)
                    prog.wait("sync", dve, dve.n)
                    prog.dma("sync", g["dup_scr"], sA[:], dS)
                    prog.dma("sync", raw[64:128, :], g["dup_scr"], dS)
                    prog.wait("sync", dS, dS.n)
                    prog.dma("sync", g[out_name], raw[:], dS)
        prog.barrier(bar)
    prog.wait("gpsimd", dS, dS.n)
    prog.wait("gpsimd", dC, dC.n)
    for name in dbg_outs:
        src = {"d_x1": x1_scr, "d_qfm": qfm_scr, "d_x2": g["o_x2"]}.get(name)
        if src is not None:
            prog.dma("gpsimd", dbg_outs[name], src, dG)
    prog.wait("gpsimd", dG, dG.n)
    prog.barrier(bar)
    prog.wait("sync", dS, dS.n)
    prog.wait("scalar", dC, dC.n)
    prog.wait("gpsimd", dG, dG.n)

    with nc.Block() as block:
        prog.emit(block)
    stack.close()
    return nc, g




def build_launch_B(dbg=()):
    nc = bass.Bass("TRN2", target_bir_lowering=False, dynamic_dma_scratch_size=16384)
    g = {}

    def din(name, shape, dt):
        g[name] = nc.dram_tensor(name, shape, dt, kind="ExternalInput").ap()
        return g[name]

    def dout(name, shape, dt):
        g[name] = nc.dram_tensor(name, shape, dt, kind="ExternalOutput").ap()
        return g[name]

    def dscr(name, shape, dt):
        g[name] = nc.dram_tensor(name, shape, dt, kind="Internal").ap()
        return g[name]

    x2_in = din("x2_in", [D, L], F32)
    gate_in = din("gate_in", [128, 16 * L], BF16)
    B_in = din("B_in", [DSTATE, L], F32)
    C_in = din("C_in", [DSTATE, L], F32)
    rows_in = din("rows_in", [2, L], F32)
    aparts = din("aparts", [1, 8], F32)
    din("wout", [E2, D], F32)
    din("rwT", [D, NE], F32)
    din("rb", [1, NE], F32)
    din("w1", [NE, D, HID], F32)
    din("w2", [NE, HID, D], F32)
    din("b1cat", [2, NE * HID], F32)
    din("b1cols", [128, NE * 2], F32)
    din("b2", [NE, D], F32)
    identf = din("identf", [128, 128], F32)
    identb_in = din("identb", [128, 128], BF16)
    onesf = din("onesf", [1, 128], F32)
    onesb = din("onesb", [1, 128], BF16)
    o_out = dout("o_out", [D, L], F32)
    dscr("x3_scr", [D, L], F32)
    dscr("part_scr", [D, L], F32)
    g["mrow_scr"] = dscr("mrow_scr", [NE, 256], BF16)
    g["h_scr"] = dscr("h_scrB", [128, NCH * L], BF16)
    dscr("mfm_scr", [512, 256], BF16)
    dscr("vrow_scr", [16, NE * T3], BF16)
    dscr("w1b_scr", [NE, D, HID], BF16)
    dscr("w2b_scr", [NE, HID, D], BF16)
    darow_scr = dscr("darow_scr", [1, L], F32)

    dbg_outs = {}
    for name, shape, dt in dbg:
        dbg_outs[name] = dout(name, shape, dt)

    prog = Prog(nc)
    stack = contextlib.ExitStack()
    sb = lambda name, shape, dt: stack.enter_context(nc.sbuf_tensor(name, shape, dt))
    S = {n: Sem(stack.enter_context(nc.semaphore(n))) for n in
         ("dS", "dC", "dG", "dW", "pe", "act", "dve", "bar")}
    dS, dC, dG, pe, act, dve, bar = (S[n] for n in
                                     ("dS", "dC", "dG", "pe", "act", "dve", "bar"))

    ident_f = sb("identB_f", [128, 128], F32)
    ident_bb = sb("identB_b", [128, 128], BF16)
    ones_f = sb("onesB_f", [1, 128], F32)
    ones_b = sb("onesB_b", [1, 128], BF16)
    prog.dma("sync", ident_f[:], identf, dS)
    prog.dma("sync", ident_bb[:], identb_in, dS)
    prog.dma("sync", ones_f[:], onesf, dS)
    prog.dma("sync", ones_b[:], onesb, dS)
    consts_ready = dS.n
    cst = sb("cstB", [128, 8], F32)
    onesc_f = sb("onescB_f", [128, 1], F32)
    onesc_b5 = sb("onescB_b", [128, 1], BF16)
    zrow_b5 = sb("zrowB_b", [1, 512], BF16)
    prog.op("vector", lambda e: e.memset(cst[:, 0:1], EPS))
    prog.op("vector", lambda e: e.memset(cst[:, 1:2], 0.5))
    prog.op("vector", lambda e: e.memset(cst[:, 2:3], -0.5))
    prog.op("vector", lambda e: e.memset(cst[:, 3:4], 1.0))
    prog.op("vector", lambda e: e.memset(cst[:, 4:5], -1.0))
    prog.op("vector", lambda e: e.memset(onesc_f[:], 1.0))
    prog.op("vector", lambda e: e.memset(onesc_b5[:], 1.0))
    prog.op("vector", lambda e: e.memset(zrow_b5[:], 0.0), inc=dve)
    cst_ready = dve.n
    prog.wait("scalar", dve, cst_ready)
    prog.wait("tensor", dve, cst_ready)

    with contextlib.ExitStack() as ph:
        psb = lambda name, shape, dt: ph.enter_context(nc.sbuf_tensor(name, shape, dt))
        pps = lambda name, shape, dt: ph.enter_context(nc.psum_tensor(name, shape, dt))
        ap8 = psb("ap8", [1, 8], F32)
        asc = psb("asc", [128, 2], F32)
        dtF = psb("dtF", [128, 32], F32)
        dAF = psb("dAF", [128, 32], F32)
        du_row = psb("du_row", [1, L], F32)
        dA_row = psb("dA_row", [1, L], F32)
        y_row = psb("y_row", [1, L], F32)
        dAb = psb("dAb", [128, L], F32)
        dBu = psb("dBu", [128, L], F32)
        Bst = psb("Bst", [128, L], F32)
        Cst = psb("Cst", [128, L], F32)
        states = psb("states", [128, L], F32)
        sps = pps("sps", [128, 1024], F32)

        prog.dma("sync", ap8[:], aparts, dS)
        prog.dma("sync", dtF[:],
                 rows_in[0:1, :].rearrange("a (p f) -> (a p) f", p=128), dS)
        prog.dma("sync", du_row[:], rows_in[1:2, :], dS)
        prog.dma("scalar", Bst[:], B_in, dC)
        prog.dma("scalar", Cst[:], C_in, dC)
        ld5 = dS.n
        prog.wait("vector", dS, ld5)
        prog.op("vector", lambda e: e.reduce_sum(
            out=asc[0:1, 0:1], in_=ap8[:], axis=AX.X), dr=True)
        prog.op("vector", lambda e: e.tensor_scalar_mul(
            out=asc[0:1, 0:1], in0=asc[0:1, 0:1], scalar1=1.0 / (8 * L)), inc=dve)
        prog.wait("tensor", dve, dve.n)
        prog.wait("tensor", dS, consts_ready)
        prog.op("tensor", lambda e: e.matmul(
            sps[:, 0:1], ones_f[:], asc[0:1, 0:1], start=True, stop=True), inc=pe)
        prog.wait("vector", pe, pe.n)
        prog.op("vector", lambda e: e.tensor_copy(out=asc[:, 1:2], in_=sps[:, 0:1]),
                inc=dve)
        prog.op("vector", lambda e: e.tensor_scalar(
            out=dAF[:], in0=dtF[:], scalar1=asc[:, 1:2], scalar2=None,
            op0=ALU.mult), inc=dve)
        prog.wait("scalar", dve, dve.n)
        prog.op("scalar", lambda e: e.activation(
            out=dAF[:], in_=dAF[:], func=AF.Exp), inc=act)
        prog.wait("sync", act, act.n)
        prog.dma("sync", darow_scr.rearrange("a (p f) -> (a p) f", p=128), dAF[:], dS)
        prog.dma("sync", dA_row[:], darow_scr, dS)
        rowb = dS.n
        prog.wait("tensor", dS, rowb)
        prog.wait("vector", dC, dC.n)
        for cc in range(NCH):
            prog.op("tensor", lambda e, cc=cc: e.matmul(
                sps[:, 0:512], ones_f[:], dA_row[0:1, cc * 512:(cc + 1) * 512],
                start=True, stop=True), inc=pe)
            prog.wait("vector", pe, pe.n)
            prog.op("vector", lambda e, cc=cc: e.tensor_copy(
                out=dAb[:, cc * 512:(cc + 1) * 512], in_=sps[:, 0:512]), inc=dve)
            prog.wait("tensor", dve, dve.n)
            prog.op("tensor", lambda e, cc=cc: e.matmul(
                sps[:, 512:1024], ones_f[:], du_row[0:1, cc * 512:(cc + 1) * 512],
                start=True, stop=True), inc=pe)
            prog.wait("vector", pe, pe.n)
            prog.op("vector", lambda e, cc=cc: e.tensor_tensor(
                out=dBu[:, cc * 512:(cc + 1) * 512],
                in0=Bst[:, cc * 512:(cc + 1) * 512],
                in1=sps[:, 512:1024], op=ALU.mult), inc=dve)
            prog.wait("tensor", dve, dve.n)
        prog.op("vector", lambda e: e.tensor_tensor_scan(
            out=states[:], data0=dAb[:], data1=dBu[:],
            initial=0.0, op0=ALU.mult, op1=ALU.add), dr=True)
        prog.op("vector", lambda e: e.tensor_tensor(
            out=states[:], in0=states[:], in1=Cst[:], op=ALU.mult), inc=dve)
        prog.wait("tensor", dve, dve.n)
        for cc in range(NCH):
            prog.op("tensor", lambda e, cc=cc: e.matmul(
                sps[0:1, 0:512], onesc_f[:], states[:, cc * 512:(cc + 1) * 512],
                start=True, stop=True), inc=pe)
            prog.wait("vector", pe, pe.n)
            prog.op("vector", lambda e, cc=cc: e.tensor_copy(
                out=y_row[0:1, cc * 512:(cc + 1) * 512], in_=sps[0:1, 0:512]),
                inc=dve)
            prog.wait("tensor", dve, dve.n)

        with contextlib.ExitStack() as p2h:
            qsb = lambda name, shape, dt: p2h.enter_context(nc.sbuf_tensor(name, shape, dt))
            qps = lambda name, shape, dt: p2h.enter_context(nc.psum_tensor(name, shape, dt))
            wout_sb = qsb("wout_sb", [128, 16 * D], BF16)
            prog.dma("gpsimd", wout_sb[:].rearrange("p (j o) -> p j o", j=16),
                     g["wout"].rearrange("(j p) o -> p j o", p=128), dG)
            wout_ld = dG.n
            gt5 = qsb("gt5", [128, 16 * T3], BF16)
            yg5 = qsb("yg5", [128, 16 * T3], BF16)
            x25 = qsb("x25", [128, NCH * T3], F32)
            zsc5 = qsb("zsc5", [128, NCH * T3], F32)
            zps5 = qps("zps5", [128, NCH * T3], F32)
            bps5 = qps("bps5", [128, 512], F32)
            prog.wait("tensor", dG, wout_ld)
            for t in range(L // T3):
                prog.wait("sync", pe, pe.n)
                prog.wait("sync", act, act.n)
                prog.wait("sync", dve, dve.n)
                prog.dma("sync", x25[:].rearrange("p (c t) -> p c t", c=NCH),
                         x2_in[:, t * T3:(t + 1) * T3].rearrange(
                             "(c p) t -> p c t", p=128), dS)
                prog.dma("sync", gt5[:].rearrange("p (c t) -> p c t", c=16),
                         gate_in[:].rearrange("p (c q) -> p c q", c=16)[
                             :, :, t * T3:(t + 1) * T3], dS)
                g_ld = dS.n
                prog.wait("tensor", dve, dve.n)
                prog.op("tensor", lambda e, t=t: e.matmul(
                    bps5[:, 0:T3], ones_f[:], y_row[0:1, t * T3:(t + 1) * T3],
                    start=True, stop=True), inc=pe)
                prog.wait("vector", pe, pe.n)
                prog.wait("vector", dS, g_ld)
                for gc in range(16):
                    prog.op("vector", lambda e, gc=gc: e.tensor_tensor(
                        out=yg5[:, gc * T3:(gc + 1) * T3],
                        in0=gt5[:, gc * T3:(gc + 1) * T3],
                        in1=bps5[:, 0:T3], op=ALU.mult),
                        inc=(dve if gc == 15 else None))
                prog.wait("tensor", dve, dve.n)
                prog.wait("tensor", act, act.n)
                for bb in range(4):
                    prog.op("tensor", lambda e, bb=bb: e.matmul(
                        zps5[:, bb * 512:(bb + 1) * 512], zrow_b5[0:1, 0:128],
                        zrow_b5[:], start=True, stop=False, skip_group_check=True))
                for jc in range(16):
                    for oc in range(NCH):
                        prog.op("tensor", lambda e, jc=jc, oc=oc: e.matmul(
                            zps5[:, oc * T3:(oc + 1) * T3],
                            wout_sb[:, jc * D + oc * 128:jc * D + oc * 128 + 128],
                            yg5[:, jc * T3:(jc + 1) * T3],
                            start=False, stop=(jc == 15), skip_group_check=True),
                            inc=(pe if jc == 15 and oc == NCH - 1 else None))
                prog.wait("scalar", pe, pe.n)
                prog.op("scalar", lambda e: e.activation(
                    out=zsc5[:], in_=zps5[:], func=AF.Identity), inc=act)
                prog.wait("vector", act, act.n)
                prog.wait("vector", dS, g_ld)
                prog.op("vector", lambda e: e.tensor_tensor(
                    out=x25[:], in0=x25[:], in1=zsc5[:], op=ALU.add), inc=dve)
                prog.wait("sync", dve, dve.n)
                prog.dma("sync",
                         g["x3_scr"][:, t * T3:(t + 1) * T3].rearrange(
                             "(c p) t -> p c t", p=128),
                         x25[:].rearrange("p (c t) -> p c t", c=NCH), dS)
    prog.barrier(bar)

    emit_ffn_pass(nc, prog, g, S, cst, ones_f, onesc_f, ident_f,
                  consts_ready, src="x3_scr", dst="o_out", ident_b=ident_bb,
                  onesc_b=onesc_b5)
    prog.barrier(bar)

    prog.wait("gpsimd", dS, dS.n)
    prog.wait("gpsimd", dC, dC.n)
    for name in dbg_outs:
        srcmap = {"d_x3": g["x3_scr"], "d_y": None}
        if name == "d_x3":
            prog.dma("gpsimd", dbg_outs[name], g["x3_scr"], dG)
    prog.wait("gpsimd", dG, dG.n)
    prog.barrier(bar)
    prog.wait("sync", dS, dS.n)
    prog.wait("scalar", dC, dC.n)
    prog.wait("gpsimd", dG, dG.n)

    with nc.Block() as block:
        prog.emit(block)
    stack.close()
    return nc, g


def weight_map(prep):
    return {
        "rwT": prep["rwT"], "rb": prep["rb"], "w1": prep["w1"], "w2": prep["w2"],
        "b1cat": prep["b1cat"], "b1cols": prep["b1cols"], "b2": prep["b2"],
        "identf": prep["identf"], "onesf": prep["onesf"], "onesb": prep["onesb"],
        "identb": prep["identb"],
    }


def in_map_A(prep, b):
    m = weight_map(prep)
    m.update({
        "x_in": prep["x_cores"][b],
        "wqT": prep["wqT"], "wkT": prep["wkT"], "wvT": prep["wvT"], "woT": prep["woT"],
        "ropeq": prep["ropeq"], "ropek": prep["ropek"],
        "wincat": prep["wincat"],
        "wB": prep["wB"], "wC": prep["wC"], "Bb": prep["Bb"], "Cb": prep["Cb"],
        "dtb": prep["dtb"], "Alog": prep["Alog"],
        "identb": prep["identb"],
    })
    return m


def in_map_B(prep, resA_b, aparts):
    m = weight_map(prep)
    m.update({
        "x2_in": np.asarray(resA_b["o_x2"]),
        "gate_in": np.asarray(resA_b["o_gate"]),
        "B_in": np.asarray(resA_b["o_B"]),
        "C_in": np.asarray(resA_b["o_C"]),
        "rows_in": np.asarray(resA_b["o_rows"]),
        "aparts": aparts,
        "wout": prep["wout"],
    })
    return m


def kernel(**inputs):
    prep = host_prep(inputs)
    ncA, gA = build_launch_A()
    mapsA = [in_map_A(prep, b) for b in range(8)]
    resA = run_bass_kernel_spmd(ncA, mapsA, core_ids=list(range(8)))
    aparts = np.concatenate(
        [np.asarray(resA.results[b]["o_apart"], np.float32) for b in range(8)],
        axis=1)
    ncB, gB = build_launch_B()
    mapsB = [in_map_B(prep, resA.results[b], aparts) for b in range(8)]
    resB = run_bass_kernel_spmd(ncB, mapsB, core_ids=list(range(8)))
    out = np.stack([np.ascontiguousarray(np.asarray(resB.results[b]["o_out"]).T)
                    for b in range(8)], axis=0)
    return out

